# version 03
# baseline (speedup 1.0000x reference)
import numpy as np
import ml_dtypes
from contextlib import ExitStack

import concourse.bass as bass
import concourse.mybir as mybir
from concourse.bass_utils import run_bass_kernel_spmd

F32 = mybir.dt.float32
BF16 = mybir.dt.bfloat16
AF = mybir.ActivationFunctionType
ALU = mybir.AluOpType
AX = mybir.AxisListType

NCORES = 8
TOK = 4096
D = 1024
E = 8
FF = 4096
NH = 4
DK = 256
NSUB = 32
G = 512
NG = 8
SPG = 4
CAP = 5120.0
EPS_LN = 1e-5
EPS_RMS = 1e-5
BIG = 1e9


class Plan:
    ENGS = ("tensor", "vector", "scalar", "gpsimd", "sync")

    def __init__(self, nc, ctx):
        self.nc = nc
        self.ctx = ctx
        self.events = {e: [] for e in self.ENGS}
        self.sems = {}
        self.counts = {}
        self.waited = {}
        self.writes = {}
        self.reads = {}
        self.phase = 1
        self.ordinal = {}
        self.consumed = set()
        self.last_consumed_tok = {}
        self.psum_read = {}

    def sem(self, name, handle=None):
        self.sems[name] = handle if handle is not None else \
            self.ctx.enter_context(self.nc.semaphore(name))
        self.counts[name] = 0

    def _dep_waits(self, eng, reads, writes):
        waits = []
        for nm, lo, hi in reads:
            for (l2, h2, tok) in self.writes.get(nm, ()):
                if lo < h2 and l2 < hi:
                    waits.append(tok)
        for nm, lo, hi in writes:
            for (l2, h2, tok) in self.writes.get(nm, ()):
                if lo < h2 and l2 < hi:
                    waits.append(tok)
            for (l2, h2, tok) in self.reads.get(nm, ()):
                if lo < h2 and l2 < hi:
                    waits.append(tok)
        if eng == "tensor":
            waits = [w for w in waits if w[0] != "ts"]
        if eng == "sync":
            waits = [w for w in waits if w[0] != "yq"]
        if eng == "gpsimd":
            waits = [w for w in waits if w[0] != "gq"]
        return waits

    def _record(self, reads, writes, tok):
        for nm, lo, hi in writes:
            lst = self.writes.setdefault(nm, [])
            lst[:] = [e for e in lst if not (lo <= e[0] and e[1] <= hi)]
            lst.append((lo, hi, tok))
            rl = self.reads.setdefault(nm, [])
            rl[:] = [e for e in rl if not (lo <= e[0] and e[1] <= hi)]
        for nm, lo, hi in reads:
            self.reads.setdefault(nm, []).append((lo, hi, tok))

    def op(self, eng, fn, reads=(), writes=(), extra=(), inc=None, inc_by=None):
        if inc is None:
            inc = {"tensor": "ts", "vector": "vs", "scalar": "ss",
                   "gpsimd": "gq", "sync": "yq"}[eng]
        if inc_by is None:
            inc_by = 16 if eng in ("gpsimd", "sync") else 1
        extra2 = list(extra)
        if eng in ("vector", "scalar"):
            other = "scalar" if eng == "vector" else "vector"
            for nm, lo, hi in reads:
                if isinstance(nm, str) and nm.startswith("psum"):
                    t_ = self.psum_read.get((other, nm))
                    if t_ is not None:
                        extra2.append(t_)
        self.ordinal[inc] = self.ordinal.get(inc, 0) + 1
        my_ord = (inc, self.ordinal[inc])
        if PRUNE and self.phase == 2 and inc not in ("gq", "yq", "cc") and my_ord not in self.consumed:
            tok = self.last_consumed_tok.get(inc, (inc, 0))
            do_inc = False
        else:
            self.counts[inc] += inc_by
            tok = (inc, self.counts[inc])
            self.last_consumed_tok[inc] = tok
            do_inc = True
        waits = extra2 + self._dep_waits(eng, reads, writes)
        need = []
        for w in waits:
            if w is None:
                continue
            s, v, *rest = w
            if self.phase == 1:
                if rest:
                    self.consumed.add((s, rest[0]))
            if v <= 0 or self.waited.get((eng, s), 0) >= v:
                continue
            self.waited[(eng, s)] = v
            need.append((s, v))

        def ev(e, need=tuple(need), fn=fn, inc=inc, inc_by=inc_by, do_inc=do_inc):
            for s, v in need:
                e.wait_ge(self.sems[s], v)
            r = fn(e)
            if do_inc:
                r.then_inc(self.sems[inc], inc_by)

        self.events[eng].append(ev)
        tok3 = (tok[0], tok[1], my_ord)
        self._record(reads, writes, tok3)
        if eng in ("vector", "scalar"):
            for nm, lo, hi in reads:
                if isinstance(nm, str) and nm.startswith("psum"):
                    self.psum_read[(eng, nm)] = tok3
        return tok3

    def t(self, fn, r=(), w=(), extra=()):
        return self.op("tensor", fn, r, w, extra)

    def v(self, fn, r=(), w=(), extra=()):
        return self.op("vector", fn, r, w, extra)

    def s(self, fn, r=(), w=(), extra=()):
        return self.op("scalar", fn, r, w, extra)

    def g(self, fn, r=(), w=(), extra=()):
        return self.op("gpsimd", fn, r, w, extra)

    def y(self, fn, r=(), w=(), extra=()):
        return self.op("sync", fn, r, w, extra)

    def last_tok(self, sem):
        return (sem, self.counts.get(sem, 0), (sem, self.ordinal.get(sem, 0)))

    def wait_only(self, eng, waits):
        need = []
        for wv in waits:
            if wv is None:
                continue
            s, v, *rest = wv
            if self.phase == 1 and rest:
                self.consumed.add((s, rest[0]))
            if v <= 0 or self.waited.get((eng, s), 0) >= v:
                continue
            self.waited[(eng, s)] = v
            need.append((s, v))
        if not need:
            return

        def ev(e, need=tuple(need)):
            for s, v in need:
                e.wait_ge(self.sems[s], v)

        self.events[eng].append(ev)

    def replay(self, block):
        plan = self

        def mk(name):
            def run(e):
                for ev in plan.events[name]:
                    ev(e)
            return run

        block.tensor(mk("tensor"))
        block.vector(mk("vector"))
        block.scalar(mk("scalar"))
        block.gpsimd(mk("gpsimd"))
        block.sync(mk("sync"))


PRUNE = False

SIZE_PROBE_G = 99

def build(npass=3, dbg=()):
    nc = bass.Bass("TRN2", target_bir_lowering=False)

    dr = {}
    def din(name, shape):
        dr[name] = nc.dram_tensor(name, list(shape), F32, kind="ExternalInput").ap()

    din("x0", (TOK, D))
    din("w1s", (E, 8, 128, 8, 512))
    din("w2s", (E, 2, 4, 128, 8, 512))
    din("wqs", (128, 8, 1024))
    din("wks", (128, 8, 1024))
    din("wvs", (128, 8, 1024))
    din("wos", (128, 8, 1024))
    din("wbs", (128, 8, 4))
    din("wrs", (128, 8, 8))
    din("b1s", (128, 8, 32))
    din("b2s", (E, D))
    din("gb", (128, 256))
    din("coremask", (8, 1))
    din("ident", (128, 128))
    din("triu_inc", (128, 128))
    din("triu_str", (128, 128))
    din("triu32", (32, 32))
    din("iota8", (128, 8))
    din("ones_col", (128, 1))
    din("ones1", (1, 128))

    out = nc.dram_tensor("out", [TOK, D], F32, kind="ExternalOutput").ap()
    rb1 = nc.dram_tensor("rb1", [TOK, D], F32, kind="Internal").ap()
    w1c = nc.dram_tensor("w1c", [E, 8, 128, 8, 512], BF16, kind="Internal").ap()
    w2c = nc.dram_tensor("w2c", [E, 2, 4, 128, 8, 512], BF16, kind="Internal").ap()
    rb2 = nc.dram_tensor("rb2", [TOK, D], F32, kind="Internal").ap()
    trow = nc.dram_tensor("trow", [1, 512], F32, kind="Internal").ap()
    orow = nc.dram_tensor("orow", [2, 32, 8], F32, kind="Internal").ap()
    cc_in = nc.dram_tensor("cc_in", [1, 16], F32, kind="Internal").ap()
    cc_out = nc.dram_tensor("cc_out", [8, 16], F32, kind="Internal").ap()

    dbg_out = {}
    for nm, shape in dbg:
        dbg_out[nm] = nc.dram_tensor(nm, list(shape), F32, kind="ExternalOutput").ap()

    ctx = ExitStack()
    with ctx:
        def sb(name, shape, dt):
            return ctx.enter_context(nc.sbuf_tensor(name, list(shape), dt))

        def ps(name, shape, dt):
            return ctx.enter_context(nc.psum_tensor(name, list(shape), dt))

        A = sb("A", (128, 24576), BF16)
        B = sb("B", (128, 16384), BF16)
        Rg = [sb(f"Rg{i}", (128, 4096), F32) for i in range(2)]
        xls = [sb(f"xls{i}", (128, 1024), BF16) for i in range(2)]
        xlg = [sb(f"xlg{i}", (128, 4096), BF16) for i in range(2)]
        SCRBF = sb("SCRBF", (128, 4096), BF16)
        Vsb = SCRBF
        SCRF = sb("SCRF", (128, 1024), F32)
        Sf = sb("Sf", (128, 2048), F32)
        stmp = [sb(f"stmp{i}", (128, 256), F32) for i in range(2)]
        Sbf = SCRF.ap().bitcast(BF16)
        Zb = [sb(f"Zb{h}", (128, 9 * 128), BF16) for h in range(NH)]
        Xb = [sb(f"Xb{h}", (128, 512), BF16) for h in range(NH)]
        Ub = [sb(f"Ub{h}", (128, 512), BF16) for h in range(NH)]
        LTb = [sb(f"LTb{h}", (128, 128), BF16) for h in range(NH)]
        Zt = [sb(f"Zt{h}", (128, 256), F32) for h in range(NH)]
        Ot1 = [sb(f"Ot1{h}", (128, 256), F32) for h in range(NH)]
        Ot2 = [sb(f"Ot2{h}", (128, 256), F32) for h in range(2)]
        On = [sb(f"On{h}", (128, 256), BF16) for h in range(NH)]
        otb = [sb(f"otb{i}", (128, 1024), BF16) for i in range(2)]
        krbs = [sb(f"krb{h}", (128, 128), F32) for h in range(NH)]
        qsq = [sb(f"qsq{i}", (128, 512), F32) for i in range(1)]
        stat_mu = [sb(f"smu{i}", (128, 32), F32) for i in range(2)]
        stat_rs = [sb(f"srs{i}", (128, 32), F32) for i in range(2)]
        att_mu = sb("amu", (128, 32), F32)
        att_rs = sb("ars", (128, 32), F32)
        msum = sb("msum", (128, 32), F32)
        mus2 = sb("mus2", (128, 32), F32)
        var_c = sb("var_c", (128, 32), F32)
        var2 = sb("var2", (128, 32), F32)
        rsq = [SCRF]
        w_all = sb("w_all", (128, 256), F32)
        wT_all = SCRBF
        oh1_all = sb("oh1_all", (128, 256), BF16)
        oh2_all = sb("oh2_all", (128, 256), BF16)
        lgb = [sb(f"lgb{i}", (128, 8), F32) for i in range(4)]
        top = [sb(f"top{i}", (128, 48), F32) for i in range(4)]
        wcols = sb("wcols", (128, 64), F32)
        b1c = sb("b1c", (128, 256), F32)
        b2c = sb("b2c", (8, 1024), BF16)
        gbc = sb("gbc", (128, 256), F32)
        wrb = sb("wrb", (128, 64), BF16)
        wbb = sb("wbb", (128, 32), BF16)
        identb = sb("identb", (128, 128), BF16)
        triuIb = sb("triuIb", (128, 128), BF16)
        triuSb = sb("triuSb", (128, 128), BF16)
        wbt = sb("wbt", (128, 8), BF16)
        tri32 = sb("tri32", (32, 32), F32)
        iota = sb("iota", (128, 8), F32)
        onesc = sb("onesc", (128, 1), F32)
        onescb = sb("onescb", (128, 1), BF16)
        ones1b = sb("ones1b", (1, 128), BF16)
        ones1 = sb("ones1_sb", (1, 128), F32)
        onesc32 = sb("onesc32", (32, 1), F32)
        cmask = sb("cmask", (8, 1), F32)
        rowpool = sb("rowpool", (1, 1024), F32)
        nrowb = rowpool.ap().bitcast(BF16)
        totrow = rowpool[0:1, 0:512]
        tot32 = [sb(f"tot32{i}", (32, 8), F32) for i in range(2)]
        offs32 = [sb(f"offs32{i}", (32, 8), F32) for i in range(2)]
        cntrow = sb("cntrow", (1, 16), F32)
        gath = sb("gath", (8, 16), F32)
        baser = sb("baser", (1, 16), F32)

        rstdc = sb("rstdc", (128, 64), F32)
        beta_s = sb("beta_s", (128, 16), F32)
        ebuf = sb("ebuf", (128, 48), F32)
        sumsq = sb("sumsq", (128, 32), F32)
        dbg_lg = Sf[:, 0:256]

        P = [ps(f"P{i}", (128, 512), F32) for i in range(6)]
        PB = [ps(f"PB{i}", (128, 1024), BF16) for i in range(2)]

        sem_handles = {s_: ctx.enter_context(nc.semaphore(s_))
                       for s_ in ("ts", "vs", "ss", "gq", "yq", "cc")}

        def emit_all(p):
            for s_ in ("ts", "vs", "ss", "gq", "yq", "cc"):
                p.sem(s_, sem_handles[s_])

            def R(t_, lo=0, hi=1 << 30):
                return (t_, lo, hi)

            def L(tensor, lo=0, hi=1 << 30):
                return (id(tensor), lo, hi)

            PSUM = [("psum%d" % i, 0, 1) for i in range(8)]

            def PR(i):
                return PSUM[i]

            for t_, nm in [(tri32, "triu32"), (iota, "iota8"), (onesc, "ones_col"),
                           (ones1, "ones1"), (cmask, "coremask"), (b1c, "b1s"),
                           (gbc, "gb")]:
                p.y(lambda e, t_=t_, nm=nm: e.dma_start(t_[:, :], dr[nm]), w=[L(t_)])
            p.y(lambda e: e.dma_start(onesc32[:, :], dr["ones_col"][0:32, :]), w=[L(onesc32)])
            for t_, nm in [(identb, "ident"), (wrb, "wrs"), (wbb, "wbs"), (b2c, "b2s"),
                           (triuIb, "triu_inc"), (triuSb, "triu_str"),
                           (onescb, "ones_col"), (ones1b, "ones1")]:
                p.g(lambda e, t_=t_, nm=nm: e.dma_start(t_[:, :], dr[nm]), w=[L(t_)])

            def ln_make_xlg(src_tok_fn, rbuf_t, gslot, smu, srs, g):
                last = None
                for sl in range(SPG):
                    si = g * SPG + sl
                    xb = xls[sl % 2]
                    t_xl = p.v(lambda e, sl=sl, si=si, xb=xb, rbuf_t=rbuf_t: e.tensor_scalar(
                        xb[:, :], rbuf_t[:, sl * 1024:(sl + 1) * 1024],
                        smu[:, si:si + 1], srs[:, si:si + 1], ALU.subtract, ALU.mult),
                        r=[L(rbuf_t, sl * 1024, (sl + 1) * 1024), L(smu, si, si + 1),
                           L(srs, si, si + 1)],
                        w=[L(xb)])
                    for dc in range(8):
                        pb = 6 + (dc % 2)
                        p.t(lambda e, xb=xb, dc=dc, pb=pb: e.transpose(
                            PB[pb - 6][:, 0:128], xb[:, dc * 128:(dc + 1) * 128], identb[:, :]),
                            r=[L(xb, dc * 128, (dc + 1) * 128), L(identb)], w=[PR(pb)])
                        last = p.v(lambda e, dc=dc, sl=sl, gslot=gslot, pb=pb: e.tensor_copy(
                            xlg[gslot][:, dc * 512 + sl * 128: dc * 512 + (sl + 1) * 128],
                            PB[pb - 6][:, 0:128]),
                            r=[PR(pb)],
                            w=[L(xlg[gslot], dc * 512 + sl * 128, dc * 512 + (sl + 1) * 128)])
                return last

            def group_rload(src, src_name, g, dst_t):
                src_ap = src.rearrange("(gg s pp) d -> gg pp s d", gg=NG, s=SPG, pp=128)[g]
                return p.y(lambda e, sap=src_ap, dst_t=dst_t: e.dma_start(
                    dst_t.ap().rearrange("p (s d) -> p s d", s=SPG), sap),
                    r=[(src_name, g, g + 1)], w=[L(dst_t)])

            def moe_pass(pi, src, src_name, dst, dst_name, parity):
                smu, srs = stat_mu[parity], stat_rs[parity]

                for g in range(NG):
                    rbuf_t = Rg[g % 2]
                    group_rload(src, src_name, g, rbuf_t)
                    for sl in range(SPG):
                        si = g * SPG + sl
                        rt_lo, rt_hi = sl * 1024, (sl + 1) * 1024
                        rq = rsq[0]
                        p.v(lambda e, si=si, rbuf_t=rbuf_t, rt_lo=rt_lo, rt_hi=rt_hi: e.reduce_sum(
                            msum[:, si:si + 1], rbuf_t[:, rt_lo:rt_hi], AX.X),
                            r=[L(rbuf_t, rt_lo, rt_hi)], w=[L(msum, si, si + 1)])
                        p.v(lambda e, si=si: e.tensor_scalar_mul(
                            smu[:, si:si + 1], msum[:, si:si + 1], 1.0 / D),
                            r=[L(msum, si, si + 1)], w=[L(smu, si, si + 1)])
                        p.v(lambda e, rq=rq, rbuf_t=rbuf_t, rt_lo=rt_lo, rt_hi=rt_hi: e.tensor_mul(
                            rq[:, :], rbuf_t[:, rt_lo:rt_hi], rbuf_t[:, rt_lo:rt_hi]),
                            r=[L(rbuf_t, rt_lo, rt_hi)], w=[("scrf", 0, 4096)])
                        p.v(lambda e, si=si, rq=rq: e.reduce_sum(
                            var_c[:, si:si + 1], rq[:, :], AX.X),
                            r=[("scrf", 0, 4096)], w=[L(var_c, si, si + 1)])
                        p.v(lambda e, si=si: e.tensor_scalar(
                            mus2[:, si:si + 1], smu[:, si:si + 1], smu[:, si:si + 1],
                            -EPS_LN, ALU.mult, ALU.add),
                            r=[L(smu, si, si + 1)], w=[L(mus2, si, si + 1)])
                        p.v(lambda e, si=si: e.scalar_tensor_tensor(
                            var_c[:, si:si + 1], var_c[:, si:si + 1], 1.0 / D,
                            mus2[:, si:si + 1], ALU.mult, ALU.subtract),
                            r=[L(var_c, si, si + 1), L(mus2, si, si + 1)],
                            w=[L(var_c, si, si + 1)])
                    c0 = g * SPG
                    p.s(lambda e, c0=c0: e.activation(
                        var2[:, c0:c0 + SPG], var_c[:, c0:c0 + SPG], AF.Ln),
                        r=[L(var_c, c0, c0 + SPG)], w=[L(var2, c0, c0 + SPG)])
                    p.s(lambda e, c0=c0: e.activation(
                        srs[:, c0:c0 + SPG], var2[:, c0:c0 + SPG], AF.Exp, scale=-0.5),
                        r=[L(var2, c0, c0 + SPG)], w=[L(srs, c0, c0 + SPG)])
                    ln_make_xlg(None, rbuf_t, g % 2, smu, srs, g)
                    if "d_xl" in dbg_out and pi == 0 and g == 0:
                        p.g(lambda e: e.dma_start(dbg_out["d_xl"], xlg[0][:, 0:512]),
                            r=[L(xlg[0], 0, 512)], w=[])
                    for sl in range(SPG):
                        si = g * SPG + sl
                        pbank = sl % 2
                        for dc in range(8):
                            p.t(lambda e, dc=dc, g=g, sl=sl, pbank=pbank: e.matmul(
                                P[pbank][:, 0:8],
                                xlg[g % 2][:, dc * 512 + sl * 128: dc * 512 + (sl + 1) * 128],
                                wrb[:, dc * 8:(dc + 1) * 8],
                                start=(dc == 0), stop=(dc == 7)),
                                r=[L(xlg[g % 2], dc * 512 + sl * 128, dc * 512 + (sl + 1) * 128),
                                   L(wrb)],
                                w=[PR(pbank)])
                        lg = lgb[sl]
                        p.v(lambda e, lg=lg, pbank=pbank: e.tensor_copy(lg[:, :], P[pbank][:, 0:8]),
                            r=[PR(pbank)], w=[L(lg)])
                        if "d_lg" in dbg_out and pi == 0:
                            p.v(lambda e, lg=lg, si=si: e.tensor_copy(
                                mus2[:, 0:8] if False else dbg_lg[:, si * 8:(si + 1) * 8], lg[:, :]),
                                r=[L(lg)], w=[L(dbg_lg, si * 8, (si + 1) * 8)])
                        tp = top[sl]
                        o1 = (id(oh1_all), si * 8, (si + 1) * 8)
                        o2 = (id(oh2_all), si * 8, (si + 1) * 8)
                        p.v(lambda e, tp=tp, lg=lg: e.reduce_max(tp[:, 0:1], lg[:, :], AX.X),
                            r=[L(lg)], w=[L(tp, 0, 1)])
                        p.v(lambda e, tp=tp, lg=lg: e.tensor_single_scalar(
                            tp[:, 8:16], lg[:, :], tp[:, 0:1], ALU.is_equal),
                            r=[L(lg), L(tp, 0, 1)], w=[L(tp, 8, 16)])
                        p.v(lambda e, tp=tp: e.tensor_scalar(
                            tp[:, 16:24], tp[:, 8:16], -BIG, BIG, ALU.mult, ALU.add),
                            r=[L(tp, 8, 16)], w=[L(tp, 16, 24)])
                        p.v(lambda e, tp=tp: e.tensor_add(tp[:, 24:32], tp[:, 16:24], iota[:, :]),
                            r=[L(tp, 16, 24), L(iota)], w=[L(tp, 24, 32)])
                        p.v(lambda e, tp=tp: e.tensor_reduce(tp[:, 1:2], tp[:, 24:32], AX.X, ALU.min),
                            r=[L(tp, 24, 32)], w=[L(tp, 1, 2)])
                        p.v(lambda e, tp=tp, si=si: e.tensor_single_scalar(
                            oh1_all[:, si * 8:(si + 1) * 8], iota[:, :], tp[:, 1:2], ALU.is_equal),
                            r=[L(iota), L(tp, 1, 2)], w=[o1])
                        p.v(lambda e, tp=tp, lg=lg, si=si: e.scalar_tensor_tensor(
                            tp[:, 16:24], oh1_all[:, si * 8:(si + 1) * 8], -BIG, lg[:, :],
                            ALU.mult, ALU.add),
                            r=[o1, L(lg)], w=[L(tp, 16, 24)])
                        p.v(lambda e, tp=tp: e.reduce_max(tp[:, 2:3], tp[:, 16:24], AX.X),
                            r=[L(tp, 16, 24)], w=[L(tp, 2, 3)])
                        p.v(lambda e, tp=tp: e.tensor_single_scalar(
                            tp[:, 32:40], tp[:, 16:24], tp[:, 2:3], ALU.is_equal),
                            r=[L(tp, 16, 24), L(tp, 2, 3)], w=[L(tp, 32, 40)])
                        p.v(lambda e, tp=tp: e.tensor_scalar(
                            tp[:, 40:48], tp[:, 32:40], -BIG, BIG, ALU.mult, ALU.add),
                            r=[L(tp, 32, 40)], w=[L(tp, 40, 48)])
                        p.v(lambda e, tp=tp: e.tensor_add(tp[:, 32:40], tp[:, 40:48], iota[:, :]),
                            r=[L(tp, 40, 48), L(iota)], w=[L(tp, 32, 40)])
                        p.v(lambda e, tp=tp: e.tensor_reduce(tp[:, 3:4], tp[:, 32:40], AX.X, ALU.min),
                            r=[L(tp, 32, 40)], w=[L(tp, 3, 4)])
                        p.v(lambda e, tp=tp, si=si: e.tensor_single_scalar(
                            oh2_all[:, si * 8:(si + 1) * 8], iota[:, :], tp[:, 3:4], ALU.is_equal),
                            r=[L(iota), L(tp, 3, 4)], w=[o2])
                        p.v(lambda e, tp=tp: e.tensor_sub(tp[:, 4:5], tp[:, 2:3], tp[:, 0:1]),
                            r=[L(tp, 0, 1), L(tp, 2, 3)], w=[L(tp, 4, 5)])
                        p.s(lambda e, tp=tp: e.activation(tp[:, 5:6], tp[:, 4:5], AF.Exp),
                            r=[L(tp, 4, 5)], w=[L(tp, 5, 6)])
                        p.v(lambda e, tp=tp: e.tensor_scalar_add(tp[:, 6:7], tp[:, 5:6], 1.0),
                            r=[L(tp, 5, 6)], w=[L(tp, 6, 7)])
                        p.v(lambda e, tp=tp, si=si: e.reciprocal(wcols[:, si:si + 1], tp[:, 6:7]),
                            r=[L(tp, 6, 7)], w=[L(wcols, si, si + 1)])
                        p.v(lambda e, tp=tp, si=si: e.tensor_mul(
                            wcols[:, 32 + si:33 + si], tp[:, 5:6], wcols[:, si:si + 1]),
                            r=[L(tp, 5, 6), L(wcols, si, si + 1)], w=[L(wcols, 32 + si, 33 + si)])
                        p.t(lambda e, si=si: e.matmul(
                            P[2][0:1, si * 8:(si + 1) * 8], onescb[:, :],
                            oh1_all[:, si * 8:(si + 1) * 8], start=(si == 0), stop=False),
                            r=[o1, L(onescb)], w=[PR(2)])
                        p.t(lambda e, si=si: e.matmul(
                            P[2][0:1, 256 + si * 8: 256 + (si + 1) * 8], onescb[:, :],
                            oh2_all[:, si * 8:(si + 1) * 8], start=False, stop=(si == NSUB - 1)),
                            r=[o2, L(onescb)], w=[PR(2)])

                p.v(lambda e: e.tensor_copy(rowpool[0:1, 0:512], P[2][0:1, 0:512]),
                    r=[PR(2)], w=[L(rowpool, 0, 2048)])
                p.g(lambda e: e.dma_start(trow, rowpool[0:1, 0:512]),
                    r=[L(rowpool, 0, 2048)], w=[("trow", 0, 512)])
                p.g(lambda e: e.dma_start(
                    tot32[0][:, :], trow[0:1, 0:256].rearrange("o (s e) -> (o s) e", s=32)),
                    r=[("trow", 0, 256)], w=[L(tot32[0])])
                p.g(lambda e: e.dma_start(
                    tot32[1][:, :], trow[0:1, 256:512].rearrange("o (s e) -> (o s) e", s=32)),
                    r=[("trow", 256, 512)], w=[L(tot32[1])])
                p.t(lambda e: e.matmul(P[3][0:32, 0:8], tri32[:, :], tot32[0][:, :],
                                       start=True, stop=False),
                    r=[L(tri32), L(tot32[0])], w=[PR(3)])
                p.t(lambda e: e.matmul(P[3][0:32, 8:16], tri32[:, :], tot32[1][:, :],
                                       start=False, stop=False),
                    r=[L(tri32), L(tot32[1])], w=[PR(3)])
                p.t(lambda e: e.matmul(P[4][0:1, 0:8], onesc32[:, :], tot32[0][:, :],
                                       start=True, stop=False),
                    r=[L(onesc32), L(tot32[0])], w=[PR(4)])
                p.t(lambda e: e.matmul(P[4][0:1, 8:16], onesc32[:, :], tot32[1][:, :],
                                       start=False, stop=True),
                    r=[L(onesc32), L(tot32[1])], w=[PR(4)])
                p.v(lambda e: e.tensor_copy(cntrow[:, :], P[4][0:1, 0:16]),
                    r=[PR(4)], w=[L(cntrow)])
                tok_ci = p.g(lambda e: e.dma_start(cc_in, cntrow[:, :]), r=[L(cntrow)], w=[])
                tok_cc = p.op("gpsimd", lambda e: e.collective_compute(
                    "AllGather", ALU.bypass, replica_groups=[list(range(8))],
                    ins=[cc_in], outs=[cc_out]), reads=[], writes=[],
                    extra=[tok_ci], inc="cc", inc_by=1)
                p.g(lambda e: e.dma_start(gath[:, :], cc_out), r=[], w=[L(gath)],
                    extra=[tok_cc])
                p.t(lambda e: e.matmul(P[4][0:1, 0:16], cmask[:, :], gath[:, :],
                                       start=True, stop=True),
                    r=[L(cmask), L(gath)], w=[PR(4)])
                p.v(lambda e: e.tensor_copy(baser[:, :], P[4][0:1, 0:16]),
                    r=[PR(4)], w=[L(baser)])
                p.t(lambda e: e.matmul(P[3][0:32, 0:8], ones1[:, 0:32], baser[:, 0:8],
                                       start=False, stop=False),
                    r=[L(ones1), L(baser, 0, 8)], w=[PR(3)])
                p.t(lambda e: e.matmul(P[3][0:32, 8:16], ones1[:, 0:32], baser[:, 8:16],
                                       start=False, stop=True),
                    r=[L(ones1), L(baser, 8, 16)], w=[PR(3)])
                p.v(lambda e: e.tensor_copy(offs32[0][:, :], P[3][0:32, 0:8]),
                    r=[PR(3)], w=[L(offs32[0])])
                p.v(lambda e: e.tensor_copy(offs32[1][:, :], P[3][0:32, 8:16]),
                    r=[PR(3)], w=[L(offs32[1])])
                for k_ in range(2):
                    p.g(lambda e, k_=k_: e.dma_start(orow[k_], offs32[k_][:, :]),
                        r=[L(offs32[k_])], w=[("orow", k_ * 256, (k_ + 1) * 256)])
                    p.g(lambda e, k_=k_: e.dma_start(
                        rowpool[0:1, 512 + 256 * k_: 768 + 256 * k_],
                        orow[k_].rearrange("s e -> (s e)")[None, :]),
                        r=[("orow", k_ * 256, (k_ + 1) * 256)],
                        w=[L(rowpool, 2048 + 1024 * k_, 3072 + 1024 * k_)])
                for si in range(NSUB):
                    pb = 4 + (si % 2)
                    o1 = (id(oh1_all), si * 8, (si + 1) * 8)
                    o2 = (id(oh2_all), si * 8, (si + 1) * 8)
                    p.t(lambda e, si=si, pb=pb: e.matmul(
                        P[pb][:, 0:8], triuIb[:, :], oh1_all[:, si * 8:(si + 1) * 8],
                        start=True, stop=False),
                        r=[L(triuIb), o1], w=[PR(pb)])
                    p.t(lambda e, si=si, pb=pb: e.matmul(
                        P[pb][:, 0:8], ones1[:, :],
                        rowpool.ap()[0:1, 512 + si * 8: 512 + (si + 1) * 8],
                        start=False, stop=False),
                        r=[L(ones1), L(rowpool, 2048 + si * 32, 2048 + (si + 1) * 32)], w=[PR(pb)])
                    p.t(lambda e, si=si, pb=pb: e.matmul(
                        P[pb][:, 8:16], triuIb[:, :], oh2_all[:, si * 8:(si + 1) * 8],
                        start=False, stop=False),
                        r=[L(triuIb), o2], w=[PR(pb)])
                    p.t(lambda e, si=si, pb=pb: e.matmul(
                        P[pb][:, 8:16], ones1[:, :],
                        rowpool.ap()[0:1, 768 + si * 8: 768 + (si + 1) * 8],
                        start=False, stop=True),
                        r=[L(ones1), L(rowpool, 3072 + si * 32, 3072 + (si + 1) * 32)], w=[PR(pb)])
                    tp = top[si % 4]
                    p.v(lambda e, tp=tp, pb=pb: e.tensor_single_scalar(
                        tp[:, 8:16], P[pb][:, 0:8], CAP, ALU.is_le),
                        r=[PR(pb)], w=[L(tp, 8, 16)])
                    p.v(lambda e, tp=tp, pb=pb: e.tensor_single_scalar(
                        tp[:, 16:24], P[pb][:, 8:16], CAP, ALU.is_le),
                        r=[PR(pb)], w=[L(tp, 16, 24)])
                    p.v(lambda e, tp=tp, si=si: e.tensor_mul(
                        tp[:, 24:32], oh1_all[:, si * 8:(si + 1) * 8], tp[:, 8:16]),
                        r=[o1, L(tp, 8, 16)], w=[L(tp, 24, 32)])
                    p.v(lambda e, tp=tp, si=si: e.tensor_scalar(
                        tp[:, 40:48], tp[:, 24:32], wcols[:, si:si + 1], 0.5, ALU.mult, ALU.mult),
                        r=[L(tp, 24, 32), L(wcols, si, si + 1)], w=[L(tp, 40, 48)])
                    p.v(lambda e, tp=tp, si=si: e.tensor_mul(
                        tp[:, 32:40], oh2_all[:, si * 8:(si + 1) * 8], tp[:, 16:24]),
                        r=[o2, L(tp, 16, 24)], w=[L(tp, 32, 40)])
                    p.v(lambda e, tp=tp, si=si: e.tensor_scalar(
                        tp[:, 24:32], tp[:, 32:40], wcols[:, 32 + si:33 + si], 0.5, ALU.mult, ALU.mult),
                        r=[L(tp, 32, 40), L(wcols, 32 + si, 33 + si)], w=[L(tp, 24, 32)])
                    p.v(lambda e, tp=tp, si=si: e.tensor_add(
                        w_all[:, si * 8:(si + 1) * 8], tp[:, 40:48], tp[:, 24:32]),
                        r=[L(tp, 40, 48), L(tp, 24, 32)], w=[L(w_all, si * 8, (si + 1) * 8)])
                    p.v(lambda e, si=si: e.tensor_copy(
                        wbt[:, :], w_all[:, si * 8:(si + 1) * 8]),
                        r=[L(w_all, si * 8, (si + 1) * 8)], w=[L(wbt)])
                    p.t(lambda e, si=si: e.transpose(
                        PB[0][0:8, 0:128], wbt[:, :], identb[:, :]),
                        r=[L(wbt), L(identb)], w=[PR(6)])
                    p.v(lambda e, si=si: e.tensor_copy(
                        wT_all[0:8, si * 128:(si + 1) * 128], PB[0][0:8, 0:128]),
                        r=[PR(6)], w=[L(wT_all, si * 128, (si + 1) * 128)])

                if "d_lg" in dbg_out and pi == 0:
                    p.y(lambda e: e.dma_start(dbg_out["d_lg"], dbg_lg[:, :]), r=[L(dbg_lg)], w=[])
                if "d_w" in dbg_out and pi == 0:
                    p.y(lambda e: e.dma_start(dbg_out["d_w"], w_all[:, :]), r=[L(w_all)], w=[])
                    p.y(lambda e: e.dma_start(dbg_out["d_counts"], gath[:, :]), r=[L(gath)], w=[])
                    p.y(lambda e: e.dma_start(dbg_out["d_mu"], smu[:, :]), r=[L(smu)], w=[])
                    p.y(lambda e: e.dma_start(dbg_out["d_rs"], srs[:, :]), r=[L(srs)], w=[])
                    p.g(lambda e: e.dma_start(dbg_out["d_oh1"], oh1_all[:, :]), r=[L(oh1_all)], w=[])
                    p.g(lambda e: e.dma_start(dbg_out["d_oh2"], oh2_all[:, :]), r=[L(oh2_all)], w=[])

                w1v = [A[:, i * 4096:(i + 1) * 4096] for i in range(3)]
                w2v = [A[:, (3 + i) * 4096:(4 + i) * 4096] for i in range(3)]

                for g in range(NG):
                    group_rload(src, src_name, g, Rg[0])
                    ln_make_xlg(None, Rg[0], g % 2, smu, srs, g)
                    if "d_xl2" in dbg_out and pi == 0 and g == 0:
                        p.g(lambda e: e.dma_start(dbg_out["d_xl2"], xlg[0][:, :]),
                            r=[L(xlg[0], 0, 4096)], w=[])
                    w1sched = [(e_, pn) for e_ in range(E) for pn in range(8)]
                    w2sched = [(e_, dh, q) for e_ in range(E) for dh in range(2)
                               for q in range(4)]

                    def emit_w1(k):
                        if k >= len(w1sched):
                            return
                        e2, pn = w1sched[k]
                        slot = k % 3
                        p.g(lambda e, e2=e2, pn=pn, slot=slot: e.dma_start(
                            w1v[slot].rearrange("p (dc f) -> p dc f", dc=8),
                            dr["w1s"][e2, pn]),
                            r=[], w=[L(A, slot * 4096, (slot + 1) * 4096)])
                        if "d_w1" in dbg_out and pi == 0 and g == 0 and k == 0:
                            p.g(lambda e: e.dma_start(dbg_out["d_w1"], w1v[0]),
                                r=[L(A, 0, 4096)], w=[])

                    def emit_w2(k):
                        if k >= len(w2sched):
                            return
                        e2, dh, q = w2sched[k]
                        slot = k % 3
                        p.g(lambda e, e2=e2, dh=dh, q=q, slot=slot: e.dma_start(
                            w2v[slot].rearrange("p (j f) -> p j f", j=8),
                            dr["w2s"][e2, dh, q]),
                            r=[], w=[L(A, (3 + slot) * 4096, (4 + slot) * 4096)])

                    for k in range(3):
                        emit_w1(k)
                        emit_w2(k)
                    rcur = 0
                    for e_ in range(E):
                        for fc in range(32):
                            pn = fc // 4
                            slot1 = (e_ * 8 + pn) % 3
                            pbk = fc % 2
                            for dc in range(8):
                                p.t(lambda e, fc=fc, dc=dc, pbk=pbk, slot1=slot1, g=g: e.matmul(
                                    P[pbk][:, :],
                                    w1v[slot1][:, dc * 512 + (fc % 4) * 128: dc * 512 + (fc % 4) * 128 + 128],
                                    xlg[g % 2][:, dc * 512:(dc + 1) * 512],
                                    start=(dc == 0), stop=(dc == 7)),
                                    r=[L(A, slot1 * 4096 + dc * 512 + (fc % 4) * 128,
                                         slot1 * 4096 + dc * 512 + (fc % 4) * 128 + 128),
                                       L(xlg[g % 2], dc * 512, (dc + 1) * 512)],
                                    w=[PR(pbk)])
                            if fc % 4 == 3:
                                emit_w1(e_ * 8 + pn + 3)
                            p.s(lambda e, fc=fc, pbk=pbk, e_=e_: e.activation(
                                B[:, fc * 512:(fc + 1) * 512], P[pbk][:, :], AF.Gelu,
                                bias=b1c[:, e_ * 32 + fc: e_ * 32 + fc + 1]),
                                r=[PR(pbk), L(b1c, e_ * 32 + fc, e_ * 32 + fc + 1)],
                                w=[L(B, fc * 512, (fc + 1) * 512)])
                            if "d_h0" in dbg_out and pi == 0 and g == 0 and e_ == 0 and fc == 0:
                                p.g(lambda e: e.dma_start(dbg_out["d_h0"], B[:, 0:512]),
                                    r=[L(B, 0, 512)], w=[])
                                p.v(lambda e, pbk=pbk: e.tensor_copy(
                                    Ot1[0][:, :] and Sf[:, 1024:1536], P[pbk][:, :]),
                                    r=[PR(pbk)], w=[L(Sf, 1024, 1536)],
                                    extra=[p.last_tok("ss")])
                                p.y(lambda e: e.dma_start(dbg_out["d_hp"], Sf[:, 1024:1536]),
                                    r=[L(Sf, 1024, 1536)], w=[])
                        for dh in range(2):
                            for q in range(4):
                                slot2 = (e_ * 8 + dh * 4 + q) % 3
                                for j in range(8):
                                    fc = 8 * q + j
                                    for sl in range(SPG):
                                        p.t(lambda e, fc=fc, sl=sl, slot2=slot2, j=j, q=q: e.matmul(
                                            P[2 + sl][:, :],
                                            B[:, fc * 512 + sl * 128: fc * 512 + (sl + 1) * 128],
                                            w2v[slot2][:, j * 512:(j + 1) * 512],
                                            start=(q == 0 and j == 0), stop=(q == 3 and j == 7)),
                                            r=[L(B, fc * 512 + sl * 128, fc * 512 + (sl + 1) * 128),
                                               L(A, (3 + slot2) * 4096 + j * 512,
                                                 (3 + slot2) * 4096 + (j + 1) * 512)],
                                            w=[PR(2 + sl)])
                                emit_w2(e_ * 8 + dh * 4 + q + 3)
                            for sl in range(SPG):
                                si = g * SPG + sl
                                lo = sl * 1024 + dh * 512
                                p.v(lambda e, sl=sl, lo=lo, si=si, e_=e_, rcur=rcur: e.scalar_tensor_tensor(
                                    Rg[1 - rcur][:, lo:lo + 512],
                                    P[2 + sl][:, :], w_all[:, si * 8 + e_: si * 8 + e_ + 1],
                                    Rg[rcur][:, lo:lo + 512],
                                    ALU.mult, ALU.add),
                                    r=[PR(2 + sl), L(w_all, si * 8 + e_, si * 8 + e_ + 1),
                                       L(Rg[rcur], lo, lo + 512)],
                                    w=[L(Rg[1 - rcur], lo, lo + 512)])
                        if "d_r0" in dbg_out and pi == 0 and g == 0 and e_ == 0:
                            _rc = rcur
                            p.y(lambda e, rc=_rc: e.dma_start(
                                dbg_out["d_r0"], Rg[1 - rc][:, 0:1024]),
                                r=[L(Rg[1 - _rc], 0, 1024)], w=[])
                        rcur = 1 - rcur
                    for sl in range(SPG):
                        si = g * SPG + sl
                        for dh in range(2):
                            p.t(lambda e, si=si, dh=dh, sl=sl: e.matmul(
                                P[2 + sl][:, :], wT_all[0:8, si * 128:(si + 1) * 128],
                                b2c[:, dh * 512:(dh + 1) * 512],
                                start=True, stop=True),
                                r=[L(wT_all, si * 128, (si + 1) * 128), L(b2c)],
                                w=[PR(2 + sl)])
                            lo = sl * 1024 + dh * 512
                            p.v(lambda e, sl=sl, lo=lo, rcur=rcur: e.tensor_add(
                                Rg[1 - rcur][:, lo:lo + 512],
                                Rg[rcur][:, lo:lo + 512],
                                P[2 + sl][:, :]),
                                r=[PR(2 + sl), L(Rg[rcur], lo, lo + 512)],
                                w=[L(Rg[1 - rcur], lo, lo + 512)])
                    dst_ap = dst.rearrange("(gg s pp) d -> gg pp s d", gg=NG, s=SPG, pp=128)[g]
                    p.y(lambda e, dap=dst_ap, rcur=rcur: e.dma_start(
                        dap, Rg[1 - rcur].ap().rearrange("p (s d) -> p s d", s=SPG)),
                        r=[L(Rg[1 - rcur])], w=[(dst_name, g, g + 1)])
                return


            def attn_pass(src, src_name, dst, dst_name):
                wq_sb = A[:, 0:8192]
                wk_sb = A[:, 8192:16384]
                wv_sb = A[:, 16384:24576]
                wo_sb = B[:, 0:8192]
                for dst_t, lo, nm in [(wq_sb, 0, "wqs"), (wk_sb, 8192, "wks"),
                                      (wv_sb, 16384, "wvs")]:
                    p.g(lambda e, dst_t=dst_t, nm=nm: e.dma_start(
                        dst_t.rearrange("p (dc c) -> p dc c", dc=8), dr[nm]),
                        r=[], w=[L(A, lo, lo + 8192)])
                p.g(lambda e: e.dma_start(
                    wo_sb.rearrange("p (oc c) -> p oc c", oc=8), dr["wos"]),
                    r=[], w=[L(B, 0, 8192)])

                def QTs(qc, c=None):
                    lo = 8192 + qc * 512 + (0 if c is None else c * 128)
                    hi = 8192 + (qc + 1) * 512 if c is None else lo + 128
                    return B[:, lo:hi], L(B, lo, hi)

                def KTs(qc, c=None):
                    lo = 12288 + qc * 512 + (0 if c is None else c * 128)
                    hi = 12288 + (qc + 1) * 512 if c is None else lo + 128
                    return B[:, lo:hi], L(B, lo, hi)

                def proj(qc, wofs, dst_lo, gs, do_sq):
                    pb = qc % 2
                    for dc in range(8):
                        p.t(lambda e, qc=qc, dc=dc, pb=pb, gs=gs, wofs=wofs: e.matmul(
                            P[pb][:, :],
                            A[:, wofs + dc * 1024 + qc * 128: wofs + dc * 1024 + (qc + 1) * 128],
                            xlg[gs][:, dc * 512:(dc + 1) * 512],
                            start=(dc == 0), stop=(dc == 7)),
                            r=[L(A, wofs + dc * 1024 + qc * 128,
                                 wofs + dc * 1024 + (qc + 1) * 128),
                               L(xlg[gs], dc * 512, (dc + 1) * 512)],
                            w=[PR(pb)])
                    p.v(lambda e, pb=pb, dst_lo=dst_lo: e.tensor_copy(
                        B[:, dst_lo:dst_lo + 512], P[pb][:, :]),
                        r=[PR(pb)], w=[L(B, dst_lo, dst_lo + 512)])
                    if do_sq:
                        p.s(lambda e, dst_lo=dst_lo: e.activation(
                            qsq[0][:, :], B[:, dst_lo:dst_lo + 512], AF.Square),
                            r=[L(B, dst_lo, dst_lo + 512)], w=[L(qsq[0])])
                        h = qc // 2
                        base = 0 if wofs == 0 else 16
                        for sl in range(SPG):
                            p.t(lambda e, sl=sl, h=h, qc=qc, base=base: e.matmul(
                                P[2][:, base + h * 4 + sl: base + h * 4 + sl + 1],
                                qsq[0][:, sl * 128:(sl + 1) * 128], onesc[:, :],
                                start=(base == 0 and qc == 0 and sl == 0), stop=False),
                                r=[L(qsq[0], sl * 128, (sl + 1) * 128), L(onesc)],
                                w=[PR(2)])
                        if wofs != 0:
                            p.t(lambda e, qc=qc: e.matmul(
                                P[3][0:1, 0:512], onesc[:, :], qsq[0][:, :],
                                start=(qc % 2 == 0), stop=(qc % 2 == 1)),
                                r=[L(qsq[0]), L(onesc)], w=[PR(3)])
                            if qc % 2 == 1:
                                h = qc // 2
                                p.s(lambda e: e.activation(
                                    qsq[1][0:1, 0:512] if False else qsq[0][0:1, 0:512],
                                    P[3][0:1, 0:512], AF.Ln),
                                    r=[PR(3)], w=[L(qsq[0], 0, 512)])
                                p.s(lambda e, h=h: e.activation(
                                    nrowb[0:1, h * 512:(h + 1) * 512],
                                    qsq[0][0:1, 0:512], AF.Exp, scale=-0.5),
                                    r=[L(qsq[0], 0, 512)],
                                    w=[L(rowpool, h * 1024, (h + 1) * 1024)])

                for b in range(2):
                    p.v(lambda e: e.memset(Sf[:, :], 0.0), r=[], w=[L(Sf)])
                    p.v(lambda e: e.memset(Sbf[:, :], 0.0), r=[], w=[("scrf", 0, 4096)])
                    for a in range(4):
                        g = b * 4 + a
                        gs = g % 2
                        group_rload(src, src_name, g, Rg[0])
                        for sl in range(SPG):
                            si = g * SPG + sl
                            lo = sl * 1024
                            p.v(lambda e, si=si, lo=lo: e.reduce_sum(
                                msum[:, si % 4: si % 4 + 1], Rg[0][:, lo:lo + 1024], AX.X),
                                r=[L(Rg[0], lo, lo + 1024)], w=[L(msum, si % 4, si % 4 + 1)])
                            p.v(lambda e, si=si: e.tensor_scalar_mul(
                                att_mu[:, si:si + 1], msum[:, si % 4: si % 4 + 1], 1.0 / D),
                                r=[L(msum, si % 4, si % 4 + 1)], w=[L(att_mu, si, si + 1)])
                            for hf in range(2):
                                p.v(lambda e, lo=lo, hf=hf: e.tensor_mul(
                                    qsq[0][:, :], Rg[0][:, lo + hf * 512:lo + (hf + 1) * 512],
                                    Rg[0][:, lo + hf * 512:lo + (hf + 1) * 512]),
                                    r=[L(Rg[0], lo + hf * 512, lo + (hf + 1) * 512)],
                                    w=[L(qsq[0])])
                                p.v(lambda e, si=si, hf=hf: e.reduce_sum(
                                    mus2[:, 8 + hf: 9 + hf], qsq[0][:, :], AX.X),
                                    r=[L(qsq[0])], w=[L(mus2, 8 + hf, 9 + hf)])
                            p.v(lambda e, si=si: e.tensor_add(
                                var_c[:, si % 4: si % 4 + 1], mus2[:, 8:9], mus2[:, 9:10]),
                                r=[L(mus2, 8, 10)], w=[L(var_c, si % 4, si % 4 + 1)])
                            p.v(lambda e, si=si: e.tensor_scalar(
                                mus2[:, si % 4: si % 4 + 1], att_mu[:, si:si + 1],
                                att_mu[:, si:si + 1], -EPS_LN, ALU.mult, ALU.add),
                                r=[L(att_mu, si, si + 1)], w=[L(mus2, si % 4, si % 4 + 1)])
                            p.v(lambda e, si=si: e.scalar_tensor_tensor(
                                var_c[:, 4 + si % 4: 5 + si % 4], var_c[:, si % 4: si % 4 + 1],
                                1.0 / D, mus2[:, si % 4: si % 4 + 1], ALU.mult, ALU.subtract),
                                r=[L(var_c, si % 4, si % 4 + 1), L(mus2, si % 4, si % 4 + 1)],
                                w=[L(var_c, 4 + si % 4, 5 + si % 4)])
                        c0 = g * SPG
                        p.s(lambda e: e.activation(var2[:, 0:4], var_c[:, 4:8], AF.Ln),
                            r=[L(var_c, 4, 8)], w=[L(var2, 0, 4)])
                        p.s(lambda e, c0=c0: e.activation(
                            att_rs[:, c0:c0 + 4], var2[:, 0:4], AF.Exp, scale=-0.5),
                            r=[L(var2, 0, 4)], w=[L(att_rs, c0, c0 + 4)])
                        ln_make_xlg(None, Rg[0], gs, att_mu, att_rs, g)

                        for qc in range(8):
                            proj(qc, 0, 8192 + qc * 512, gs, True)
                        ev_q = p.v(lambda e: e.tensor_copy(sumsq[:, 0:16], P[2][:, 0:16]),
                                   r=[PR(2)], w=[L(sumsq, 0, 16)])
                        for qc in range(8):
                            proj(qc, 8192, 12288 + qc * 512, gs, True)
                        p.v(lambda e: e.tensor_copy(sumsq[:, 16:32], P[2][:, 16:32]),
                            r=[PR(2)], w=[L(sumsq, 16, 32)])
                        p.s(lambda e: e.activation(var2[:, 0:32], sumsq[:, 0:32], AF.Ln),
                            r=[L(sumsq, 0, 32)], w=[L(var2, 0, 32)])
                        p.s(lambda e: e.activation(rstdc[:, 0:32], var2[:, 0:32], AF.Exp,
                                                   scale=-0.5),
                            r=[L(var2, 0, 32)], w=[L(rstdc, 0, 32)])
                        p.v(lambda e: e.tensor_scalar_mul(rstdc[:, 32:48], rstdc[:, 16:32], -1.0),
                            r=[L(rstdc, 16, 32)], w=[L(rstdc, 32, 48)])
                        for sl in range(SPG):
                            for half in range(2):
                                pb = 4
                                for dc in range(8):
                                    p.t(lambda e, sl=sl, half=half, dc=dc, gs=gs: e.matmul(
                                        P[4][:, :],
                                        xlg[gs][:, dc * 512 + sl * 128: dc * 512 + (sl + 1) * 128],
                                        wv_sb[:, dc * 1024 + half * 512: dc * 1024 + (half + 1) * 512],
                                        start=(dc == 0), stop=(dc == 7)),
                                        r=[L(xlg[gs], dc * 512 + sl * 128, dc * 512 + (sl + 1) * 128),
                                           L(A, 16384 + dc * 1024 + half * 512,
                                             16384 + dc * 1024 + (half + 1) * 512)],
                                        w=[PR(4)])
                                p.v(lambda e, sl=sl, half=half: e.tensor_copy(
                                    Vsb[:, sl * 1024 + half * 512: sl * 1024 + (half + 1) * 512],
                                    P[4][:, :]),
                                    r=[PR(4)],
                                    w=[L(SCRBF, sl * 1024 + half * 512, sl * 1024 + (half + 1) * 512)])
                        for sl in range(SPG):
                            for dc in range(8):
                                p.t(lambda e, sl=sl, dc=dc, gs=gs: e.matmul(
                                    P[5][:, 0:4],
                                    xlg[gs][:, dc * 512 + sl * 128: dc * 512 + (sl + 1) * 128],
                                    wbb[:, dc * 4:(dc + 1) * 4],
                                    start=(dc == 0), stop=(dc == 7)),
                                    r=[L(xlg[gs], dc * 512 + sl * 128, dc * 512 + (sl + 1) * 128),
                                       L(wbb)],
                                    w=[PR(5)])
                            p.v(lambda e, sl=sl: e.tensor_scalar_mul(
                                ebuf[:, sl * 4:(sl + 1) * 4], P[5][:, 0:4], -1.0),
                                r=[PR(5)], w=[L(ebuf, sl * 4, (sl + 1) * 4)])
                            p.s(lambda e, sl=sl: e.activation(
                                ebuf[:, 16 + sl * 4: 16 + (sl + 1) * 4],
                                ebuf[:, sl * 4:(sl + 1) * 4], AF.Exp),
                                r=[L(ebuf, sl * 4, (sl + 1) * 4)],
                                w=[L(ebuf, 16 + sl * 4, 16 + (sl + 1) * 4)])
                            p.v(lambda e, sl=sl: e.tensor_scalar_add(
                                ebuf[:, 32 + sl * 4: 32 + (sl + 1) * 4],
                                ebuf[:, 16 + sl * 4: 16 + (sl + 1) * 4], 1.0),
                                r=[L(ebuf, 16 + sl * 4, 16 + (sl + 1) * 4)],
                                w=[L(ebuf, 32 + sl * 4, 32 + (sl + 1) * 4)])
                            p.v(lambda e, sl=sl: e.reciprocal(
                                beta_s[:, sl * 4:(sl + 1) * 4],
                                ebuf[:, 32 + sl * 4: 32 + (sl + 1) * 4]),
                                r=[L(ebuf, 32 + sl * 4, 32 + (sl + 1) * 4)],
                                w=[L(beta_s, sl * 4, (sl + 1) * 4)])
                            p.v(lambda e, sl=sl: e.tensor_mul(
                                rstdc[:, 48 + sl * 4: 48 + (sl + 1) * 4],
                                beta_s[:, sl * 4:(sl + 1) * 4],
                                rstdc[:, 16 + sl * 4: 16 + (sl + 1) * 4]),
                                r=[L(beta_s, sl * 4, (sl + 1) * 4), L(rstdc, 16 + sl * 4, 16 + (sl + 1) * 4)],
                                w=[L(rstdc, 48 + sl * 4, 48 + (sl + 1) * 4)])

                        if "a_ss" in dbg_out and b == 0 and a == 0:
                            p.y(lambda e: e.dma_start(dbg_out["a_ss"], sumsq[:, :]),
                                r=[L(sumsq)], w=[])
                            p.y(lambda e: e.dma_start(dbg_out["a_rs"], rstdc[:, :]),
                                r=[L(rstdc)], w=[])
                            p.y(lambda e: e.dma_start(dbg_out["a_bs"], beta_s[:, :]),
                                r=[L(beta_s)], w=[])
                            p.g(lambda e: e.dma_start(dbg_out["a_kr"], rowpool[0:1, :]),
                                r=[L(rowpool, 0, 4096)], w=[])
                            p.g(lambda e: e.dma_start(dbg_out["a_qt"], B[:, 8192:12288]),
                                r=[L(B, 8192, 12288)], w=[])
                            p.g(lambda e: e.dma_start(dbg_out["a_kt"], B[:, 12288:16384]),
                                r=[L(B, 12288, 16384)], w=[])
                            p.g(lambda e: e.dma_start(dbg_out["a_v"], Vsb[:, :]),
                                r=[L(SCRBF, 0, 4096)], w=[])
                        bankpair = {0: (0, 1), 1: (2, 3), 2: (4, 5), 3: (0, 1)}
                        for c in range(4):
                            for h in range(NH):
                                al, be = bankpair[h]
                                qr = (id(rstdc), 0 + h * 4 + c, 1 + h * 4 + c)
                                kr = (id(rstdc), 16 + h * 4 + c, 17 + h * 4 + c)
                                nk = (id(rstdc), 32 + h * 4 + c, 33 + h * 4 + c)
                                bk = (id(rstdc), 48 + c * 4 + h, 49 + c * 4 + h)
                                bcol = (id(beta_s), c * 4 + h, c * 4 + h + 1)
                                Zh, Xh, Uh = Zb[h], Xb[h], Ub[h]
                                sA = (id(Sf), (2 * h) * 256, (2 * h + 1) * 256)
                                sB = (id(Sf), (2 * h + 1) * 256, (2 * h + 2) * 256)
                                sbA = ("scrf", (2 * h) * 512, (2 * h + 1) * 512, (2 * h) * 256)
                                sbB = ("scrf", (2 * h + 1) * 512, (2 * h + 2) * 512, (2 * h + 1) * 256)
                                for i, qc in enumerate((2 * h, 2 * h + 1)):
                                    kap, krd = KTs(qc, c)
                                    p.t(lambda e, kap=kap, al=al, i=i: e.matmul(
                                        P[al][:, 0:128], kap, kap, start=(i == 0), stop=(i == 1)),
                                        r=[krd], w=[PR(al)])
                                p.t(lambda e, h=h, c=c, be=be: e.matmul(
                                    P[be][:, 0:128], ones1b[:, :],
                                    nrowb[0:1, h * 512 + c * 128: h * 512 + (c + 1) * 128],
                                    start=True, stop=True),
                                    r=[L(ones1b),
                                       (id(rowpool), (h * 512 + c * 128) * 2, (h * 512 + (c + 1) * 128) * 2)],
                                    w=[PR(be)])
                                p.v(lambda e, h=h, be=be: e.tensor_copy(krbs[h][:, :], P[be][:, 0:128]),
                                    r=[PR(be)], w=[L(krbs[h])])
                                if "a_m" in dbg_out and b == 0 and a == 0 and c == 0 and h == 0:
                                    p.v(lambda e, al=al: e.tensor_copy(
                                        qsq[0][:, 0:128], P[al][:, 0:128]),
                                        r=[PR(al)], w=[L(qsq[0], 0, 128)])
                                    p.y(lambda e: e.dma_start(dbg_out["a_m"], qsq[0][:, 0:128]),
                                        r=[L(qsq[0], 0, 128)], w=[])
                                p.v(lambda e, h=h, al=al: e.tensor_tensor(
                                    Zt[h][:, 0:128], P[al][:, 0:128], triuSb[:, :], ALU.mult),
                                    r=[PR(al), L(triuSb)], w=[L(Zt[h], 0, 128)])
                                p.v(lambda e, h=h: e.tensor_mul(
                                    Zt[h][:, 128:256], Zt[h][:, 0:128], krbs[h][:, :]),
                                    r=[L(Zt[h], 0, 128), L(krbs[h])], w=[L(Zt[h], 128, 256)])
                                p.v(lambda e, h=h, bk=bk, Zh=Zh: e.tensor_scalar_mul(
                                    Zh[:, 0:128], Zt[h][:, 128:256], rstdc[:, bk[1]:bk[2]]),
                                    r=[L(Zt[h], 128, 256), bk], w=[L(Zh, 0, 128)])
                                if "a_m" in dbg_out and b == 0 and a == 0 and c == 0 and h == 0:
                                    p.y(lambda e: e.dma_start(dbg_out["a_zt"], Zt[0][:, :]),
                                        r=[L(Zt[0])], w=[])
                                    p.y(lambda e: e.dma_start(dbg_out["a_krb"], krbs[0][:, :]),
                                        r=[L(krbs[0])], w=[])
                                if "a_z0i" in dbg_out and b == 0 and a == 0 and c == 0 and h == 0:
                                    p.g(lambda e: e.dma_start(dbg_out["a_z0i"], Zb[0][:, 0:128]),
                                        r=[L(Zb[0], 0, 128)], w=[])
                                pbt = 6 + (h % 2)
                                p.t(lambda e, h=h, pbt=pbt: e.transpose(
                                    PB[pbt - 6][:, 0:128], Zb[h][:, 0:128], identb[:, :]),
                                    r=[L(Zh, 0, 128), L(identb)], w=[PR(pbt)])
                                p.v(lambda e, h=h, pbt=pbt, Zh=Zh: e.tensor_copy(
                                    Zh[:, 7 * 128:8 * 128], PB[pbt - 6][:, 0:128]),
                                    r=[PR(pbt)], w=[L(Zh, 7 * 128, 8 * 128)])
                                for j in range(1, 7):
                                    zt_prev = 7 * 128 if j % 2 == 1 else 8 * 128
                                    zt_new = 8 * 128 if j % 2 == 1 else 7 * 128
                                    bank = al if j % 2 == 1 else be
                                    p.t(lambda e, j=j, h=h, bank=bank, zt_prev=zt_prev, Zh=Zh: e.matmul(
                                        P[bank][:, 0:128], Zh[:, zt_prev:zt_prev + 128],
                                        Zh[:, (j - 1) * 128: j * 128], start=True, stop=True),
                                        r=[L(Zh, zt_prev, zt_prev + 128),
                                           L(Zh, (j - 1) * 128, j * 128)],
                                        w=[PR(bank)])
                                    p.v(lambda e, j=j, h=h, bank=bank, Zh=Zh: e.tensor_copy(
                                        Zh[:, j * 128:(j + 1) * 128], P[bank][:, 0:128]),
                                        r=[PR(bank)], w=[L(Zh, j * 128, (j + 1) * 128)])
                                    if j < 6:
                                        bank2 = be if j % 2 == 1 else al
                                        p.t(lambda e, j=j, h=h, bank2=bank2, zt_prev=zt_prev, Zh=Zh: e.matmul(
                                            P[bank2][:, 0:128], Zh[:, (j - 1) * 128: j * 128],
                                            Zh[:, zt_prev:zt_prev + 128], start=True, stop=True),
                                            r=[L(Zh, zt_prev, zt_prev + 128),
                                               L(Zh, (j - 1) * 128, j * 128)],
                                            w=[PR(bank2)])
                                        p.v(lambda e, j=j, h=h, bank2=bank2, zt_new=zt_new, Zh=Zh: e.tensor_copy(
                                            Zh[:, zt_new:zt_new + 128], P[bank2][:, 0:128]),
                                            r=[PR(bank2)], w=[L(Zh, zt_new, zt_new + 128)])
                                for i, qc in enumerate((2 * h, 2 * h + 1)):
                                    kap, krd = KTs(qc, c)
                                    ss_ = sbA if i == 0 else sbB
                                    p.t(lambda e, kap=kap, al=al, i=i, h=h, ss_=ss_: e.matmul(
                                        P[al][:, 0:256], kap,
                                        Sbf[:, ss_[3]:ss_[3] + 256], start=(i == 0), stop=(i == 1)),
                                        r=[krd, ss_[:3]], w=[PR(al)])
                                p.v(lambda e, h=h, c=c, al=al, nk=nk, Xh=Xh: e.scalar_tensor_tensor(
                                    Xh[:, 0:256], P[al][:, 0:256], rstdc[:, nk[1]:nk[2]],
                                    Vsb[:, c * 1024 + h * 256: c * 1024 + (h + 1) * 256],
                                    ALU.mult, ALU.add),
                                    r=[PR(al), nk,
                                       L(SCRBF, c * 1024 + h * 256, c * 1024 + (h + 1) * 256)],
                                    w=[L(Xh, 0, 256)])
                                xcur = 0
                                for j in range(6, -1, -1):
                                    bank = al if j % 2 == 0 else be
                                    p.t(lambda e, j=j, h=h, bank=bank, xcur=xcur, Zh=Zh, Xh=Xh: e.matmul(
                                        P[bank][:, 0:256], Zh[:, j * 128:(j + 1) * 128],
                                        Xh[:, xcur * 256:(xcur + 1) * 256], start=True, stop=True),
                                        r=[L(Zh, j * 128, (j + 1) * 128),
                                           L(Xh, xcur * 256, (xcur + 1) * 256)],
                                        w=[PR(bank)])
                                    op_ = ALU.add if j > 0 else ALU.subtract
                                    p.v(lambda e, h=h, bank=bank, xcur=xcur, op_=op_, Xh=Xh: e.tensor_tensor(
                                        Xh[:, (1 - xcur) * 256:(2 - xcur) * 256],
                                        Xh[:, xcur * 256:(xcur + 1) * 256], P[bank][:, 0:256], op_),
                                        r=[L(Xh, xcur * 256, (xcur + 1) * 256), PR(bank)],
                                        w=[L(Xh, (1 - xcur) * 256, (2 - xcur) * 256)])
                                    xcur = 1 - xcur
                                p.v(lambda e, h=h, xcur=xcur, bcol=bcol, Uh=Uh, Xh=Xh: e.tensor_scalar_mul(
                                    Uh[:, 0:256], Xh[:, xcur * 256:(xcur + 1) * 256],
                                    beta_s[:, bcol[1]:bcol[2]]),
                                    r=[L(Xh, xcur * 256, (xcur + 1) * 256), bcol],
                                    w=[L(Uh, 0, 256)])
                                p.v(lambda e, h=h, xcur=xcur, bk=bk, Uh=Uh, Xh=Xh: e.tensor_scalar_mul(
                                    Uh[:, 256:512], Xh[:, xcur * 256:(xcur + 1) * 256],
                                    rstdc[:, bk[1]:bk[2]]),
                                    r=[L(Xh, xcur * 256, (xcur + 1) * 256), bk],
                                    w=[L(Uh, 256, 512)])
                                for i, qc in enumerate((2 * h, 2 * h + 1)):
                                    kap, krd = KTs(qc, c)
                                    qap, qrd = QTs(qc, c)
                                    p.t(lambda e, kap=kap, qap=qap, be=be, i=i: e.matmul(
                                        P[be][:, 0:128], kap, qap, start=(i == 0), stop=(i == 1)),
                                        r=[krd, qrd], w=[PR(be)])
                                p.v(lambda e, h=h, be=be: e.tensor_tensor(
                                    Zt[h][:, 0:128], P[be][:, 0:128], triuIb[:, :], ALU.mult),
                                    r=[PR(be), L(triuIb)], w=[L(Zt[h], 0, 128)])
                                p.v(lambda e, h=h, kr=kr: e.tensor_scalar_mul(
                                    LTb[h][:, :], Zt[h][:, 0:128], rstdc[:, kr[1]:kr[2]]),
                                    r=[L(Zt[h], 0, 128), kr], w=[L(LTb[h])])
                                for i, qc in enumerate((2 * h, 2 * h + 1)):
                                    qap, qrd = QTs(qc, c)
                                    ss_ = sbA if i == 0 else sbB
                                    p.t(lambda e, qap=qap, al=al, i=i, ss_=ss_: e.matmul(
                                        P[al][:, 0:256], qap, Sbf[:, ss_[3]:ss_[3] + 256],
                                        start=(i == 0), stop=False),
                                        r=[qrd, ss_[:3]], w=[PR(al)])
                                p.t(lambda e, h=h, al=al, Uh=Uh: e.matmul(
                                    P[al][:, 0:256], LTb[h][:, :], Uh[:, 0:256],
                                    start=False, stop=True),
                                    r=[L(LTb[h]), L(Uh, 0, 256)], w=[PR(al)])
                                p.v(lambda e, h=h, al=al, qr=qr: e.tensor_scalar_mul(
                                    Ot1[h][:, :], P[al][:, 0:256], rstdc[:, qr[1]:qr[2]]),
                                    r=[PR(al), qr], w=[L(Ot1[h])])
                                p.v(lambda e, h=h: e.tensor_mul(
                                    Ot2[h % 2][:, :], Ot1[h][:, :], Ot1[h][:, :]),
                                    r=[L(Ot1[h])], w=[L(Ot2[h % 2])])
                                p.v(lambda e, h=h: e.reduce_sum(
                                    Zt[h][:, 0:1], Ot2[h % 2][:, :], AX.X),
                                    r=[L(Ot2[h % 2])], w=[L(Zt[h], 0, 1)])
                                p.v(lambda e, h=h: e.tensor_scalar(
                                    Zt[h][:, 1:2], Zt[h][:, 0:1], 1.0 / 256, EPS_RMS,
                                    ALU.mult, ALU.add),
                                    r=[L(Zt[h], 0, 1)], w=[L(Zt[h], 1, 2)])
                                p.s(lambda e, h=h: e.activation(
                                    Zt[h][:, 2:3], Zt[h][:, 1:2], AF.Ln),
                                    r=[L(Zt[h], 1, 2)], w=[L(Zt[h], 2, 3)])
                                p.s(lambda e, h=h: e.activation(
                                    Zt[h][:, 3:4], Zt[h][:, 2:3], AF.Exp, scale=-0.5),
                                    r=[L(Zt[h], 2, 3)], w=[L(Zt[h], 3, 4)])
                                p.v(lambda e, h=h: e.tensor_scalar_mul(
                                    Ot2[h % 2][:, :], Ot1[h][:, :], Zt[h][:, 3:4]),
                                    r=[L(Ot1[h]), L(Zt[h], 3, 4)], w=[L(Ot2[h % 2])])
                                p.v(lambda e, h=h: e.tensor_mul(
                                    On[h][:, :], Ot2[h % 2][:, :], gbc[:, :]),
                                    r=[L(Ot2[h % 2]), L(gbc)], w=[L(On[h])])
                                for jj in range(2):
                                    pbt = 6 + ((h + jj) % 2)
                                    oc = 2 * h + jj
                                    p.t(lambda e, h=h, jj=jj, pbt=pbt: e.transpose(
                                        PB[pbt - 6][:, 0:128], On[h][:, jj * 128:(jj + 1) * 128],
                                        identb[:, :]),
                                        r=[L(On[h], jj * 128, (jj + 1) * 128), L(identb)],
                                        w=[PR(pbt)])
                                    p.v(lambda e, c=c, oc=oc, pbt=pbt: e.tensor_copy(
                                        otb[c % 2][:, oc * 128:(oc + 1) * 128],
                                        PB[pbt - 6][:, 0:128]),
                                        r=[PR(pbt)],
                                        w=[L(otb[c % 2], oc * 128, (oc + 1) * 128)])
                                for i, qc in enumerate((2 * h, 2 * h + 1)):
                                    kap, krd = KTs(qc, c)
                                    pbt = 6 + ((h + i) % 2)
                                    p.t(lambda e, kap=kap, pbt=pbt: e.transpose(
                                        PB[pbt - 6][:, 0:128], kap, identb[:, :]),
                                        r=[krd, L(identb)], w=[PR(pbt)])
                                    p.v(lambda e, h=h, pbt=pbt, Zh=Zh: e.tensor_copy(
                                        Zh[:, 128:256], PB[pbt - 6][:, 0:128]),
                                        r=[PR(pbt)], w=[L(Zh, 128, 256)])
                                    bank = al if i == 0 else be
                                    p.t(lambda e, h=h, bank=bank, Zh=Zh, Uh=Uh: e.matmul(
                                        P[bank][:, 0:256], Zh[:, 128:256], Uh[:, 256:512],
                                        start=True, stop=True),
                                        r=[L(Zh, 128, 256), L(Uh, 256, 512)], w=[PR(bank)])
                                    ss_f = sA if i == 0 else sB
                                    ss_b = (sbA if i == 0 else sbB)
                                    st_ = stmp[i]
                                    p.v(lambda e, bank=bank, ss_f=ss_f, st_=st_: e.tensor_add(
                                        st_[:, :], Sf[:, ss_f[1]:ss_f[2]], P[bank][:, 0:256]),
                                        r=[ss_f, PR(bank)], w=[L(st_)])
                                    p.v(lambda e, ss_f=ss_f, st_=st_: e.tensor_copy(
                                        Sf[:, ss_f[1]:ss_f[2]], st_[:, :]),
                                        r=[L(st_)], w=[ss_f])
                                    p.v(lambda e, ss_b=ss_b, st_=st_: e.tensor_copy(
                                        Sbf[:, ss_b[3]:ss_b[3] + 256], st_[:, :]),
                                        r=[L(st_)], w=[ss_b[:3]])
                            if "a_x0" in dbg_out and b == 0 and a == 0 and c == 0:
                                p.g(lambda e: e.dma_start(dbg_out["a_x0"], Xb[0][:, :]),
                                    r=[L(Xb[0])], w=[])
                                p.g(lambda e: e.dma_start(dbg_out["a_u0"], Ub[0][:, :]),
                                    r=[L(Ub[0])], w=[])
                                p.g(lambda e: e.dma_start(dbg_out["a_z0"], Zb[0][:, :]),
                                    r=[L(Zb[0])], w=[])
                                p.g(lambda e: e.dma_start(dbg_out["a_on"], On[0][:, :]),
                                    r=[L(On[0])], w=[])
                                p.y(lambda e: e.dma_start(dbg_out["a_sf"], Sf[:, :]),
                                    r=[L(Sf)], w=[])
                            for dh in range(2):
                                for oc in range(8):
                                    p.t(lambda e, c=c, oc=oc, dh=dh: e.matmul(
                                        P[0][:, :], otb[c % 2][:, oc * 128:(oc + 1) * 128],
                                        wo_sb[:, oc * 1024 + dh * 512: oc * 1024 + (dh + 1) * 512],
                                        start=(oc == 0), stop=(oc == 7)),
                                        r=[L(otb[c % 2], oc * 128, (oc + 1) * 128),
                                           L(B, oc * 1024 + dh * 512, oc * 1024 + (dh + 1) * 512)],
                                        w=[PR(0)])
                                lo = c * 1024 + dh * 512
                                p.v(lambda e, lo=lo: e.tensor_add(
                                    Rg[1][:, lo:lo + 512], Rg[0][:, lo:lo + 512], P[0][:, :]),
                                    r=[L(Rg[0], lo, lo + 512), PR(0)],
                                    w=[L(Rg[1], lo, lo + 512)])
                        dst_ap = dst.rearrange("(gg s pp) d -> gg pp s d", gg=NG, s=SPG, pp=128)[g]
                        p.y(lambda e, dap=dst_ap: e.dma_start(
                            dap, Rg[1].ap().rearrange("p (s d) -> p s d", s=SPG)),
                            r=[L(Rg[1])], w=[(dst_name, g, g + 1)])
                return

            moe_pass(0, dr["x0"], "x0", rb1 if npass > 1 else out,
                     "rb1" if npass > 1 else "out", parity=0)
            if npass == 5:
                pass
            if npass == 5:
                pass
            elif npass == 2:
                attn_pass(rb1, "rb1", out, "out")
            elif npass == 4:
                moe_pass(1, rb1, "rb1", out, "out", parity=1)
            elif npass >= 3:
                attn_pass(rb1, "rb1", rb2, "rb2")
                moe_pass(1, rb2, "rb2", out, "out", parity=1)
            p.wait_only("sync", [("yq", p.counts["yq"])])
            p.wait_only("gpsimd", [("gq", p.counts["gq"])])


        p1 = Plan(nc, ctx)
        p1.phase = 1
        emit_all(p1)
        p = Plan(nc, ctx)
        p.phase = 2
        p.consumed = p1.consumed
        emit_all(p)
        block = ctx.enter_context(nc.Block())
        p.replay(block)

    return nc


def make_consts():
    c = {}
    c["ident"] = np.eye(128, dtype=np.float32)
    c["triu_inc"] = np.triu(np.ones((128, 128), np.float32))
    c["triu_str"] = np.triu(np.ones((128, 128), np.float32), 1)
    c["triu32"] = np.triu(np.ones((32, 32), np.float32), 1)
    c["iota8"] = np.broadcast_to(np.arange(8, dtype=np.float32), (128, 8)).copy()
    c["ones_col"] = np.ones((128, 1), np.float32)
    c["ones1"] = np.ones((1, 128), np.float32)
    return c


def make_weight_maps(inp):
    m = {}
    W1 = np.asarray(inp["L0_ffn_W1"], dtype=np.float32)
    m["w1s"] = np.ascontiguousarray(
        W1.reshape(E, 8, 128, 8, 512).transpose(0, 3, 2, 1, 4))
    W2 = np.asarray(inp["L0_ffn_W2"], dtype=np.float32)
    m["w2s"] = np.ascontiguousarray(
        W2.reshape(E, 4, 8, 128, 2, 512).transpose(0, 4, 1, 3, 2, 5))
    for nm, key in [("wqs", "L0_attn_Wq"), ("wks", "L0_attn_Wk"),
                    ("wvs", "L0_attn_Wv"), ("wos", "L0_attn_Wo")]:
        W = np.asarray(inp[key], dtype=np.float32)
        m[nm] = np.ascontiguousarray(W.reshape(8, 128, 1024).transpose(1, 0, 2))
    Wb = np.asarray(inp["L0_attn_Wbeta"], dtype=np.float32)
    m["wbs"] = np.ascontiguousarray(Wb.reshape(8, 128, 4).transpose(1, 0, 2))
    Wr = np.asarray(inp["L0_ffn_Wr"], dtype=np.float32)
    m["wrs"] = np.ascontiguousarray(Wr.reshape(8, 128, 8).transpose(1, 0, 2))
    b1 = np.asarray(inp["L0_ffn_b1"], dtype=np.float32)
    m["b1s"] = np.ascontiguousarray(b1.reshape(8, 32, 128).transpose(2, 0, 1))
    m["b2s"] = np.ascontiguousarray(np.asarray(inp["L0_ffn_b2"], dtype=np.float32))
    gn = np.asarray(inp["L0_attn_g_norm"], dtype=np.float32)
    m["gb"] = np.ascontiguousarray(np.broadcast_to(gn, (128, 256)))
    m.update(make_consts())
    return m


def make_in_maps(inp):
    shared = make_weight_maps(inp)
    x = np.asarray(inp["L0_attn_x"], dtype=np.float32)
    maps = []
    for c in range(NCORES):
        d = dict(shared)
        d["x0"] = np.ascontiguousarray(x[2 * c:2 * c + 2].reshape(TOK, D))
        cm = np.zeros((8, 1), np.float32)
        cm[:c] = 1.0
        d["coremask"] = cm
        maps.append(d)
    return maps


_NC_CACHE = {}
KN_NPASS = 3
KN_DBG = ()


def kernel(**inputs):
    key = (KN_NPASS, tuple(KN_DBG), PRUNE, SIZE_PROBE_G)
    if key not in _NC_CACHE:
        _NC_CACHE[key] = build(npass=KN_NPASS, dbg=KN_DBG)
    nc = _NC_CACHE[key]
    maps = make_in_maps(inputs)
    res = run_bass_kernel_spmd(nc, maps, core_ids=list(range(NCORES)))
    outs = [np.asarray(res.results[c]["out"]) for c in range(NCORES)]
    full = np.concatenate([o.reshape(2, 2048, 1024) for o in outs], axis=0)
    if KN_DBG:
        return full, res
    return full


# version 05
# speedup vs baseline: 1.0067x; 1.0067x over previous
import numpy as np
import ml_dtypes
from contextlib import ExitStack

import concourse.bass as bass
import concourse.mybir as mybir
from concourse.bass_utils import run_bass_kernel_spmd

F32 = mybir.dt.float32
BF16 = mybir.dt.bfloat16
AF = mybir.ActivationFunctionType
ALU = mybir.AluOpType
AX = mybir.AxisListType

NCORES = 8
TOK = 4096
D = 1024
E = 8
FF = 4096
NH = 4
DK = 256
NSUB = 32
G = 512
NG = 8
SPG = 4
CAP = 5120.0
EPS_LN = 1e-5
EPS_RMS = 1e-5
BIG = 1e9


class Plan:
    ENGS = ("tensor", "vector", "scalar", "gpsimd", "sync")

    def __init__(self, nc, ctx):
        self.nc = nc
        self.ctx = ctx
        self.events = {e: [] for e in self.ENGS}
        self.sems = {}
        self.counts = {}
        self.waited = {}
        self.writes = {}
        self.reads = {}
        self.phase = 1
        self.ordinal = {}
        self.consumed = set()
        self.last_consumed_tok = {}
        self.psum_read = {}

    def sem(self, name, handle=None):
        self.sems[name] = handle if handle is not None else \
            self.ctx.enter_context(self.nc.semaphore(name))
        self.counts[name] = 0

    def _dep_waits(self, eng, reads, writes):
        waits = []
        for nm, lo, hi in reads:
            for (l2, h2, tok) in self.writes.get(nm, ()):
                if lo < h2 and l2 < hi:
                    waits.append(tok)
        for nm, lo, hi in writes:
            for (l2, h2, tok) in self.writes.get(nm, ()):
                if lo < h2 and l2 < hi:
                    waits.append(tok)
            for (l2, h2, tok) in self.reads.get(nm, ()):
                if lo < h2 and l2 < hi:
                    waits.append(tok)
        if eng == "tensor":
            waits = [w for w in waits if w[0] != "ts"]
        if eng == "sync":
            waits = [w for w in waits if w[0] != "yq"]
        if eng == "gpsimd":
            waits = [w for w in waits if w[0] != "gq"]
        return waits

    def _record(self, reads, writes, tok):
        for nm, lo, hi in writes:
            lst = self.writes.setdefault(nm, [])
            lst[:] = [e for e in lst if not (lo <= e[0] and e[1] <= hi)]
            lst.append((lo, hi, tok))
            rl = self.reads.setdefault(nm, [])
            rl[:] = [e for e in rl if not (lo <= e[0] and e[1] <= hi)]
        for nm, lo, hi in reads:
            self.reads.setdefault(nm, []).append((lo, hi, tok))

    def op(self, eng, fn, reads=(), writes=(), extra=(), inc=None, inc_by=None):
        if inc is None:
            inc = {"tensor": "ts", "vector": "vs", "scalar": "ss",
                   "gpsimd": "gq", "sync": "yq"}[eng]
        if inc_by is None:
            inc_by = 16 if eng in ("gpsimd", "sync") else 1
        extra2 = list(extra)
        if eng in ("vector", "scalar"):
            other = "scalar" if eng == "vector" else "vector"
            for nm, lo, hi in reads:
                if isinstance(nm, str) and nm.startswith("psum"):
                    t_ = self.psum_read.get((other, nm))
                    if t_ is not None:
                        extra2.append(t_)
        self.ordinal[inc] = self.ordinal.get(inc, 0) + 1
        my_ord = (inc, self.ordinal[inc])
        if PRUNE and self.phase == 2 and inc not in ("gq", "yq", "cc") and my_ord not in self.consumed:
            tok = self.last_consumed_tok.get(inc, (inc, 0))
            do_inc = False
        else:
            self.counts[inc] += inc_by
            tok = (inc, self.counts[inc])
            self.last_consumed_tok[inc] = tok
            do_inc = True
        waits = extra2 + self._dep_waits(eng, reads, writes)
        need = []
        for w in waits:
            if w is None:
                continue
            s, v, *rest = w
            if self.phase == 1:
                if rest:
                    self.consumed.add((s, rest[0]))
            if v <= 0 or self.waited.get((eng, s), 0) >= v:
                continue
            self.waited[(eng, s)] = v
            need.append((s, v))

        def ev(e, need=tuple(need), fn=fn, inc=inc, inc_by=inc_by, do_inc=do_inc):
            for s, v in need:
                e.wait_ge(self.sems[s], v)
            r = fn(e)
            if do_inc:
                r.then_inc(self.sems[inc], inc_by)

        self.events[eng].append(ev)
        tok3 = (tok[0], tok[1], my_ord)
        self._record(reads, writes, tok3)
        if eng in ("vector", "scalar"):
            for nm, lo, hi in reads:
                if isinstance(nm, str) and nm.startswith("psum"):
                    self.psum_read[(eng, nm)] = tok3
        return tok3

    def t(self, fn, r=(), w=(), extra=()):
        return self.op("tensor", fn, r, w, extra)

    def v(self, fn, r=(), w=(), extra=()):
        return self.op("vector", fn, r, w, extra)

    def s(self, fn, r=(), w=(), extra=()):
        return self.op("scalar", fn, r, w, extra)

    def g(self, fn, r=(), w=(), extra=()):
        return self.op("gpsimd", fn, r, w, extra)

    def y(self, fn, r=(), w=(), extra=()):
        return self.op("sync", fn, r, w, extra)

    def last_tok(self, sem):
        return (sem, self.counts.get(sem, 0), (sem, self.ordinal.get(sem, 0)))

    def wait_only(self, eng, waits):
        need = []
        for wv in waits:
            if wv is None:
                continue
            s, v, *rest = wv
            if self.phase == 1 and rest:
                self.consumed.add((s, rest[0]))
            if v <= 0 or self.waited.get((eng, s), 0) >= v:
                continue
            self.waited[(eng, s)] = v
            need.append((s, v))
        if not need:
            return

        def ev(e, need=tuple(need)):
            for s, v in need:
                e.wait_ge(self.sems[s], v)

        self.events[eng].append(ev)

    def replay(self, block):
        plan = self

        def mk(name):
            def run(e):
                for ev in plan.events[name]:
                    ev(e)
            return run

        block.tensor(mk("tensor"))
        block.vector(mk("vector"))
        block.scalar(mk("scalar"))
        block.gpsimd(mk("gpsimd"))
        block.sync(mk("sync"))


PRUNE = False

SIZE_PROBE_G = 99

def build(npass=3, dbg=()):
    nc = bass.Bass("TRN2", target_bir_lowering=False)

    dr = {}
    def din(name, shape):
        dr[name] = nc.dram_tensor(name, list(shape), F32, kind="ExternalInput").ap()

    din("x0", (TOK, D))
    din("w1s", (E, 8, 128, 8, 512))
    din("w2s", (E, 2, 4, 128, 8, 512))
    din("wqs", (128, 8, 1024))
    din("wks", (128, 8, 1024))
    din("wvs", (128, 8, 1024))
    din("wos", (128, 8, 1024))
    din("wbs", (128, 8, 4))
    din("wrs", (128, 8, 8))
    din("b1s", (128, 8, 32))
    din("b2s", (E, D))
    din("gb", (128, 256))
    din("coremask", (8, 1))
    din("ident", (128, 128))
    din("triu_inc", (128, 128))
    din("triu_str", (128, 128))
    din("triu32", (32, 32))
    din("iota8", (128, 8))
    din("ones_col", (128, 1))
    din("ones1", (1, 128))

    out = nc.dram_tensor("out", [TOK, D], F32, kind="ExternalOutput").ap()
    rb1 = nc.dram_tensor("rb1", [TOK, D], F32, kind="Internal").ap()
    w1c = nc.dram_tensor("w1c", [E, 8, 128, 8, 512], BF16, kind="Internal").ap()
    w2c = nc.dram_tensor("w2c", [E, 2, 4, 128, 8, 512], BF16, kind="Internal").ap()
    rb2 = nc.dram_tensor("rb2", [TOK, D], F32, kind="Internal").ap()
    trow = nc.dram_tensor("trow", [1, 512], F32, kind="Internal").ap()
    orow = nc.dram_tensor("orow", [2, 32, 8], F32, kind="Internal").ap()
    cc_in = nc.dram_tensor("cc_in", [1, 16], F32, kind="Internal").ap()
    cc_out = nc.dram_tensor("cc_out", [8, 16], F32, kind="Internal").ap()

    dbg_out = {}
    for nm, shape in dbg:
        dbg_out[nm] = nc.dram_tensor(nm, list(shape), F32, kind="ExternalOutput").ap()

    ctx = ExitStack()
    with ctx:
        def sb(name, shape, dt):
            return ctx.enter_context(nc.sbuf_tensor(name, list(shape), dt))

        def ps(name, shape, dt):
            return ctx.enter_context(nc.psum_tensor(name, list(shape), dt))

        A = sb("A", (128, 24576), BF16)
        B = sb("B", (128, 16384), BF16)
        Rg = [sb(f"Rg{i}", (128, 4096), F32) for i in range(2)]
        xls = [sb(f"xls{i}", (128, 1024), BF16) for i in range(2)]
        xlg = [sb(f"xlg{i}", (128, 4096), BF16) for i in range(2)]
        SCRBF = sb("SCRBF", (128, 4096), BF16)
        Vsb = SCRBF
        SCRF = sb("SCRF", (128, 1024), F32)
        Sf = sb("Sf", (128, 2048), F32)
        stmp = [sb(f"stmp{i}", (128, 256), F32) for i in range(2)]
        Sbf = SCRF.ap().bitcast(BF16)
        Zb = [sb(f"Zb{h}", (128, 9 * 128), BF16) for h in range(NH)]
        Xb = [sb(f"Xb{h}", (128, 512), BF16) for h in range(NH)]
        Ub = [sb(f"Ub{h}", (128, 512), BF16) for h in range(NH)]
        LTb = [sb(f"LTb{h}", (128, 128), BF16) for h in range(NH)]
        Zt = [sb(f"Zt{h}", (128, 256), F32) for h in range(NH)]
        Ot1 = [sb(f"Ot1{h}", (128, 256), F32) for h in range(NH)]
        Ot2 = [sb(f"Ot2{h}", (128, 256), F32) for h in range(2)]
        On = [sb(f"On{h}", (128, 256), BF16) for h in range(NH)]
        otb = [sb(f"otb{i}", (128, 1024), BF16) for i in range(2)]
        krbs = [sb(f"krb{h}", (128, 128), F32) for h in range(NH)]
        qsq = [sb(f"qsq{i}", (128, 512), F32) for i in range(1)]
        stat_mu = [sb(f"smu{i}", (128, 32), F32) for i in range(2)]
        stat_rs = [sb(f"srs{i}", (128, 32), F32) for i in range(2)]
        att_mu = sb("amu", (128, 32), F32)
        att_rs = sb("ars", (128, 32), F32)
        msum = sb("msum", (128, 32), F32)
        mus2 = sb("mus2", (128, 32), F32)
        var_c = sb("var_c", (128, 32), F32)
        var2 = sb("var2", (128, 32), F32)
        rsq = [SCRF]
        w_all = sb("w_all", (128, 256), F32)
        wT_all = SCRBF
        oh1_all = sb("oh1_all", (128, 256), BF16)
        oh2_all = sb("oh2_all", (128, 256), BF16)
        lgb = [sb(f"lgb{i}", (128, 8), F32) for i in range(4)]
        top = [sb(f"top{i}", (128, 48), F32) for i in range(4)]
        wcols = sb("wcols", (128, 64), F32)
        b1c = sb("b1c", (128, 256), F32)
        b2c = sb("b2c", (8, 1024), BF16)
        gbc = sb("gbc", (128, 256), F32)
        wrb = sb("wrb", (128, 64), BF16)
        wbb = sb("wbb", (128, 32), BF16)
        identb = sb("identb", (128, 128), BF16)
        triuIb = sb("triuIb", (128, 128), BF16)
        triuSb = sb("triuSb", (128, 128), BF16)
        wbt = sb("wbt", (128, 8), BF16)
        tri32 = sb("tri32", (32, 32), F32)
        iota = sb("iota", (128, 8), F32)
        onesc = sb("onesc", (128, 1), F32)
        onescb = sb("onescb", (128, 1), BF16)
        ones1b = sb("ones1b", (1, 128), BF16)
        ones1 = sb("ones1_sb", (1, 128), F32)
        onesc32 = sb("onesc32", (32, 1), F32)
        cmask = sb("cmask", (8, 1), F32)
        rowpool = sb("rowpool", (1, 1024), F32)
        nrowb = rowpool.ap().bitcast(BF16)
        totrow = rowpool[0:1, 0:512]
        tot32 = [sb(f"tot32{i}", (32, 8), F32) for i in range(2)]
        offs32 = [sb(f"offs32{i}", (32, 8), F32) for i in range(2)]
        cntrow = sb("cntrow", (1, 16), F32)
        gath = sb("gath", (8, 16), F32)
        baser = sb("baser", (1, 16), F32)

        rstdc = sb("rstdc", (128, 64), F32)
        beta_s = sb("beta_s", (128, 16), F32)
        ebuf = sb("ebuf", (128, 48), F32)
        sumsq = sb("sumsq", (128, 32), F32)
        dbg_lg = Sf[:, 0:256]

        P = [ps(f"P{i}", (128, 512), F32) for i in range(6)]
        PB = [ps(f"PB{i}", (128, 1024), BF16) for i in range(2)]

        sem_handles = {s_: ctx.enter_context(nc.semaphore(s_))
                       for s_ in ("ts", "vs", "ss", "gq", "yq", "cc")}

        def emit_all(p):
            for s_ in ("ts", "vs", "ss", "gq", "yq", "cc"):
                p.sem(s_, sem_handles[s_])

            def R(t_, lo=0, hi=1 << 30):
                return (t_, lo, hi)

            def L(tensor, lo=0, hi=1 << 30):
                return (id(tensor), lo, hi)

            PSUM = [("psum%d" % i, 0, 1) for i in range(8)]

            def PR(i):
                return PSUM[i]

            for t_, nm in [(tri32, "triu32"), (iota, "iota8"), (onesc, "ones_col"),
                           (ones1, "ones1"), (cmask, "coremask"), (b1c, "b1s"),
                           (gbc, "gb")]:
                p.y(lambda e, t_=t_, nm=nm: e.dma_start(t_[:, :], dr[nm]), w=[L(t_)])
            p.y(lambda e: e.dma_start(onesc32[:, :], dr["ones_col"][0:32, :]), w=[L(onesc32)])
            for t_, nm in [(identb, "ident"), (wrb, "wrs"), (wbb, "wbs"), (b2c, "b2s"),
                           (triuIb, "triu_inc"), (triuSb, "triu_str"),
                           (onescb, "ones_col"), (ones1b, "ones1")]:
                p.g(lambda e, t_=t_, nm=nm: e.dma_start(t_[:, :], dr[nm]), w=[L(t_)])

            def ln_make_xlg(src_tok_fn, rbuf_t, gslot, smu, srs, g):
                last = None
                for sl in range(SPG):
                    si = g * SPG + sl
                    xb = xls[sl % 2]
                    t_xl = p.v(lambda e, sl=sl, si=si, xb=xb, rbuf_t=rbuf_t: e.tensor_scalar(
                        xb[:, :], rbuf_t[:, sl * 1024:(sl + 1) * 1024],
                        smu[:, si:si + 1], srs[:, si:si + 1], ALU.subtract, ALU.mult),
                        r=[L(rbuf_t, sl * 1024, (sl + 1) * 1024), L(smu, si, si + 1),
                           L(srs, si, si + 1)],
                        w=[L(xb)])
                    for dc in range(8):
                        pb = 6 + (dc % 2)
                        p.t(lambda e, xb=xb, dc=dc, pb=pb: e.transpose(
                            PB[pb - 6][:, 0:128], xb[:, dc * 128:(dc + 1) * 128], identb[:, :]),
                            r=[L(xb, dc * 128, (dc + 1) * 128), L(identb)], w=[PR(pb)])
                        last = p.v(lambda e, dc=dc, sl=sl, gslot=gslot, pb=pb: e.tensor_copy(
                            xlg[gslot][:, dc * 512 + sl * 128: dc * 512 + (sl + 1) * 128],
                            PB[pb - 6][:, 0:128]),
                            r=[PR(pb)],
                            w=[L(xlg[gslot], dc * 512 + sl * 128, dc * 512 + (sl + 1) * 128)])
                return last

            def group_rload(src, src_name, g, dst_t):
                src_ap = src.rearrange("(gg s pp) d -> gg pp s d", gg=NG, s=SPG, pp=128)[g]
                return p.y(lambda e, sap=src_ap, dst_t=dst_t: e.dma_start(
                    dst_t.ap().rearrange("p (s d) -> p s d", s=SPG), sap),
                    r=[(src_name, g, g + 1)], w=[L(dst_t)])

            def moe_pass(pi, src, src_name, dst, dst_name, parity):
                smu, srs = stat_mu[parity], stat_rs[parity]

                for g in range(NG):
                    rbuf_t = Rg[g % 2]
                    group_rload(src, src_name, g, rbuf_t)
                    for sl in range(SPG):
                        si = g * SPG + sl
                        rt_lo, rt_hi = sl * 1024, (sl + 1) * 1024
                        rq = rsq[0]
                        p.v(lambda e, si=si, rbuf_t=rbuf_t, rt_lo=rt_lo, rt_hi=rt_hi: e.reduce_sum(
                            msum[:, si:si + 1], rbuf_t[:, rt_lo:rt_hi], AX.X),
                            r=[L(rbuf_t, rt_lo, rt_hi)], w=[L(msum, si, si + 1)])
                        p.v(lambda e, si=si: e.tensor_scalar_mul(
                            smu[:, si:si + 1], msum[:, si:si + 1], 1.0 / D),
                            r=[L(msum, si, si + 1)], w=[L(smu, si, si + 1)])
                        p.v(lambda e, rq=rq, rbuf_t=rbuf_t, rt_lo=rt_lo, rt_hi=rt_hi: e.tensor_mul(
                            rq[:, :], rbuf_t[:, rt_lo:rt_hi], rbuf_t[:, rt_lo:rt_hi]),
                            r=[L(rbuf_t, rt_lo, rt_hi)], w=[("scrf", 0, 4096)])
                        p.v(lambda e, si=si, rq=rq: e.reduce_sum(
                            var_c[:, si:si + 1], rq[:, :], AX.X),
                            r=[("scrf", 0, 4096)], w=[L(var_c, si, si + 1)])
                        p.v(lambda e, si=si: e.tensor_scalar(
                            mus2[:, si:si + 1], smu[:, si:si + 1], smu[:, si:si + 1],
                            -EPS_LN, ALU.mult, ALU.add),
                            r=[L(smu, si, si + 1)], w=[L(mus2, si, si + 1)])
                        p.v(lambda e, si=si: e.scalar_tensor_tensor(
                            var_c[:, si:si + 1], var_c[:, si:si + 1], 1.0 / D,
                            mus2[:, si:si + 1], ALU.mult, ALU.subtract),
                            r=[L(var_c, si, si + 1), L(mus2, si, si + 1)],
                            w=[L(var_c, si, si + 1)])
                    c0 = g * SPG
                    p.s(lambda e, c0=c0: e.activation(
                        var2[:, c0:c0 + SPG], var_c[:, c0:c0 + SPG], AF.Ln),
                        r=[L(var_c, c0, c0 + SPG)], w=[L(var2, c0, c0 + SPG)])
                    p.s(lambda e, c0=c0: e.activation(
                        srs[:, c0:c0 + SPG], var2[:, c0:c0 + SPG], AF.Exp, scale=-0.5),
                        r=[L(var2, c0, c0 + SPG)], w=[L(srs, c0, c0 + SPG)])
                    ln_make_xlg(None, rbuf_t, g % 2, smu, srs, g)
                    if "d_xl" in dbg_out and pi == 0 and g == 0:
                        p.g(lambda e: e.dma_start(dbg_out["d_xl"], xlg[0][:, 0:512]),
                            r=[L(xlg[0], 0, 512)], w=[])
                    for sl in range(SPG):
                        si = g * SPG + sl
                        pbank = sl % 2
                        for dc in range(8):
                            p.t(lambda e, dc=dc, g=g, sl=sl, pbank=pbank: e.matmul(
                                P[pbank][:, 0:8],
                                xlg[g % 2][:, dc * 512 + sl * 128: dc * 512 + (sl + 1) * 128],
                                wrb[:, dc * 8:(dc + 1) * 8],
                                start=(dc == 0), stop=(dc == 7)),
                                r=[L(xlg[g % 2], dc * 512 + sl * 128, dc * 512 + (sl + 1) * 128),
                                   L(wrb)],
                                w=[PR(pbank)])
                        lg = lgb[sl]
                        p.v(lambda e, lg=lg, pbank=pbank: e.tensor_copy(lg[:, :], P[pbank][:, 0:8]),
                            r=[PR(pbank)], w=[L(lg)])
                        if "d_lg" in dbg_out and pi == 0:
                            p.v(lambda e, lg=lg, si=si: e.tensor_copy(
                                mus2[:, 0:8] if False else dbg_lg[:, si * 8:(si + 1) * 8], lg[:, :]),
                                r=[L(lg)], w=[L(dbg_lg, si * 8, (si + 1) * 8)])
                        tp = top[sl]
                        o1 = (id(oh1_all), si * 8, (si + 1) * 8)
                        o2 = (id(oh2_all), si * 8, (si + 1) * 8)
                        p.v(lambda e, tp=tp, lg=lg: e.reduce_max(tp[:, 0:1], lg[:, :], AX.X),
                            r=[L(lg)], w=[L(tp, 0, 1)])
                        p.v(lambda e, tp=tp, lg=lg: e.tensor_single_scalar(
                            tp[:, 8:16], lg[:, :], tp[:, 0:1], ALU.is_equal),
                            r=[L(lg), L(tp, 0, 1)], w=[L(tp, 8, 16)])
                        p.v(lambda e, tp=tp: e.tensor_scalar(
                            tp[:, 16:24], tp[:, 8:16], -BIG, BIG, ALU.mult, ALU.add),
                            r=[L(tp, 8, 16)], w=[L(tp, 16, 24)])
                        p.v(lambda e, tp=tp: e.tensor_add(tp[:, 24:32], tp[:, 16:24], iota[:, :]),
                            r=[L(tp, 16, 24), L(iota)], w=[L(tp, 24, 32)])
                        p.v(lambda e, tp=tp: e.tensor_reduce(tp[:, 1:2], tp[:, 24:32], AX.X, ALU.min),
                            r=[L(tp, 24, 32)], w=[L(tp, 1, 2)])
                        p.v(lambda e, tp=tp, si=si: e.tensor_single_scalar(
                            oh1_all[:, si * 8:(si + 1) * 8], iota[:, :], tp[:, 1:2], ALU.is_equal),
                            r=[L(iota), L(tp, 1, 2)], w=[o1])
                        p.v(lambda e, tp=tp, lg=lg, si=si: e.scalar_tensor_tensor(
                            tp[:, 16:24], oh1_all[:, si * 8:(si + 1) * 8], -BIG, lg[:, :],
                            ALU.mult, ALU.add),
                            r=[o1, L(lg)], w=[L(tp, 16, 24)])
                        p.v(lambda e, tp=tp: e.reduce_max(tp[:, 2:3], tp[:, 16:24], AX.X),
                            r=[L(tp, 16, 24)], w=[L(tp, 2, 3)])
                        p.v(lambda e, tp=tp: e.tensor_single_scalar(
                            tp[:, 32:40], tp[:, 16:24], tp[:, 2:3], ALU.is_equal),
                            r=[L(tp, 16, 24), L(tp, 2, 3)], w=[L(tp, 32, 40)])
                        p.v(lambda e, tp=tp: e.tensor_scalar(
                            tp[:, 40:48], tp[:, 32:40], -BIG, BIG, ALU.mult, ALU.add),
                            r=[L(tp, 32, 40)], w=[L(tp, 40, 48)])
                        p.v(lambda e, tp=tp: e.tensor_add(tp[:, 32:40], tp[:, 40:48], iota[:, :]),
                            r=[L(tp, 40, 48), L(iota)], w=[L(tp, 32, 40)])
                        p.v(lambda e, tp=tp: e.tensor_reduce(tp[:, 3:4], tp[:, 32:40], AX.X, ALU.min),
                            r=[L(tp, 32, 40)], w=[L(tp, 3, 4)])
                        p.v(lambda e, tp=tp, si=si: e.tensor_single_scalar(
                            oh2_all[:, si * 8:(si + 1) * 8], iota[:, :], tp[:, 3:4], ALU.is_equal),
                            r=[L(iota), L(tp, 3, 4)], w=[o2])
                        p.v(lambda e, tp=tp: e.tensor_sub(tp[:, 4:5], tp[:, 2:3], tp[:, 0:1]),
                            r=[L(tp, 0, 1), L(tp, 2, 3)], w=[L(tp, 4, 5)])
                        p.s(lambda e, tp=tp: e.activation(tp[:, 5:6], tp[:, 4:5], AF.Exp),
                            r=[L(tp, 4, 5)], w=[L(tp, 5, 6)])
                        p.v(lambda e, tp=tp: e.tensor_scalar_add(tp[:, 6:7], tp[:, 5:6], 1.0),
                            r=[L(tp, 5, 6)], w=[L(tp, 6, 7)])
                        p.v(lambda e, tp=tp, si=si: e.reciprocal(wcols[:, si:si + 1], tp[:, 6:7]),
                            r=[L(tp, 6, 7)], w=[L(wcols, si, si + 1)])
                        p.v(lambda e, tp=tp, si=si: e.tensor_mul(
                            wcols[:, 32 + si:33 + si], tp[:, 5:6], wcols[:, si:si + 1]),
                            r=[L(tp, 5, 6), L(wcols, si, si + 1)], w=[L(wcols, 32 + si, 33 + si)])
                        p.t(lambda e, si=si: e.matmul(
                            P[2][0:1, si * 8:(si + 1) * 8], onescb[:, :],
                            oh1_all[:, si * 8:(si + 1) * 8], start=(si == 0), stop=False),
                            r=[o1, L(onescb)], w=[PR(2)])
                        p.t(lambda e, si=si: e.matmul(
                            P[2][0:1, 256 + si * 8: 256 + (si + 1) * 8], onescb[:, :],
                            oh2_all[:, si * 8:(si + 1) * 8], start=False, stop=(si == NSUB - 1)),
                            r=[o2, L(onescb)], w=[PR(2)])

                p.v(lambda e: e.tensor_copy(rowpool[0:1, 0:512], P[2][0:1, 0:512]),
                    r=[PR(2)], w=[L(rowpool, 0, 2048)])
                p.g(lambda e: e.dma_start(trow, rowpool[0:1, 0:512]),
                    r=[L(rowpool, 0, 2048)], w=[("trow", 0, 512)])
                p.g(lambda e: e.dma_start(
                    tot32[0][:, :], trow[0:1, 0:256].rearrange("o (s e) -> (o s) e", s=32)),
                    r=[("trow", 0, 256)], w=[L(tot32[0])])
                p.g(lambda e: e.dma_start(
                    tot32[1][:, :], trow[0:1, 256:512].rearrange("o (s e) -> (o s) e", s=32)),
                    r=[("trow", 256, 512)], w=[L(tot32[1])])
                p.t(lambda e: e.matmul(P[3][0:32, 0:8], tri32[:, :], tot32[0][:, :],
                                       start=True, stop=False),
                    r=[L(tri32), L(tot32[0])], w=[PR(3)])
                p.t(lambda e: e.matmul(P[3][0:32, 8:16], tri32[:, :], tot32[1][:, :],
                                       start=False, stop=False),
                    r=[L(tri32), L(tot32[1])], w=[PR(3)])
                p.t(lambda e: e.matmul(P[4][0:1, 0:8], onesc32[:, :], tot32[0][:, :],
                                       start=True, stop=False),
                    r=[L(onesc32), L(tot32[0])], w=[PR(4)])
                p.t(lambda e: e.matmul(P[4][0:1, 8:16], onesc32[:, :], tot32[1][:, :],
                                       start=False, stop=True),
                    r=[L(onesc32), L(tot32[1])], w=[PR(4)])
                p.v(lambda e: e.tensor_copy(cntrow[:, :], P[4][0:1, 0:16]),
                    r=[PR(4)], w=[L(cntrow)])
                tok_ci = p.g(lambda e: e.dma_start(cc_in, cntrow[:, :]), r=[L(cntrow)], w=[])
                tok_cc = p.op("gpsimd", lambda e: e.collective_compute(
                    "AllGather", ALU.bypass, replica_groups=[list(range(8))],
                    ins=[cc_in], outs=[cc_out]), reads=[], writes=[],
                    extra=[tok_ci], inc="cc", inc_by=1)
                p.g(lambda e: e.dma_start(gath[:, :], cc_out), r=[], w=[L(gath)],
                    extra=[tok_cc])
                p.t(lambda e: e.matmul(P[4][0:1, 0:16], cmask[:, :], gath[:, :],
                                       start=True, stop=True),
                    r=[L(cmask), L(gath)], w=[PR(4)])
                p.v(lambda e: e.tensor_copy(baser[:, :], P[4][0:1, 0:16]),
                    r=[PR(4)], w=[L(baser)])
                p.t(lambda e: e.matmul(P[3][0:32, 0:8], ones1[:, 0:32], baser[:, 0:8],
                                       start=False, stop=False),
                    r=[L(ones1), L(baser, 0, 8)], w=[PR(3)])
                p.t(lambda e: e.matmul(P[3][0:32, 8:16], ones1[:, 0:32], baser[:, 8:16],
                                       start=False, stop=True),
                    r=[L(ones1), L(baser, 8, 16)], w=[PR(3)])
                p.v(lambda e: e.tensor_copy(offs32[0][:, :], P[3][0:32, 0:8]),
                    r=[PR(3)], w=[L(offs32[0])])
                p.v(lambda e: e.tensor_copy(offs32[1][:, :], P[3][0:32, 8:16]),
                    r=[PR(3)], w=[L(offs32[1])])
                for k_ in range(2):
                    p.g(lambda e, k_=k_: e.dma_start(orow[k_], offs32[k_][:, :]),
                        r=[L(offs32[k_])], w=[("orow", k_ * 256, (k_ + 1) * 256)])
                    p.g(lambda e, k_=k_: e.dma_start(
                        rowpool[0:1, 512 + 256 * k_: 768 + 256 * k_],
                        orow[k_].rearrange("s e -> (s e)")[None, :]),
                        r=[("orow", k_ * 256, (k_ + 1) * 256)],
                        w=[L(rowpool, 2048 + 1024 * k_, 3072 + 1024 * k_)])
                for si in range(NSUB):
                    pb = 4 + (si % 2)
                    o1 = (id(oh1_all), si * 8, (si + 1) * 8)
                    o2 = (id(oh2_all), si * 8, (si + 1) * 8)
                    p.t(lambda e, si=si, pb=pb: e.matmul(
                        P[pb][:, 0:8], triuIb[:, :], oh1_all[:, si * 8:(si + 1) * 8],
                        start=True, stop=False),
                        r=[L(triuIb), o1], w=[PR(pb)])
                    p.t(lambda e, si=si, pb=pb: e.matmul(
                        P[pb][:, 0:8], ones1[:, :],
                        rowpool.ap()[0:1, 512 + si * 8: 512 + (si + 1) * 8],
                        start=False, stop=False),
                        r=[L(ones1), L(rowpool, 2048 + si * 32, 2048 + (si + 1) * 32)], w=[PR(pb)])
                    p.t(lambda e, si=si, pb=pb: e.matmul(
                        P[pb][:, 8:16], triuIb[:, :], oh2_all[:, si * 8:(si + 1) * 8],
                        start=False, stop=False),
                        r=[L(triuIb), o2], w=[PR(pb)])
                    p.t(lambda e, si=si, pb=pb: e.matmul(
                        P[pb][:, 8:16], ones1[:, :],
                        rowpool.ap()[0:1, 768 + si * 8: 768 + (si + 1) * 8],
                        start=False, stop=True),
                        r=[L(ones1), L(rowpool, 3072 + si * 32, 3072 + (si + 1) * 32)], w=[PR(pb)])
                    tp = top[si % 4]
                    p.v(lambda e, tp=tp, pb=pb: e.tensor_single_scalar(
                        tp[:, 8:16], P[pb][:, 0:8], CAP, ALU.is_le),
                        r=[PR(pb)], w=[L(tp, 8, 16)])
                    p.v(lambda e, tp=tp, pb=pb: e.tensor_single_scalar(
                        tp[:, 16:24], P[pb][:, 8:16], CAP, ALU.is_le),
                        r=[PR(pb)], w=[L(tp, 16, 24)])
                    p.v(lambda e, tp=tp, si=si: e.tensor_mul(
                        tp[:, 24:32], oh1_all[:, si * 8:(si + 1) * 8], tp[:, 8:16]),
                        r=[o1, L(tp, 8, 16)], w=[L(tp, 24, 32)])
                    p.v(lambda e, tp=tp, si=si: e.tensor_scalar(
                        tp[:, 40:48], tp[:, 24:32], wcols[:, si:si + 1], 0.5, ALU.mult, ALU.mult),
                        r=[L(tp, 24, 32), L(wcols, si, si + 1)], w=[L(tp, 40, 48)])
                    p.v(lambda e, tp=tp, si=si: e.tensor_mul(
                        tp[:, 32:40], oh2_all[:, si * 8:(si + 1) * 8], tp[:, 16:24]),
                        r=[o2, L(tp, 16, 24)], w=[L(tp, 32, 40)])
                    p.v(lambda e, tp=tp, si=si: e.tensor_scalar(
                        tp[:, 24:32], tp[:, 32:40], wcols[:, 32 + si:33 + si], 0.5, ALU.mult, ALU.mult),
                        r=[L(tp, 32, 40), L(wcols, 32 + si, 33 + si)], w=[L(tp, 24, 32)])
                    p.v(lambda e, tp=tp, si=si: e.tensor_add(
                        w_all[:, si * 8:(si + 1) * 8], tp[:, 40:48], tp[:, 24:32]),
                        r=[L(tp, 40, 48), L(tp, 24, 32)], w=[L(w_all, si * 8, (si + 1) * 8)])
                    p.v(lambda e, si=si: e.tensor_copy(
                        wbt[:, :], w_all[:, si * 8:(si + 1) * 8]),
                        r=[L(w_all, si * 8, (si + 1) * 8)], w=[L(wbt)])
                    p.t(lambda e, si=si: e.transpose(
                        PB[0][0:8, 0:128], wbt[:, :], identb[:, :]),
                        r=[L(wbt), L(identb)], w=[PR(6)])
                    p.v(lambda e, si=si: e.tensor_copy(
                        wT_all[0:8, si * 128:(si + 1) * 128], PB[0][0:8, 0:128]),
                        r=[PR(6)], w=[L(wT_all, si * 128, (si + 1) * 128)])

                if "d_lg" in dbg_out and pi == 0:
                    p.y(lambda e: e.dma_start(dbg_out["d_lg"], dbg_lg[:, :]), r=[L(dbg_lg)], w=[])
                if "d_w" in dbg_out and pi == 0:
                    p.y(lambda e: e.dma_start(dbg_out["d_w"], w_all[:, :]), r=[L(w_all)], w=[])
                    p.y(lambda e: e.dma_start(dbg_out["d_counts"], gath[:, :]), r=[L(gath)], w=[])
                    p.y(lambda e: e.dma_start(dbg_out["d_mu"], smu[:, :]), r=[L(smu)], w=[])
                    p.y(lambda e: e.dma_start(dbg_out["d_rs"], srs[:, :]), r=[L(srs)], w=[])
                    p.g(lambda e: e.dma_start(dbg_out["d_oh1"], oh1_all[:, :]), r=[L(oh1_all)], w=[])
                    p.g(lambda e: e.dma_start(dbg_out["d_oh2"], oh2_all[:, :]), r=[L(oh2_all)], w=[])

                w1v = [A[:, i * 4096:(i + 1) * 4096] for i in range(3)]
                w2v = [A[:, (3 + i) * 4096:(4 + i) * 4096] for i in range(3)]

                w1sched = [(g_, e_, pn) for g_ in range(NG)
                           for e_ in range(E) for pn in range(8)]
                w2sched = [(g_, e_, dh, q) for g_ in range(NG)
                           for e_ in range(E) for dh in range(2) for q in range(4)]

                def emit_w1(k):
                    if k >= len(w1sched):
                        return
                    g_, e2, pn = w1sched[k]
                    slot = k % 3
                    if pi == 0 and g_ == 0:
                        p.g(lambda e, e2=e2, pn=pn, slot=slot: e.dma_start(
                            w1v[slot].rearrange("p (dc f) -> p dc f", dc=8),
                            dr["w1s"][e2, pn]),
                            r=[], w=[L(A, slot * 4096, (slot + 1) * 4096)])
                        p.y(lambda e, e2=e2, pn=pn, slot=slot: e.dma_start(
                            w1c[e2, pn],
                            w1v[slot].rearrange("p (dc f) -> p dc f", dc=8)),
                            r=[L(A, slot * 4096, (slot + 1) * 4096)],
                            w=[("w1c", e2 * 8 + pn, e2 * 8 + pn + 1)])
                    else:
                        p.y(lambda e, e2=e2, pn=pn, slot=slot: e.dma_start(
                            w1v[slot].rearrange("p (dc f) -> p dc f", dc=8),
                            w1c[e2, pn]),
                            r=[("w1c", e2 * 8 + pn, e2 * 8 + pn + 1)],
                            w=[L(A, slot * 4096, (slot + 1) * 4096)])

                def emit_w2(k):
                    if k >= len(w2sched):
                        return
                    g_, e2, dh, q = w2sched[k]
                    slot = k % 3
                    if pi == 0 and g_ == 0:
                        p.g(lambda e, e2=e2, dh=dh, q=q, slot=slot: e.dma_start(
                            w2v[slot].rearrange("p (j f) -> p j f", j=8),
                            dr["w2s"][e2, dh, q]),
                            r=[], w=[L(A, (3 + slot) * 4096, (4 + slot) * 4096)])
                        p.y(lambda e, e2=e2, dh=dh, q=q, slot=slot: e.dma_start(
                            w2c[e2, dh, q],
                            w2v[slot].rearrange("p (j f) -> p j f", j=8)),
                            r=[L(A, (3 + slot) * 4096, (4 + slot) * 4096)],
                            w=[("w2c", e2 * 8 + dh * 4 + q, e2 * 8 + dh * 4 + q + 1)])
                    else:
                        p.y(lambda e, e2=e2, dh=dh, q=q, slot=slot: e.dma_start(
                            w2v[slot].rearrange("p (j f) -> p j f", j=8),
                            w2c[e2, dh, q]),
                            r=[("w2c", e2 * 8 + dh * 4 + q, e2 * 8 + dh * 4 + q + 1)],
                            w=[L(A, (3 + slot) * 4096, (4 + slot) * 4096)])

                for k in range(3):
                    emit_w1(k)
                    emit_w2(k)

                for g in range(NG):
                    group_rload(src, src_name, g, Rg[0])
                    ln_make_xlg(None, Rg[0], g % 2, smu, srs, g)
                    if "d_xl2" in dbg_out and pi == 0 and g == 0:
                        p.g(lambda e: e.dma_start(dbg_out["d_xl2"], xlg[0][:, :]),
                            r=[L(xlg[0], 0, 4096)], w=[])
                    rcur = 0
                    for e_ in range(E):
                        for fc in range(32):
                            pn = fc // 4
                            slot1 = (g * 64 + e_ * 8 + pn) % 3
                            pbk = fc % 2
                            for dc in range(8):
                                p.t(lambda e, fc=fc, dc=dc, pbk=pbk, slot1=slot1, g=g: e.matmul(
                                    P[pbk][:, :],
                                    w1v[slot1][:, dc * 512 + (fc % 4) * 128: dc * 512 + (fc % 4) * 128 + 128],
                                    xlg[g % 2][:, dc * 512:(dc + 1) * 512],
                                    start=(dc == 0), stop=(dc == 7)),
                                    r=[L(A, slot1 * 4096 + dc * 512 + (fc % 4) * 128,
                                         slot1 * 4096 + dc * 512 + (fc % 4) * 128 + 128),
                                       L(xlg[g % 2], dc * 512, (dc + 1) * 512)],
                                    w=[PR(pbk)])
                            if fc % 4 == 3:
                                emit_w1(g * 64 + e_ * 8 + pn + 3)
                            p.s(lambda e, fc=fc, pbk=pbk, e_=e_: e.activation(
                                B[:, fc * 512:(fc + 1) * 512], P[pbk][:, :], AF.Gelu,
                                bias=b1c[:, e_ * 32 + fc: e_ * 32 + fc + 1]),
                                r=[PR(pbk), L(b1c, e_ * 32 + fc, e_ * 32 + fc + 1)],
                                w=[L(B, fc * 512, (fc + 1) * 512)])
                            if "d_h0" in dbg_out and pi == 0 and g == 0 and e_ == 0 and fc == 0:
                                p.g(lambda e: e.dma_start(dbg_out["d_h0"], B[:, 0:512]),
                                    r=[L(B, 0, 512)], w=[])
                                p.v(lambda e, pbk=pbk: e.tensor_copy(
                                    Ot1[0][:, :] and Sf[:, 1024:1536], P[pbk][:, :]),
                                    r=[PR(pbk)], w=[L(Sf, 1024, 1536)],
                                    extra=[p.last_tok("ss")])
                                p.y(lambda e: e.dma_start(dbg_out["d_hp"], Sf[:, 1024:1536]),
                                    r=[L(Sf, 1024, 1536)], w=[])
                        for dh in range(2):
                            for q in range(4):
                                slot2 = (g * 64 + e_ * 8 + dh * 4 + q) % 3
                                for j in range(8):
                                    fc = 8 * q + j
                                    for sl in range(SPG):
                                        p.t(lambda e, fc=fc, sl=sl, slot2=slot2, j=j, q=q: e.matmul(
                                            P[2 + sl][:, :],
                                            B[:, fc * 512 + sl * 128: fc * 512 + (sl + 1) * 128],
                                            w2v[slot2][:, j * 512:(j + 1) * 512],
                                            start=(q == 0 and j == 0), stop=(q == 3 and j == 7)),
                                            r=[L(B, fc * 512 + sl * 128, fc * 512 + (sl + 1) * 128),
                                               L(A, (3 + slot2) * 4096 + j * 512,
                                                 (3 + slot2) * 4096 + (j + 1) * 512)],
                                            w=[PR(2 + sl)])
                                emit_w2(g * 64 + e_ * 8 + dh * 4 + q + 3)
                            for sl in range(SPG):
                                si = g * SPG + sl
                                lo = sl * 1024 + dh * 512
                                p.v(lambda e, sl=sl, lo=lo, si=si, e_=e_, rcur=rcur: e.scalar_tensor_tensor(
                                    Rg[1 - rcur][:, lo:lo + 512],
                                    P[2 + sl][:, :], w_all[:, si * 8 + e_: si * 8 + e_ + 1],
                                    Rg[rcur][:, lo:lo + 512],
                                    ALU.mult, ALU.add),
                                    r=[PR(2 + sl), L(w_all, si * 8 + e_, si * 8 + e_ + 1),
                                       L(Rg[rcur], lo, lo + 512)],
                                    w=[L(Rg[1 - rcur], lo, lo + 512)])
                        if "d_r0" in dbg_out and pi == 0 and g == 0 and e_ == 0:
                            _rc = rcur
                            p.y(lambda e, rc=_rc: e.dma_start(
                                dbg_out["d_r0"], Rg[1 - rc][:, 0:1024]),
                                r=[L(Rg[1 - _rc], 0, 1024)], w=[])
                        rcur = 1 - rcur
                    for sl in range(SPG):
                        si = g * SPG + sl
                        for dh in range(2):
                            p.t(lambda e, si=si, dh=dh, sl=sl: e.matmul(
                                P[2 + sl][:, :], wT_all[0:8, si * 128:(si + 1) * 128],
                                b2c[:, dh * 512:(dh + 1) * 512],
                                start=True, stop=True),
                                r=[L(wT_all, si * 128, (si + 1) * 128), L(b2c)],
                                w=[PR(2 + sl)])
                            lo = sl * 1024 + dh * 512
                            p.v(lambda e, sl=sl, lo=lo, rcur=rcur: e.tensor_add(
                                Rg[1 - rcur][:, lo:lo + 512],
                                Rg[rcur][:, lo:lo + 512],
                                P[2 + sl][:, :]),
                                r=[PR(2 + sl), L(Rg[rcur], lo, lo + 512)],
                                w=[L(Rg[1 - rcur], lo, lo + 512)])
                    dst_ap = dst.rearrange("(gg s pp) d -> gg pp s d", gg=NG, s=SPG, pp=128)[g]
                    p.y(lambda e, dap=dst_ap, rcur=rcur: e.dma_start(
                        dap, Rg[1 - rcur].ap().rearrange("p (s d) -> p s d", s=SPG)),
                        r=[L(Rg[1 - rcur])], w=[(dst_name, g, g + 1)])
                return


            def attn_pass(src, src_name, dst, dst_name):
                wq_sb = A[:, 0:8192]
                wk_sb = A[:, 8192:16384]
                wv_sb = A[:, 16384:24576]
                wo_sb = B[:, 0:8192]
                for dst_t, lo, nm in [(wq_sb, 0, "wqs"), (wk_sb, 8192, "wks"),
                                      (wv_sb, 16384, "wvs")]:
                    p.g(lambda e, dst_t=dst_t, nm=nm: e.dma_start(
                        dst_t.rearrange("p (dc c) -> p dc c", dc=8), dr[nm]),
                        r=[], w=[L(A, lo, lo + 8192)])
                p.g(lambda e: e.dma_start(
                    wo_sb.rearrange("p (oc c) -> p oc c", oc=8), dr["wos"]),
                    r=[], w=[L(B, 0, 8192)])

                def QTs(qc, c=None):
                    lo = 8192 + qc * 512 + (0 if c is None else c * 128)
                    hi = 8192 + (qc + 1) * 512 if c is None else lo + 128
                    return B[:, lo:hi], L(B, lo, hi)

                def KTs(qc, c=None):
                    lo = 12288 + qc * 512 + (0 if c is None else c * 128)
                    hi = 12288 + (qc + 1) * 512 if c is None else lo + 128
                    return B[:, lo:hi], L(B, lo, hi)

                def proj(qc, wofs, dst_lo, gs, do_sq):
                    pb = qc % 2
                    for dc in range(8):
                        p.t(lambda e, qc=qc, dc=dc, pb=pb, gs=gs, wofs=wofs: e.matmul(
                            P[pb][:, :],
                            A[:, wofs + dc * 1024 + qc * 128: wofs + dc * 1024 + (qc + 1) * 128],
                            xlg[gs][:, dc * 512:(dc + 1) * 512],
                            start=(dc == 0), stop=(dc == 7)),
                            r=[L(A, wofs + dc * 1024 + qc * 128,
                                 wofs + dc * 1024 + (qc + 1) * 128),
                               L(xlg[gs], dc * 512, (dc + 1) * 512)],
                            w=[PR(pb)])
                    p.v(lambda e, pb=pb, dst_lo=dst_lo: e.tensor_copy(
                        B[:, dst_lo:dst_lo + 512], P[pb][:, :]),
                        r=[PR(pb)], w=[L(B, dst_lo, dst_lo + 512)])
                    if do_sq:
                        p.s(lambda e, dst_lo=dst_lo: e.activation(
                            qsq[0][:, :], B[:, dst_lo:dst_lo + 512], AF.Square),
                            r=[L(B, dst_lo, dst_lo + 512)], w=[L(qsq[0])])
                        h = qc // 2
                        base = 0 if wofs == 0 else 16
                        for sl in range(SPG):
                            p.t(lambda e, sl=sl, h=h, qc=qc, base=base: e.matmul(
                                P[2][:, base + h * 4 + sl: base + h * 4 + sl + 1],
                                qsq[0][:, sl * 128:(sl + 1) * 128], onesc[:, :],
                                start=(base == 0 and qc == 0 and sl == 0), stop=False),
                                r=[L(qsq[0], sl * 128, (sl + 1) * 128), L(onesc)],
                                w=[PR(2)])
                        if wofs != 0:
                            p.t(lambda e, qc=qc: e.matmul(
                                P[3][0:1, 0:512], onesc[:, :], qsq[0][:, :],
                                start=(qc % 2 == 0), stop=(qc % 2 == 1)),
                                r=[L(qsq[0]), L(onesc)], w=[PR(3)])
                            if qc % 2 == 1:
                                h = qc // 2
                                p.s(lambda e: e.activation(
                                    qsq[1][0:1, 0:512] if False else qsq[0][0:1, 0:512],
                                    P[3][0:1, 0:512], AF.Ln),
                                    r=[PR(3)], w=[L(qsq[0], 0, 512)])
                                p.s(lambda e, h=h: e.activation(
                                    nrowb[0:1, h * 512:(h + 1) * 512],
                                    qsq[0][0:1, 0:512], AF.Exp, scale=-0.5),
                                    r=[L(qsq[0], 0, 512)],
                                    w=[L(rowpool, h * 1024, (h + 1) * 1024)])

                for b in range(2):
                    p.v(lambda e: e.memset(Sf[:, :], 0.0), r=[], w=[L(Sf)])
                    p.v(lambda e: e.memset(Sbf[:, :], 0.0), r=[], w=[("scrf", 0, 4096)])
                    for a in range(4):
                        g = b * 4 + a
                        gs = g % 2
                        group_rload(src, src_name, g, Rg[0])
                        for sl in range(SPG):
                            si = g * SPG + sl
                            lo = sl * 1024
                            p.v(lambda e, si=si, lo=lo: e.reduce_sum(
                                msum[:, si % 4: si % 4 + 1], Rg[0][:, lo:lo + 1024], AX.X),
                                r=[L(Rg[0], lo, lo + 1024)], w=[L(msum, si % 4, si % 4 + 1)])
                            p.v(lambda e, si=si: e.tensor_scalar_mul(
                                att_mu[:, si:si + 1], msum[:, si % 4: si % 4 + 1], 1.0 / D),
                                r=[L(msum, si % 4, si % 4 + 1)], w=[L(att_mu, si, si + 1)])
                            for hf in range(2):
                                p.v(lambda e, lo=lo, hf=hf: e.tensor_mul(
                                    qsq[0][:, :], Rg[0][:, lo + hf * 512:lo + (hf + 1) * 512],
                                    Rg[0][:, lo + hf * 512:lo + (hf + 1) * 512]),
                                    r=[L(Rg[0], lo + hf * 512, lo + (hf + 1) * 512)],
                                    w=[L(qsq[0])])
                                p.v(lambda e, si=si, hf=hf: e.reduce_sum(
                                    mus2[:, 8 + hf: 9 + hf], qsq[0][:, :], AX.X),
                                    r=[L(qsq[0])], w=[L(mus2, 8 + hf, 9 + hf)])
                            p.v(lambda e, si=si: e.tensor_add(
                                var_c[:, si % 4: si % 4 + 1], mus2[:, 8:9], mus2[:, 9:10]),
                                r=[L(mus2, 8, 10)], w=[L(var_c, si % 4, si % 4 + 1)])
                            p.v(lambda e, si=si: e.tensor_scalar(
                                mus2[:, si % 4: si % 4 + 1], att_mu[:, si:si + 1],
                                att_mu[:, si:si + 1], -EPS_LN, ALU.mult, ALU.add),
                                r=[L(att_mu, si, si + 1)], w=[L(mus2, si % 4, si % 4 + 1)])
                            p.v(lambda e, si=si: e.scalar_tensor_tensor(
                                var_c[:, 4 + si % 4: 5 + si % 4], var_c[:, si % 4: si % 4 + 1],
                                1.0 / D, mus2[:, si % 4: si % 4 + 1], ALU.mult, ALU.subtract),
                                r=[L(var_c, si % 4, si % 4 + 1), L(mus2, si % 4, si % 4 + 1)],
                                w=[L(var_c, 4 + si % 4, 5 + si % 4)])
                        c0 = g * SPG
                        p.s(lambda e: e.activation(var2[:, 0:4], var_c[:, 4:8], AF.Ln),
                            r=[L(var_c, 4, 8)], w=[L(var2, 0, 4)])
                        p.s(lambda e, c0=c0: e.activation(
                            att_rs[:, c0:c0 + 4], var2[:, 0:4], AF.Exp, scale=-0.5),
                            r=[L(var2, 0, 4)], w=[L(att_rs, c0, c0 + 4)])
                        ln_make_xlg(None, Rg[0], gs, att_mu, att_rs, g)

                        for qc in range(8):
                            proj(qc, 0, 8192 + qc * 512, gs, True)
                        ev_q = p.v(lambda e: e.tensor_copy(sumsq[:, 0:16], P[2][:, 0:16]),
                                   r=[PR(2)], w=[L(sumsq, 0, 16)])
                        for qc in range(8):
                            proj(qc, 8192, 12288 + qc * 512, gs, True)
                        p.v(lambda e: e.tensor_copy(sumsq[:, 16:32], P[2][:, 16:32]),
                            r=[PR(2)], w=[L(sumsq, 16, 32)])
                        p.s(lambda e: e.activation(var2[:, 0:32], sumsq[:, 0:32], AF.Ln),
                            r=[L(sumsq, 0, 32)], w=[L(var2, 0, 32)])
                        p.s(lambda e: e.activation(rstdc[:, 0:32], var2[:, 0:32], AF.Exp,
                                                   scale=-0.5),
                            r=[L(var2, 0, 32)], w=[L(rstdc, 0, 32)])
                        p.v(lambda e: e.tensor_scalar_mul(rstdc[:, 32:48], rstdc[:, 16:32], -1.0),
                            r=[L(rstdc, 16, 32)], w=[L(rstdc, 32, 48)])
                        for sl in range(SPG):
                            for half in range(2):
                                pb = 4
                                for dc in range(8):
                                    p.t(lambda e, sl=sl, half=half, dc=dc, gs=gs: e.matmul(
                                        P[4][:, :],
                                        xlg[gs][:, dc * 512 + sl * 128: dc * 512 + (sl + 1) * 128],
                                        wv_sb[:, dc * 1024 + half * 512: dc * 1024 + (half + 1) * 512],
                                        start=(dc == 0), stop=(dc == 7)),
                                        r=[L(xlg[gs], dc * 512 + sl * 128, dc * 512 + (sl + 1) * 128),
                                           L(A, 16384 + dc * 1024 + half * 512,
                                             16384 + dc * 1024 + (half + 1) * 512)],
                                        w=[PR(4)])
                                p.v(lambda e, sl=sl, half=half: e.tensor_copy(
                                    Vsb[:, sl * 1024 + half * 512: sl * 1024 + (half + 1) * 512],
                                    P[4][:, :]),
                                    r=[PR(4)],
                                    w=[L(SCRBF, sl * 1024 + half * 512, sl * 1024 + (half + 1) * 512)])
                        for sl in range(SPG):
                            for dc in range(8):
                                p.t(lambda e, sl=sl, dc=dc, gs=gs: e.matmul(
                                    P[5][:, 0:4],
                                    xlg[gs][:, dc * 512 + sl * 128: dc * 512 + (sl + 1) * 128],
                                    wbb[:, dc * 4:(dc + 1) * 4],
                                    start=(dc == 0), stop=(dc == 7)),
                                    r=[L(xlg[gs], dc * 512 + sl * 128, dc * 512 + (sl + 1) * 128),
                                       L(wbb)],
                                    w=[PR(5)])
                            p.v(lambda e, sl=sl: e.tensor_scalar_mul(
                                ebuf[:, sl * 4:(sl + 1) * 4], P[5][:, 0:4], -1.0),
                                r=[PR(5)], w=[L(ebuf, sl * 4, (sl + 1) * 4)])
                            p.s(lambda e, sl=sl: e.activation(
                                ebuf[:, 16 + sl * 4: 16 + (sl + 1) * 4],
                                ebuf[:, sl * 4:(sl + 1) * 4], AF.Exp),
                                r=[L(ebuf, sl * 4, (sl + 1) * 4)],
                                w=[L(ebuf, 16 + sl * 4, 16 + (sl + 1) * 4)])
                            p.v(lambda e, sl=sl: e.tensor_scalar_add(
                                ebuf[:, 32 + sl * 4: 32 + (sl + 1) * 4],
                                ebuf[:, 16 + sl * 4: 16 + (sl + 1) * 4], 1.0),
                                r=[L(ebuf, 16 + sl * 4, 16 + (sl + 1) * 4)],
                                w=[L(ebuf, 32 + sl * 4, 32 + (sl + 1) * 4)])
                            p.v(lambda e, sl=sl: e.reciprocal(
                                beta_s[:, sl * 4:(sl + 1) * 4],
                                ebuf[:, 32 + sl * 4: 32 + (sl + 1) * 4]),
                                r=[L(ebuf, 32 + sl * 4, 32 + (sl + 1) * 4)],
                                w=[L(beta_s, sl * 4, (sl + 1) * 4)])
                            p.v(lambda e, sl=sl: e.tensor_mul(
                                rstdc[:, 48 + sl * 4: 48 + (sl + 1) * 4],
                                beta_s[:, sl * 4:(sl + 1) * 4],
                                rstdc[:, 16 + sl * 4: 16 + (sl + 1) * 4]),
                                r=[L(beta_s, sl * 4, (sl + 1) * 4), L(rstdc, 16 + sl * 4, 16 + (sl + 1) * 4)],
                                w=[L(rstdc, 48 + sl * 4, 48 + (sl + 1) * 4)])

                        if "a_ss" in dbg_out and b == 0 and a == 0:
                            p.y(lambda e: e.dma_start(dbg_out["a_ss"], sumsq[:, :]),
                                r=[L(sumsq)], w=[])
                            p.y(lambda e: e.dma_start(dbg_out["a_rs"], rstdc[:, :]),
                                r=[L(rstdc)], w=[])
                            p.y(lambda e: e.dma_start(dbg_out["a_bs"], beta_s[:, :]),
                                r=[L(beta_s)], w=[])
                            p.g(lambda e: e.dma_start(dbg_out["a_kr"], rowpool[0:1, :]),
                                r=[L(rowpool, 0, 4096)], w=[])
                            p.g(lambda e: e.dma_start(dbg_out["a_qt"], B[:, 8192:12288]),
                                r=[L(B, 8192, 12288)], w=[])
                            p.g(lambda e: e.dma_start(dbg_out["a_kt"], B[:, 12288:16384]),
                                r=[L(B, 12288, 16384)], w=[])
                            p.g(lambda e: e.dma_start(dbg_out["a_v"], Vsb[:, :]),
                                r=[L(SCRBF, 0, 4096)], w=[])
                        bankpair = {0: (0, 1), 1: (2, 3), 2: (4, 5), 3: (0, 1)}
                        for c in range(4):
                            for h in range(NH):
                                al, be = bankpair[h]
                                qr = (id(rstdc), 0 + h * 4 + c, 1 + h * 4 + c)
                                kr = (id(rstdc), 16 + h * 4 + c, 17 + h * 4 + c)
                                nk = (id(rstdc), 32 + h * 4 + c, 33 + h * 4 + c)
                                bk = (id(rstdc), 48 + c * 4 + h, 49 + c * 4 + h)
                                bcol = (id(beta_s), c * 4 + h, c * 4 + h + 1)
                                Zh, Xh, Uh = Zb[h], Xb[h], Ub[h]
                                sA = (id(Sf), (2 * h) * 256, (2 * h + 1) * 256)
                                sB = (id(Sf), (2 * h + 1) * 256, (2 * h + 2) * 256)
                                sbA = ("scrf", (2 * h) * 512, (2 * h + 1) * 512, (2 * h) * 256)
                                sbB = ("scrf", (2 * h + 1) * 512, (2 * h + 2) * 512, (2 * h + 1) * 256)
                                for i, qc in enumerate((2 * h, 2 * h + 1)):
                                    kap, krd = KTs(qc, c)
                                    p.t(lambda e, kap=kap, al=al, i=i: e.matmul(
                                        P[al][:, 0:128], kap, kap, start=(i == 0), stop=(i == 1)),
                                        r=[krd], w=[PR(al)])
                                p.t(lambda e, h=h, c=c, be=be: e.matmul(
                                    P[be][:, 0:128], ones1b[:, :],
                                    nrowb[0:1, h * 512 + c * 128: h * 512 + (c + 1) * 128],
                                    start=True, stop=True),
                                    r=[L(ones1b),
                                       (id(rowpool), (h * 512 + c * 128) * 2, (h * 512 + (c + 1) * 128) * 2)],
                                    w=[PR(be)])
                                p.v(lambda e, h=h, be=be: e.tensor_copy(krbs[h][:, :], P[be][:, 0:128]),
                                    r=[PR(be)], w=[L(krbs[h])])
                                if "a_m" in dbg_out and b == 0 and a == 0 and c == 0 and h == 0:
                                    p.v(lambda e, al=al: e.tensor_copy(
                                        qsq[0][:, 0:128], P[al][:, 0:128]),
                                        r=[PR(al)], w=[L(qsq[0], 0, 128)])
                                    p.y(lambda e: e.dma_start(dbg_out["a_m"], qsq[0][:, 0:128]),
                                        r=[L(qsq[0], 0, 128)], w=[])
                                p.v(lambda e, h=h, al=al: e.tensor_tensor(
                                    Zt[h][:, 0:128], P[al][:, 0:128], triuSb[:, :], ALU.mult),
                                    r=[PR(al), L(triuSb)], w=[L(Zt[h], 0, 128)])
                                p.v(lambda e, h=h: e.tensor_mul(
                                    Zt[h][:, 128:256], Zt[h][:, 0:128], krbs[h][:, :]),
                                    r=[L(Zt[h], 0, 128), L(krbs[h])], w=[L(Zt[h], 128, 256)])
                                p.v(lambda e, h=h, bk=bk, Zh=Zh: e.tensor_scalar_mul(
                                    Zh[:, 0:128], Zt[h][:, 128:256], rstdc[:, bk[1]:bk[2]]),
                                    r=[L(Zt[h], 128, 256), bk], w=[L(Zh, 0, 128)])
                                if "a_m" in dbg_out and b == 0 and a == 0 and c == 0 and h == 0:
                                    p.y(lambda e: e.dma_start(dbg_out["a_zt"], Zt[0][:, :]),
                                        r=[L(Zt[0])], w=[])
                                    p.y(lambda e: e.dma_start(dbg_out["a_krb"], krbs[0][:, :]),
                                        r=[L(krbs[0])], w=[])
                                if "a_z0i" in dbg_out and b == 0 and a == 0 and c == 0 and h == 0:
                                    p.g(lambda e: e.dma_start(dbg_out["a_z0i"], Zb[0][:, 0:128]),
                                        r=[L(Zb[0], 0, 128)], w=[])
                                pbt = 6 + (h % 2)
                                p.t(lambda e, h=h, pbt=pbt: e.transpose(
                                    PB[pbt - 6][:, 0:128], Zb[h][:, 0:128], identb[:, :]),
                                    r=[L(Zh, 0, 128), L(identb)], w=[PR(pbt)])
                                p.v(lambda e, h=h, pbt=pbt, Zh=Zh: e.tensor_copy(
                                    Zh[:, 7 * 128:8 * 128], PB[pbt - 6][:, 0:128]),
                                    r=[PR(pbt)], w=[L(Zh, 7 * 128, 8 * 128)])
                                for j in range(1, 7):
                                    zt_prev = 7 * 128 if j % 2 == 1 else 8 * 128
                                    zt_new = 8 * 128 if j % 2 == 1 else 7 * 128
                                    bank = al if j % 2 == 1 else be
                                    p.t(lambda e, j=j, h=h, bank=bank, zt_prev=zt_prev, Zh=Zh: e.matmul(
                                        P[bank][:, 0:128], Zh[:, zt_prev:zt_prev + 128],
                                        Zh[:, (j - 1) * 128: j * 128], start=True, stop=True),
                                        r=[L(Zh, zt_prev, zt_prev + 128),
                                           L(Zh, (j - 1) * 128, j * 128)],
                                        w=[PR(bank)])
                                    p.v(lambda e, j=j, h=h, bank=bank, Zh=Zh: e.tensor_copy(
                                        Zh[:, j * 128:(j + 1) * 128], P[bank][:, 0:128]),
                                        r=[PR(bank)], w=[L(Zh, j * 128, (j + 1) * 128)])
                                    if j < 6:
                                        bank2 = be if j % 2 == 1 else al
                                        p.t(lambda e, j=j, h=h, bank2=bank2, zt_prev=zt_prev, Zh=Zh: e.matmul(
                                            P[bank2][:, 0:128], Zh[:, (j - 1) * 128: j * 128],
                                            Zh[:, zt_prev:zt_prev + 128], start=True, stop=True),
                                            r=[L(Zh, zt_prev, zt_prev + 128),
                                               L(Zh, (j - 1) * 128, j * 128)],
                                            w=[PR(bank2)])
                                        p.v(lambda e, j=j, h=h, bank2=bank2, zt_new=zt_new, Zh=Zh: e.tensor_copy(
                                            Zh[:, zt_new:zt_new + 128], P[bank2][:, 0:128]),
                                            r=[PR(bank2)], w=[L(Zh, zt_new, zt_new + 128)])
                                for i, qc in enumerate((2 * h, 2 * h + 1)):
                                    kap, krd = KTs(qc, c)
                                    ss_ = sbA if i == 0 else sbB
                                    p.t(lambda e, kap=kap, al=al, i=i, h=h, ss_=ss_: e.matmul(
                                        P[al][:, 0:256], kap,
                                        Sbf[:, ss_[3]:ss_[3] + 256], start=(i == 0), stop=(i == 1)),
                                        r=[krd, ss_[:3]], w=[PR(al)])
                                p.v(lambda e, h=h, c=c, al=al, nk=nk, Xh=Xh: e.scalar_tensor_tensor(
                                    Xh[:, 0:256], P[al][:, 0:256], rstdc[:, nk[1]:nk[2]],
                                    Vsb[:, c * 1024 + h * 256: c * 1024 + (h + 1) * 256],
                                    ALU.mult, ALU.add),
                                    r=[PR(al), nk,
                                       L(SCRBF, c * 1024 + h * 256, c * 1024 + (h + 1) * 256)],
                                    w=[L(Xh, 0, 256)])
                                xcur = 0
                                for j in range(6, -1, -1):
                                    bank = al if j % 2 == 0 else be
                                    p.t(lambda e, j=j, h=h, bank=bank, xcur=xcur, Zh=Zh, Xh=Xh: e.matmul(
                                        P[bank][:, 0:256], Zh[:, j * 128:(j + 1) * 128],
                                        Xh[:, xcur * 256:(xcur + 1) * 256], start=True, stop=True),
                                        r=[L(Zh, j * 128, (j + 1) * 128),
                                           L(Xh, xcur * 256, (xcur + 1) * 256)],
                                        w=[PR(bank)])
                                    op_ = ALU.add if j > 0 else ALU.subtract
                                    p.v(lambda e, h=h, bank=bank, xcur=xcur, op_=op_, Xh=Xh: e.tensor_tensor(
                                        Xh[:, (1 - xcur) * 256:(2 - xcur) * 256],
                                        Xh[:, xcur * 256:(xcur + 1) * 256], P[bank][:, 0:256], op_),
                                        r=[L(Xh, xcur * 256, (xcur + 1) * 256), PR(bank)],
                                        w=[L(Xh, (1 - xcur) * 256, (2 - xcur) * 256)])
                                    xcur = 1 - xcur
                                p.v(lambda e, h=h, xcur=xcur, bcol=bcol, Uh=Uh, Xh=Xh: e.tensor_scalar_mul(
                                    Uh[:, 0:256], Xh[:, xcur * 256:(xcur + 1) * 256],
                                    beta_s[:, bcol[1]:bcol[2]]),
                                    r=[L(Xh, xcur * 256, (xcur + 1) * 256), bcol],
                                    w=[L(Uh, 0, 256)])
                                p.v(lambda e, h=h, xcur=xcur, bk=bk, Uh=Uh, Xh=Xh: e.tensor_scalar_mul(
                                    Uh[:, 256:512], Xh[:, xcur * 256:(xcur + 1) * 256],
                                    rstdc[:, bk[1]:bk[2]]),
                                    r=[L(Xh, xcur * 256, (xcur + 1) * 256), bk],
                                    w=[L(Uh, 256, 512)])
                                for i, qc in enumerate((2 * h, 2 * h + 1)):
                                    kap, krd = KTs(qc, c)
                                    qap, qrd = QTs(qc, c)
                                    p.t(lambda e, kap=kap, qap=qap, be=be, i=i: e.matmul(
                                        P[be][:, 0:128], kap, qap, start=(i == 0), stop=(i == 1)),
                                        r=[krd, qrd], w=[PR(be)])
                                p.v(lambda e, h=h, be=be: e.tensor_tensor(
                                    Zt[h][:, 0:128], P[be][:, 0:128], triuIb[:, :], ALU.mult),
                                    r=[PR(be), L(triuIb)], w=[L(Zt[h], 0, 128)])
                                p.v(lambda e, h=h, kr=kr: e.tensor_scalar_mul(
                                    LTb[h][:, :], Zt[h][:, 0:128], rstdc[:, kr[1]:kr[2]]),
                                    r=[L(Zt[h], 0, 128), kr], w=[L(LTb[h])])
                                for i, qc in enumerate((2 * h, 2 * h + 1)):
                                    qap, qrd = QTs(qc, c)
                                    ss_ = sbA if i == 0 else sbB
                                    p.t(lambda e, qap=qap, al=al, i=i, ss_=ss_: e.matmul(
                                        P[al][:, 0:256], qap, Sbf[:, ss_[3]:ss_[3] + 256],
                                        start=(i == 0), stop=False),
                                        r=[qrd, ss_[:3]], w=[PR(al)])
                                p.t(lambda e, h=h, al=al, Uh=Uh: e.matmul(
                                    P[al][:, 0:256], LTb[h][:, :], Uh[:, 0:256],
                                    start=False, stop=True),
                                    r=[L(LTb[h]), L(Uh, 0, 256)], w=[PR(al)])
                                p.v(lambda e, h=h, al=al, qr=qr: e.tensor_scalar_mul(
                                    Ot1[h][:, :], P[al][:, 0:256], rstdc[:, qr[1]:qr[2]]),
                                    r=[PR(al), qr], w=[L(Ot1[h])])
                                p.v(lambda e, h=h: e.tensor_mul(
                                    Ot2[h % 2][:, :], Ot1[h][:, :], Ot1[h][:, :]),
                                    r=[L(Ot1[h])], w=[L(Ot2[h % 2])])
                                p.v(lambda e, h=h: e.reduce_sum(
                                    Zt[h][:, 0:1], Ot2[h % 2][:, :], AX.X),
                                    r=[L(Ot2[h % 2])], w=[L(Zt[h], 0, 1)])
                                p.v(lambda e, h=h: e.tensor_scalar(
                                    Zt[h][:, 1:2], Zt[h][:, 0:1], 1.0 / 256, EPS_RMS,
                                    ALU.mult, ALU.add),
                                    r=[L(Zt[h], 0, 1)], w=[L(Zt[h], 1, 2)])
                                p.s(lambda e, h=h: e.activation(
                                    Zt[h][:, 2:3], Zt[h][:, 1:2], AF.Ln),
                                    r=[L(Zt[h], 1, 2)], w=[L(Zt[h], 2, 3)])
                                p.s(lambda e, h=h: e.activation(
                                    Zt[h][:, 3:4], Zt[h][:, 2:3], AF.Exp, scale=-0.5),
                                    r=[L(Zt[h], 2, 3)], w=[L(Zt[h], 3, 4)])
                                p.v(lambda e, h=h: e.tensor_scalar_mul(
                                    Ot2[h % 2][:, :], Ot1[h][:, :], Zt[h][:, 3:4]),
                                    r=[L(Ot1[h]), L(Zt[h], 3, 4)], w=[L(Ot2[h % 2])])
                                p.v(lambda e, h=h: e.tensor_mul(
                                    On[h][:, :], Ot2[h % 2][:, :], gbc[:, :]),
                                    r=[L(Ot2[h % 2]), L(gbc)], w=[L(On[h])])
                                for jj in range(2):
                                    pbt = 6 + ((h + jj) % 2)
                                    oc = 2 * h + jj
                                    p.t(lambda e, h=h, jj=jj, pbt=pbt: e.transpose(
                                        PB[pbt - 6][:, 0:128], On[h][:, jj * 128:(jj + 1) * 128],
                                        identb[:, :]),
                                        r=[L(On[h], jj * 128, (jj + 1) * 128), L(identb)],
                                        w=[PR(pbt)])
                                    p.v(lambda e, c=c, oc=oc, pbt=pbt: e.tensor_copy(
                                        otb[c % 2][:, oc * 128:(oc + 1) * 128],
                                        PB[pbt - 6][:, 0:128]),
                                        r=[PR(pbt)],
                                        w=[L(otb[c % 2], oc * 128, (oc + 1) * 128)])
                                for i, qc in enumerate((2 * h, 2 * h + 1)):
                                    kap, krd = KTs(qc, c)
                                    pbt = 6 + ((h + i) % 2)
                                    p.t(lambda e, kap=kap, pbt=pbt: e.transpose(
                                        PB[pbt - 6][:, 0:128], kap, identb[:, :]),
                                        r=[krd, L(identb)], w=[PR(pbt)])
                                    p.v(lambda e, h=h, pbt=pbt, Zh=Zh: e.tensor_copy(
                                        Zh[:, 128:256], PB[pbt - 6][:, 0:128]),
                                        r=[PR(pbt)], w=[L(Zh, 128, 256)])
                                    bank = al if i == 0 else be
                                    p.t(lambda e, h=h, bank=bank, Zh=Zh, Uh=Uh: e.matmul(
                                        P[bank][:, 0:256], Zh[:, 128:256], Uh[:, 256:512],
                                        start=True, stop=True),
                                        r=[L(Zh, 128, 256), L(Uh, 256, 512)], w=[PR(bank)])
                                    ss_f = sA if i == 0 else sB
                                    ss_b = (sbA if i == 0 else sbB)
                                    st_ = stmp[i]
                                    p.v(lambda e, bank=bank, ss_f=ss_f, st_=st_: e.tensor_add(
                                        st_[:, :], Sf[:, ss_f[1]:ss_f[2]], P[bank][:, 0:256]),
                                        r=[ss_f, PR(bank)], w=[L(st_)])
                                    p.v(lambda e, ss_f=ss_f, st_=st_: e.tensor_copy(
                                        Sf[:, ss_f[1]:ss_f[2]], st_[:, :]),
                                        r=[L(st_)], w=[ss_f])
                                    p.v(lambda e, ss_b=ss_b, st_=st_: e.tensor_copy(
                                        Sbf[:, ss_b[3]:ss_b[3] + 256], st_[:, :]),
                                        r=[L(st_)], w=[ss_b[:3]])
                            if "a_x0" in dbg_out and b == 0 and a == 0 and c == 0:
                                p.g(lambda e: e.dma_start(dbg_out["a_x0"], Xb[0][:, :]),
                                    r=[L(Xb[0])], w=[])
                                p.g(lambda e: e.dma_start(dbg_out["a_u0"], Ub[0][:, :]),
                                    r=[L(Ub[0])], w=[])
                                p.g(lambda e: e.dma_start(dbg_out["a_z0"], Zb[0][:, :]),
                                    r=[L(Zb[0])], w=[])
                                p.g(lambda e: e.dma_start(dbg_out["a_on"], On[0][:, :]),
                                    r=[L(On[0])], w=[])
                                p.y(lambda e: e.dma_start(dbg_out["a_sf"], Sf[:, :]),
                                    r=[L(Sf)], w=[])
                            for dh in range(2):
                                for oc in range(8):
                                    p.t(lambda e, c=c, oc=oc, dh=dh: e.matmul(
                                        P[0][:, :], otb[c % 2][:, oc * 128:(oc + 1) * 128],
                                        wo_sb[:, oc * 1024 + dh * 512: oc * 1024 + (dh + 1) * 512],
                                        start=(oc == 0), stop=(oc == 7)),
                                        r=[L(otb[c % 2], oc * 128, (oc + 1) * 128),
                                           L(B, oc * 1024 + dh * 512, oc * 1024 + (dh + 1) * 512)],
                                        w=[PR(0)])
                                lo = c * 1024 + dh * 512
                                p.v(lambda e, lo=lo: e.tensor_add(
                                    Rg[1][:, lo:lo + 512], Rg[0][:, lo:lo + 512], P[0][:, :]),
                                    r=[L(Rg[0], lo, lo + 512), PR(0)],
                                    w=[L(Rg[1], lo, lo + 512)])
                        dst_ap = dst.rearrange("(gg s pp) d -> gg pp s d", gg=NG, s=SPG, pp=128)[g]
                        p.y(lambda e, dap=dst_ap: e.dma_start(
                            dap, Rg[1].ap().rearrange("p (s d) -> p s d", s=SPG)),
                            r=[L(Rg[1])], w=[(dst_name, g, g + 1)])
                return

            moe_pass(0, dr["x0"], "x0", rb1 if npass > 1 else out,
                     "rb1" if npass > 1 else "out", parity=0)
            if npass == 5:
                pass
            if npass == 5:
                pass
            elif npass == 2:
                attn_pass(rb1, "rb1", out, "out")
            elif npass == 4:
                moe_pass(1, rb1, "rb1", out, "out", parity=1)
            elif npass >= 3:
                attn_pass(rb1, "rb1", rb2, "rb2")
                moe_pass(1, rb2, "rb2", out, "out", parity=1)
            p.wait_only("sync", [("yq", p.counts["yq"])])
            p.wait_only("gpsimd", [("gq", p.counts["gq"])])


        p1 = Plan(nc, ctx)
        p1.phase = 1
        emit_all(p1)
        p = Plan(nc, ctx)
        p.phase = 2
        p.consumed = p1.consumed
        emit_all(p)
        block = ctx.enter_context(nc.Block())
        p.replay(block)

    return nc


def make_consts():
    c = {}
    c["ident"] = np.eye(128, dtype=np.float32)
    c["triu_inc"] = np.triu(np.ones((128, 128), np.float32))
    c["triu_str"] = np.triu(np.ones((128, 128), np.float32), 1)
    c["triu32"] = np.triu(np.ones((32, 32), np.float32), 1)
    c["iota8"] = np.broadcast_to(np.arange(8, dtype=np.float32), (128, 8)).copy()
    c["ones_col"] = np.ones((128, 1), np.float32)
    c["ones1"] = np.ones((1, 128), np.float32)
    return c


def make_weight_maps(inp):
    m = {}
    W1 = np.asarray(inp["L0_ffn_W1"], dtype=np.float32)
    m["w1s"] = np.ascontiguousarray(
        W1.reshape(E, 8, 128, 8, 512).transpose(0, 3, 2, 1, 4))
    W2 = np.asarray(inp["L0_ffn_W2"], dtype=np.float32)
    m["w2s"] = np.ascontiguousarray(
        W2.reshape(E, 4, 8, 128, 2, 512).transpose(0, 4, 1, 3, 2, 5))
    for nm, key in [("wqs", "L0_attn_Wq"), ("wks", "L0_attn_Wk"),
                    ("wvs", "L0_attn_Wv"), ("wos", "L0_attn_Wo")]:
        W = np.asarray(inp[key], dtype=np.float32)
        m[nm] = np.ascontiguousarray(W.reshape(8, 128, 1024).transpose(1, 0, 2))
    Wb = np.asarray(inp["L0_attn_Wbeta"], dtype=np.float32)
    m["wbs"] = np.ascontiguousarray(Wb.reshape(8, 128, 4).transpose(1, 0, 2))
    Wr = np.asarray(inp["L0_ffn_Wr"], dtype=np.float32)
    m["wrs"] = np.ascontiguousarray(Wr.reshape(8, 128, 8).transpose(1, 0, 2))
    b1 = np.asarray(inp["L0_ffn_b1"], dtype=np.float32)
    m["b1s"] = np.ascontiguousarray(b1.reshape(8, 32, 128).transpose(2, 0, 1))
    m["b2s"] = np.ascontiguousarray(np.asarray(inp["L0_ffn_b2"], dtype=np.float32))
    gn = np.asarray(inp["L0_attn_g_norm"], dtype=np.float32)
    m["gb"] = np.ascontiguousarray(np.broadcast_to(gn, (128, 256)))
    m.update(make_consts())
    return m


def make_in_maps(inp):
    shared = make_weight_maps(inp)
    x = np.asarray(inp["L0_attn_x"], dtype=np.float32)
    maps = []
    for c in range(NCORES):
        d = dict(shared)
        d["x0"] = np.ascontiguousarray(x[2 * c:2 * c + 2].reshape(TOK, D))
        cm = np.zeros((8, 1), np.float32)
        cm[:c] = 1.0
        d["coremask"] = cm
        maps.append(d)
    return maps


_NC_CACHE = {}
KN_NPASS = 3
KN_DBG = ()


def kernel(**inputs):
    key = (KN_NPASS, tuple(KN_DBG), PRUNE, SIZE_PROBE_G)
    if key not in _NC_CACHE:
        _NC_CACHE[key] = build(npass=KN_NPASS, dbg=KN_DBG)
    nc = _NC_CACHE[key]
    maps = make_in_maps(inputs)
    res = run_bass_kernel_spmd(nc, maps, core_ids=list(range(NCORES)))
    outs = [np.asarray(res.results[c]["out"]) for c in range(NCORES)]
    full = np.concatenate([o.reshape(2, 2048, 1024) for o in outs], axis=0)
    if KN_DBG:
        return full, res
    return full


# version 06 (final)
# speedup vs baseline: 1.0095x; 1.0028x over previous
import numpy as np
import ml_dtypes
from contextlib import ExitStack

import concourse.bass as bass
import concourse.mybir as mybir
from concourse.bass_utils import run_bass_kernel_spmd

F32 = mybir.dt.float32
BF16 = mybir.dt.bfloat16
AF = mybir.ActivationFunctionType
ALU = mybir.AluOpType
AX = mybir.AxisListType

NCORES = 8
TOK = 4096
D = 1024
E = 8
FF = 4096
NH = 4
DK = 256
NSUB = 32
G = 512
NG = 8
SPG = 4
CAP = 5120.0
EPS_LN = 1e-5
EPS_RMS = 1e-5
BIG = 1e9


class Plan:
    ENGS = ("tensor", "vector", "scalar", "gpsimd", "sync")

    def __init__(self, nc, ctx):
        self.nc = nc
        self.ctx = ctx
        self.events = {e: [] for e in self.ENGS}
        self.sems = {}
        self.counts = {}
        self.waited = {}
        self.writes = {}
        self.reads = {}
        self.phase = 1
        self.ordinal = {}
        self.consumed = set()
        self.last_consumed_tok = {}
        self.psum_read = {}

    def sem(self, name, handle=None):
        self.sems[name] = handle if handle is not None else \
            self.ctx.enter_context(self.nc.semaphore(name))
        self.counts[name] = 0

    def _dep_waits(self, eng, reads, writes):
        waits = []
        for nm, lo, hi in reads:
            for (l2, h2, tok) in self.writes.get(nm, ()):
                if lo < h2 and l2 < hi:
                    waits.append(tok)
        for nm, lo, hi in writes:
            for (l2, h2, tok) in self.writes.get(nm, ()):
                if lo < h2 and l2 < hi:
                    waits.append(tok)
            for (l2, h2, tok) in self.reads.get(nm, ()):
                if lo < h2 and l2 < hi:
                    waits.append(tok)
        if eng == "tensor":
            waits = [w for w in waits if w[0] != "ts"]
        if eng == "sync":
            waits = [w for w in waits if w[0] != "yq"]
        if eng == "gpsimd":
            waits = [w for w in waits if w[0] != "gq"]
        return waits

    def _record(self, reads, writes, tok):
        for nm, lo, hi in writes:
            lst = self.writes.setdefault(nm, [])
            lst[:] = [e for e in lst if not (lo <= e[0] and e[1] <= hi)]
            lst.append((lo, hi, tok))
            rl = self.reads.setdefault(nm, [])
            rl[:] = [e for e in rl if not (lo <= e[0] and e[1] <= hi)]
        for nm, lo, hi in reads:
            self.reads.setdefault(nm, []).append((lo, hi, tok))

    def op(self, eng, fn, reads=(), writes=(), extra=(), inc=None, inc_by=None):
        if inc is None:
            inc = {"tensor": "ts", "vector": "vs", "scalar": "ss",
                   "gpsimd": "gq", "sync": "yq"}[eng]
        if inc_by is None:
            inc_by = 16 if eng in ("gpsimd", "sync") else 1
        extra2 = list(extra)
        if eng in ("vector", "scalar"):
            other = "scalar" if eng == "vector" else "vector"
            for nm, lo, hi in reads:
                if isinstance(nm, str) and nm.startswith("psum"):
                    t_ = self.psum_read.get((other, nm))
                    if t_ is not None:
                        extra2.append(t_)
        self.ordinal[inc] = self.ordinal.get(inc, 0) + 1
        my_ord = (inc, self.ordinal[inc])
        if PRUNE and self.phase == 2 and inc not in ("gq", "yq", "cc") and my_ord not in self.consumed:
            tok = self.last_consumed_tok.get(inc, (inc, 0))
            do_inc = False
        else:
            self.counts[inc] += inc_by
            tok = (inc, self.counts[inc])
            self.last_consumed_tok[inc] = tok
            do_inc = True
        waits = extra2 + self._dep_waits(eng, reads, writes)
        need = []
        for w in waits:
            if w is None:
                continue
            s, v, *rest = w
            if self.phase == 1:
                if rest:
                    self.consumed.add((s, rest[0]))
            if v <= 0 or self.waited.get((eng, s), 0) >= v:
                continue
            self.waited[(eng, s)] = v
            need.append((s, v))

        def ev(e, need=tuple(need), fn=fn, inc=inc, inc_by=inc_by, do_inc=do_inc):
            for s, v in need:
                e.wait_ge(self.sems[s], v)
            r = fn(e)
            if do_inc:
                r.then_inc(self.sems[inc], inc_by)

        self.events[eng].append(ev)
        tok3 = (tok[0], tok[1], my_ord)
        self._record(reads, writes, tok3)
        if eng in ("vector", "scalar"):
            for nm, lo, hi in reads:
                if isinstance(nm, str) and nm.startswith("psum"):
                    self.psum_read[(eng, nm)] = tok3
        return tok3

    def t(self, fn, r=(), w=(), extra=()):
        return self.op("tensor", fn, r, w, extra)

    def v(self, fn, r=(), w=(), extra=()):
        return self.op("vector", fn, r, w, extra)

    def s(self, fn, r=(), w=(), extra=()):
        return self.op("scalar", fn, r, w, extra)

    def g(self, fn, r=(), w=(), extra=()):
        return self.op("gpsimd", fn, r, w, extra)

    def y(self, fn, r=(), w=(), extra=()):
        return self.op("sync", fn, r, w, extra)

    def last_tok(self, sem):
        return (sem, self.counts.get(sem, 0), (sem, self.ordinal.get(sem, 0)))

    def wait_only(self, eng, waits):
        need = []
        for wv in waits:
            if wv is None:
                continue
            s, v, *rest = wv
            if self.phase == 1 and rest:
                self.consumed.add((s, rest[0]))
            if v <= 0 or self.waited.get((eng, s), 0) >= v:
                continue
            self.waited[(eng, s)] = v
            need.append((s, v))
        if not need:
            return

        def ev(e, need=tuple(need)):
            for s, v in need:
                e.wait_ge(self.sems[s], v)

        self.events[eng].append(ev)

    def replay(self, block):
        plan = self

        def mk(name):
            def run(e):
                for ev in plan.events[name]:
                    ev(e)
            return run

        block.tensor(mk("tensor"))
        block.vector(mk("vector"))
        block.scalar(mk("scalar"))
        block.gpsimd(mk("gpsimd"))
        block.sync(mk("sync"))


PRUNE = False

SIZE_PROBE_G = 99

def build(npass=3, dbg=()):
    nc = bass.Bass("TRN2", target_bir_lowering=False)

    dr = {}
    def din(name, shape):
        dr[name] = nc.dram_tensor(name, list(shape), F32, kind="ExternalInput").ap()

    din("x0", (TOK, D))
    din("w1s", (E, 8, 128, 8, 512))
    din("w2s", (E, 2, 4, 128, 8, 512))
    din("wqs", (128, 8, 1024))
    din("wks", (128, 8, 1024))
    din("wvs", (128, 8, 1024))
    din("wos", (128, 8, 1024))
    din("wbs", (128, 8, 4))
    din("wrs", (128, 8, 8))
    din("b1s", (128, 8, 32))
    din("b2s", (E, D))
    din("gb", (128, 256))
    din("coremask", (8, 1))
    din("ident", (128, 128))
    din("triu_inc", (128, 128))
    din("triu_str", (128, 128))
    din("triu32", (32, 32))
    din("iota8", (128, 8))
    din("ones_col", (128, 1))
    din("ones1", (1, 128))

    out = nc.dram_tensor("out", [TOK, D], F32, kind="ExternalOutput").ap()
    rb1 = nc.dram_tensor("rb1", [TOK, D], F32, kind="Internal").ap()
    w1c = nc.dram_tensor("w1c", [E, 8, 128, 8, 512], BF16, kind="Internal").ap()
    w2c = nc.dram_tensor("w2c", [E, 2, 4, 128, 8, 512], BF16, kind="Internal").ap()
    rb2 = nc.dram_tensor("rb2", [TOK, D], F32, kind="Internal").ap()
    trow = nc.dram_tensor("trow", [1, 512], F32, kind="Internal").ap()
    orow = nc.dram_tensor("orow", [2, 32, 8], F32, kind="Internal").ap()
    cc_in = nc.dram_tensor("cc_in", [1, 16], F32, kind="Internal").ap()
    cc_out = nc.dram_tensor("cc_out", [8, 16], F32, kind="Internal").ap()

    dbg_out = {}
    for nm, shape in dbg:
        dbg_out[nm] = nc.dram_tensor(nm, list(shape), F32, kind="ExternalOutput").ap()

    ctx = ExitStack()
    with ctx:
        def sb(name, shape, dt):
            return ctx.enter_context(nc.sbuf_tensor(name, list(shape), dt))

        def ps(name, shape, dt):
            return ctx.enter_context(nc.psum_tensor(name, list(shape), dt))

        A = sb("A", (128, 24576), BF16)
        B = sb("B", (128, 16384), BF16)
        Rg = [sb(f"Rg{i}", (128, 4096), F32) for i in range(2)]
        xls = [sb(f"xls{i}", (128, 1024), BF16) for i in range(2)]
        xlg = [sb(f"xlg{i}", (128, 4096), BF16) for i in range(2)]
        SCRBF = sb("SCRBF", (128, 4096), BF16)
        Vsb = SCRBF
        SCRF = sb("SCRF", (128, 1024), F32)
        Sf = sb("Sf", (128, 2048), F32)
        stmp = [sb(f"stmp{i}", (128, 256), F32) for i in range(2)]
        Sbf = SCRF.ap().bitcast(BF16)
        Zb = [sb(f"Zb{h}", (128, 9 * 128), BF16) for h in range(NH)]
        Xb = [sb(f"Xb{h}", (128, 512), BF16) for h in range(NH)]
        Ub = [sb(f"Ub{h}", (128, 512), BF16) for h in range(NH)]
        LTb = [sb(f"LTb{h}", (128, 128), BF16) for h in range(NH)]
        Zt = [sb(f"Zt{h}", (128, 256), F32) for h in range(NH)]
        Ot1 = [sb(f"Ot1{h}", (128, 256), F32) for h in range(NH)]
        Ot2 = [sb(f"Ot2{h}", (128, 256), F32) for h in range(2)]
        On = [sb(f"On{h}", (128, 256), BF16) for h in range(NH)]
        otb = [sb(f"otb{i}", (128, 1024), BF16) for i in range(2)]
        krbs = [sb(f"krb{h}", (128, 128), F32) for h in range(NH)]
        qsq = [sb(f"qsq{i}", (128, 512), F32) for i in range(1)]
        stat_mu = [sb(f"smu{i}", (128, 32), F32) for i in range(2)]
        stat_rs = [sb(f"srs{i}", (128, 32), F32) for i in range(2)]
        att_mu = sb("amu", (128, 32), F32)
        att_rs = sb("ars", (128, 32), F32)
        msum = sb("msum", (128, 32), F32)
        mus2 = sb("mus2", (128, 32), F32)
        var_c = sb("var_c", (128, 32), F32)
        var2 = sb("var2", (128, 32), F32)
        rsq = [SCRF]
        w_all = sb("w_all", (128, 256), F32)
        wT_all = SCRBF
        oh1_all = sb("oh1_all", (128, 256), BF16)
        oh2_all = sb("oh2_all", (128, 256), BF16)
        lgb = [sb(f"lgb{i}", (128, 8), F32) for i in range(4)]
        top = [sb(f"top{i}", (128, 48), F32) for i in range(4)]
        wcols = sb("wcols", (128, 64), F32)
        b1c = sb("b1c", (128, 256), F32)
        b2c = sb("b2c", (8, 1024), BF16)
        gbc = sb("gbc", (128, 256), F32)
        wrb = sb("wrb", (128, 64), BF16)
        wbb = sb("wbb", (128, 32), BF16)
        identb = sb("identb", (128, 128), BF16)
        triuIb = sb("triuIb", (128, 128), BF16)
        triuSb = sb("triuSb", (128, 128), BF16)
        wbt = sb("wbt", (128, 8), BF16)
        tri32 = sb("tri32", (32, 32), F32)
        iota = sb("iota", (128, 8), F32)
        onesc = sb("onesc", (128, 1), F32)
        onescb = sb("onescb", (128, 1), BF16)
        ones1b = sb("ones1b", (1, 128), BF16)
        ones1 = sb("ones1_sb", (1, 128), F32)
        onesc32 = sb("onesc32", (32, 1), F32)
        cmask = sb("cmask", (8, 1), F32)
        rowpool = sb("rowpool", (1, 1024), F32)
        nrowb = rowpool.ap().bitcast(BF16)
        totrow = rowpool[0:1, 0:512]
        tot32 = [sb(f"tot32{i}", (32, 8), F32) for i in range(2)]
        offs32 = [sb(f"offs32{i}", (32, 8), F32) for i in range(2)]
        cntrow = sb("cntrow", (1, 16), F32)
        gath = sb("gath", (8, 16), F32)
        baser = sb("baser", (1, 16), F32)

        rstdc = sb("rstdc", (128, 64), F32)
        beta_s = sb("beta_s", (128, 16), F32)
        ebuf = sb("ebuf", (128, 48), F32)
        sumsq = sb("sumsq", (128, 32), F32)
        dbg_lg = Sf[:, 0:256]

        P = [ps(f"P{i}", (128, 512), F32) for i in range(6)]
        PB = [ps(f"PB{i}", (128, 1024), BF16) for i in range(2)]

        sem_handles = {s_: ctx.enter_context(nc.semaphore(s_))
                       for s_ in ("ts", "vs", "ss", "gq", "yq", "cc")}

        def emit_all(p):
            for s_ in ("ts", "vs", "ss", "gq", "yq", "cc"):
                p.sem(s_, sem_handles[s_])

            def R(t_, lo=0, hi=1 << 30):
                return (t_, lo, hi)

            def L(tensor, lo=0, hi=1 << 30):
                return (id(tensor), lo, hi)

            PSUM = [("psum%d" % i, 0, 1) for i in range(8)]

            def PR(i):
                return PSUM[i]

            for t_, nm in [(tri32, "triu32"), (iota, "iota8"), (onesc, "ones_col"),
                           (ones1, "ones1"), (cmask, "coremask"), (b1c, "b1s"),
                           (gbc, "gb")]:
                p.y(lambda e, t_=t_, nm=nm: e.dma_start(t_[:, :], dr[nm]), w=[L(t_)])
            p.y(lambda e: e.dma_start(onesc32[:, :], dr["ones_col"][0:32, :]), w=[L(onesc32)])
            for t_, nm in [(identb, "ident"), (wrb, "wrs"), (wbb, "wbs"), (b2c, "b2s"),
                           (triuIb, "triu_inc"), (triuSb, "triu_str"),
                           (onescb, "ones_col"), (ones1b, "ones1")]:
                p.g(lambda e, t_=t_, nm=nm: e.dma_start(t_[:, :], dr[nm]), w=[L(t_)])

            def ln_make_xlg(src_tok_fn, rbuf_t, gslot, smu, srs, g):
                last = None
                for sl in range(SPG):
                    si = g * SPG + sl
                    xb = xls[sl % 2]
                    t_xl = p.v(lambda e, sl=sl, si=si, xb=xb, rbuf_t=rbuf_t: e.tensor_scalar(
                        xb[:, :], rbuf_t[:, sl * 1024:(sl + 1) * 1024],
                        smu[:, si:si + 1], srs[:, si:si + 1], ALU.subtract, ALU.mult),
                        r=[L(rbuf_t, sl * 1024, (sl + 1) * 1024), L(smu, si, si + 1),
                           L(srs, si, si + 1)],
                        w=[L(xb)])
                    for dc in range(8):
                        pb = 6 + (dc % 2)
                        p.t(lambda e, xb=xb, dc=dc, pb=pb: e.transpose(
                            PB[pb - 6][:, 0:128], xb[:, dc * 128:(dc + 1) * 128], identb[:, :]),
                            r=[L(xb, dc * 128, (dc + 1) * 128), L(identb)], w=[PR(pb)])
                        last = p.v(lambda e, dc=dc, sl=sl, gslot=gslot, pb=pb: e.tensor_copy(
                            xlg[gslot][:, dc * 512 + sl * 128: dc * 512 + (sl + 1) * 128],
                            PB[pb - 6][:, 0:128]),
                            r=[PR(pb)],
                            w=[L(xlg[gslot], dc * 512 + sl * 128, dc * 512 + (sl + 1) * 128)])
                return last

            def group_rload(src, src_name, g, dst_t):
                src_ap = src.rearrange("(gg s pp) d -> gg pp s d", gg=NG, s=SPG, pp=128)[g]
                return p.y(lambda e, sap=src_ap, dst_t=dst_t: e.dma_start(
                    dst_t.ap().rearrange("p (s d) -> p s d", s=SPG), sap),
                    r=[(src_name, g, g + 1)], w=[L(dst_t)])

            def moe_pass(pi, src, src_name, dst, dst_name, parity):
                smu, srs = stat_mu[parity], stat_rs[parity]

                for g in range(NG):
                    rbuf_t = Rg[g % 2]
                    group_rload(src, src_name, g, rbuf_t)
                    for sl in range(SPG):
                        si = g * SPG + sl
                        rt_lo, rt_hi = sl * 1024, (sl + 1) * 1024
                        rq = rsq[0]
                        p.v(lambda e, si=si, rbuf_t=rbuf_t, rt_lo=rt_lo, rt_hi=rt_hi: e.reduce_sum(
                            msum[:, si:si + 1], rbuf_t[:, rt_lo:rt_hi], AX.X),
                            r=[L(rbuf_t, rt_lo, rt_hi)], w=[L(msum, si, si + 1)])
                        p.v(lambda e, si=si: e.tensor_scalar_mul(
                            smu[:, si:si + 1], msum[:, si:si + 1], 1.0 / D),
                            r=[L(msum, si, si + 1)], w=[L(smu, si, si + 1)])
                        p.v(lambda e, rq=rq, rbuf_t=rbuf_t, rt_lo=rt_lo, rt_hi=rt_hi: e.tensor_mul(
                            rq[:, :], rbuf_t[:, rt_lo:rt_hi], rbuf_t[:, rt_lo:rt_hi]),
                            r=[L(rbuf_t, rt_lo, rt_hi)], w=[("scrf", 0, 4096)])
                        p.v(lambda e, si=si, rq=rq: e.reduce_sum(
                            var_c[:, si:si + 1], rq[:, :], AX.X),
                            r=[("scrf", 0, 4096)], w=[L(var_c, si, si + 1)])
                        p.v(lambda e, si=si: e.tensor_scalar(
                            mus2[:, si:si + 1], smu[:, si:si + 1], smu[:, si:si + 1],
                            -EPS_LN, ALU.mult, ALU.add),
                            r=[L(smu, si, si + 1)], w=[L(mus2, si, si + 1)])
                        p.v(lambda e, si=si: e.scalar_tensor_tensor(
                            var_c[:, si:si + 1], var_c[:, si:si + 1], 1.0 / D,
                            mus2[:, si:si + 1], ALU.mult, ALU.subtract),
                            r=[L(var_c, si, si + 1), L(mus2, si, si + 1)],
                            w=[L(var_c, si, si + 1)])
                    c0 = g * SPG
                    p.s(lambda e, c0=c0: e.activation(
                        var2[:, c0:c0 + SPG], var_c[:, c0:c0 + SPG], AF.Ln),
                        r=[L(var_c, c0, c0 + SPG)], w=[L(var2, c0, c0 + SPG)])
                    p.s(lambda e, c0=c0: e.activation(
                        srs[:, c0:c0 + SPG], var2[:, c0:c0 + SPG], AF.Exp, scale=-0.5),
                        r=[L(var2, c0, c0 + SPG)], w=[L(srs, c0, c0 + SPG)])
                    ln_make_xlg(None, rbuf_t, g % 2, smu, srs, g)
                    if "d_xl" in dbg_out and pi == 0 and g == 0:
                        p.g(lambda e: e.dma_start(dbg_out["d_xl"], xlg[0][:, 0:512]),
                            r=[L(xlg[0], 0, 512)], w=[])
                    for sl in range(SPG):
                        si = g * SPG + sl
                        pbank = sl % 2
                        for dc in range(8):
                            p.t(lambda e, dc=dc, g=g, sl=sl, pbank=pbank: e.matmul(
                                P[pbank][:, 0:8],
                                xlg[g % 2][:, dc * 512 + sl * 128: dc * 512 + (sl + 1) * 128],
                                wrb[:, dc * 8:(dc + 1) * 8],
                                start=(dc == 0), stop=(dc == 7)),
                                r=[L(xlg[g % 2], dc * 512 + sl * 128, dc * 512 + (sl + 1) * 128),
                                   L(wrb)],
                                w=[PR(pbank)])
                        lg = lgb[sl]
                        p.v(lambda e, lg=lg, pbank=pbank: e.tensor_copy(lg[:, :], P[pbank][:, 0:8]),
                            r=[PR(pbank)], w=[L(lg)])
                        if "d_lg" in dbg_out and pi == 0:
                            p.v(lambda e, lg=lg, si=si: e.tensor_copy(
                                mus2[:, 0:8] if False else dbg_lg[:, si * 8:(si + 1) * 8], lg[:, :]),
                                r=[L(lg)], w=[L(dbg_lg, si * 8, (si + 1) * 8)])
                        tp = top[sl]
                        o1 = (id(oh1_all), si * 8, (si + 1) * 8)
                        o2 = (id(oh2_all), si * 8, (si + 1) * 8)
                        p.v(lambda e, tp=tp, lg=lg: e.reduce_max(tp[:, 0:1], lg[:, :], AX.X),
                            r=[L(lg)], w=[L(tp, 0, 1)])
                        p.v(lambda e, tp=tp, lg=lg: e.tensor_single_scalar(
                            tp[:, 8:16], lg[:, :], tp[:, 0:1], ALU.is_equal),
                            r=[L(lg), L(tp, 0, 1)], w=[L(tp, 8, 16)])
                        p.v(lambda e, tp=tp: e.tensor_scalar(
                            tp[:, 16:24], tp[:, 8:16], -BIG, BIG, ALU.mult, ALU.add),
                            r=[L(tp, 8, 16)], w=[L(tp, 16, 24)])
                        p.v(lambda e, tp=tp: e.tensor_add(tp[:, 24:32], tp[:, 16:24], iota[:, :]),
                            r=[L(tp, 16, 24), L(iota)], w=[L(tp, 24, 32)])
                        p.v(lambda e, tp=tp: e.tensor_reduce(tp[:, 1:2], tp[:, 24:32], AX.X, ALU.min),
                            r=[L(tp, 24, 32)], w=[L(tp, 1, 2)])
                        p.v(lambda e, tp=tp, si=si: e.tensor_single_scalar(
                            oh1_all[:, si * 8:(si + 1) * 8], iota[:, :], tp[:, 1:2], ALU.is_equal),
                            r=[L(iota), L(tp, 1, 2)], w=[o1])
                        p.v(lambda e, tp=tp, lg=lg, si=si: e.scalar_tensor_tensor(
                            tp[:, 16:24], oh1_all[:, si * 8:(si + 1) * 8], -BIG, lg[:, :],
                            ALU.mult, ALU.add),
                            r=[o1, L(lg)], w=[L(tp, 16, 24)])
                        p.v(lambda e, tp=tp: e.reduce_max(tp[:, 2:3], tp[:, 16:24], AX.X),
                            r=[L(tp, 16, 24)], w=[L(tp, 2, 3)])
                        p.v(lambda e, tp=tp: e.tensor_single_scalar(
                            tp[:, 32:40], tp[:, 16:24], tp[:, 2:3], ALU.is_equal),
                            r=[L(tp, 16, 24), L(tp, 2, 3)], w=[L(tp, 32, 40)])
                        p.v(lambda e, tp=tp: e.tensor_scalar(
                            tp[:, 40:48], tp[:, 32:40], -BIG, BIG, ALU.mult, ALU.add),
                            r=[L(tp, 32, 40)], w=[L(tp, 40, 48)])
                        p.v(lambda e, tp=tp: e.tensor_add(tp[:, 32:40], tp[:, 40:48], iota[:, :]),
                            r=[L(tp, 40, 48), L(iota)], w=[L(tp, 32, 40)])
                        p.v(lambda e, tp=tp: e.tensor_reduce(tp[:, 3:4], tp[:, 32:40], AX.X, ALU.min),
                            r=[L(tp, 32, 40)], w=[L(tp, 3, 4)])
                        p.v(lambda e, tp=tp, si=si: e.tensor_single_scalar(
                            oh2_all[:, si * 8:(si + 1) * 8], iota[:, :], tp[:, 3:4], ALU.is_equal),
                            r=[L(iota), L(tp, 3, 4)], w=[o2])
                        p.v(lambda e, tp=tp: e.tensor_sub(tp[:, 4:5], tp[:, 2:3], tp[:, 0:1]),
                            r=[L(tp, 0, 1), L(tp, 2, 3)], w=[L(tp, 4, 5)])
                        p.s(lambda e, tp=tp: e.activation(tp[:, 5:6], tp[:, 4:5], AF.Exp),
                            r=[L(tp, 4, 5)], w=[L(tp, 5, 6)])
                        p.v(lambda e, tp=tp: e.tensor_scalar_add(tp[:, 6:7], tp[:, 5:6], 1.0),
                            r=[L(tp, 5, 6)], w=[L(tp, 6, 7)])
                        p.v(lambda e, tp=tp, si=si: e.reciprocal(wcols[:, si:si + 1], tp[:, 6:7]),
                            r=[L(tp, 6, 7)], w=[L(wcols, si, si + 1)])
                        p.v(lambda e, tp=tp, si=si: e.tensor_mul(
                            wcols[:, 32 + si:33 + si], tp[:, 5:6], wcols[:, si:si + 1]),
                            r=[L(tp, 5, 6), L(wcols, si, si + 1)], w=[L(wcols, 32 + si, 33 + si)])
                        p.t(lambda e, si=si: e.matmul(
                            P[2][0:1, si * 8:(si + 1) * 8], onescb[:, :],
                            oh1_all[:, si * 8:(si + 1) * 8], start=(si == 0), stop=False),
                            r=[o1, L(onescb)], w=[PR(2)])
                        p.t(lambda e, si=si: e.matmul(
                            P[2][0:1, 256 + si * 8: 256 + (si + 1) * 8], onescb[:, :],
                            oh2_all[:, si * 8:(si + 1) * 8], start=False, stop=(si == NSUB - 1)),
                            r=[o2, L(onescb)], w=[PR(2)])

                p.v(lambda e: e.tensor_copy(rowpool[0:1, 0:512], P[2][0:1, 0:512]),
                    r=[PR(2)], w=[L(rowpool, 0, 2048)])
                p.g(lambda e: e.dma_start(trow, rowpool[0:1, 0:512]),
                    r=[L(rowpool, 0, 2048)], w=[("trow", 0, 512)])
                p.g(lambda e: e.dma_start(
                    tot32[0][:, :], trow[0:1, 0:256].rearrange("o (s e) -> (o s) e", s=32)),
                    r=[("trow", 0, 256)], w=[L(tot32[0])])
                p.g(lambda e: e.dma_start(
                    tot32[1][:, :], trow[0:1, 256:512].rearrange("o (s e) -> (o s) e", s=32)),
                    r=[("trow", 256, 512)], w=[L(tot32[1])])
                p.t(lambda e: e.matmul(P[3][0:32, 0:8], tri32[:, :], tot32[0][:, :],
                                       start=True, stop=False),
                    r=[L(tri32), L(tot32[0])], w=[PR(3)])
                p.t(lambda e: e.matmul(P[3][0:32, 8:16], tri32[:, :], tot32[1][:, :],
                                       start=False, stop=False),
                    r=[L(tri32), L(tot32[1])], w=[PR(3)])
                p.t(lambda e: e.matmul(P[4][0:1, 0:8], onesc32[:, :], tot32[0][:, :],
                                       start=True, stop=False),
                    r=[L(onesc32), L(tot32[0])], w=[PR(4)])
                p.t(lambda e: e.matmul(P[4][0:1, 8:16], onesc32[:, :], tot32[1][:, :],
                                       start=False, stop=True),
                    r=[L(onesc32), L(tot32[1])], w=[PR(4)])
                p.v(lambda e: e.tensor_copy(cntrow[:, :], P[4][0:1, 0:16]),
                    r=[PR(4)], w=[L(cntrow)])
                tok_ci = p.g(lambda e: e.dma_start(cc_in, cntrow[:, :]), r=[L(cntrow)], w=[])
                tok_cc = p.op("gpsimd", lambda e: e.collective_compute(
                    "AllGather", ALU.bypass, replica_groups=[list(range(8))],
                    ins=[cc_in], outs=[cc_out]), reads=[], writes=[],
                    extra=[tok_ci], inc="cc", inc_by=1)
                p.g(lambda e: e.dma_start(gath[:, :], cc_out), r=[], w=[L(gath)],
                    extra=[tok_cc])
                p.t(lambda e: e.matmul(P[4][0:1, 0:16], cmask[:, :], gath[:, :],
                                       start=True, stop=True),
                    r=[L(cmask), L(gath)], w=[PR(4)])
                p.v(lambda e: e.tensor_copy(baser[:, :], P[4][0:1, 0:16]),
                    r=[PR(4)], w=[L(baser)])
                p.t(lambda e: e.matmul(P[3][0:32, 0:8], ones1[:, 0:32], baser[:, 0:8],
                                       start=False, stop=False),
                    r=[L(ones1), L(baser, 0, 8)], w=[PR(3)])
                p.t(lambda e: e.matmul(P[3][0:32, 8:16], ones1[:, 0:32], baser[:, 8:16],
                                       start=False, stop=True),
                    r=[L(ones1), L(baser, 8, 16)], w=[PR(3)])
                p.v(lambda e: e.tensor_copy(offs32[0][:, :], P[3][0:32, 0:8]),
                    r=[PR(3)], w=[L(offs32[0])])
                p.v(lambda e: e.tensor_copy(offs32[1][:, :], P[3][0:32, 8:16]),
                    r=[PR(3)], w=[L(offs32[1])])
                for k_ in range(2):
                    p.g(lambda e, k_=k_: e.dma_start(orow[k_], offs32[k_][:, :]),
                        r=[L(offs32[k_])], w=[("orow", k_ * 256, (k_ + 1) * 256)])
                    p.g(lambda e, k_=k_: e.dma_start(
                        rowpool[0:1, 512 + 256 * k_: 768 + 256 * k_],
                        orow[k_].rearrange("s e -> (s e)")[None, :]),
                        r=[("orow", k_ * 256, (k_ + 1) * 256)],
                        w=[L(rowpool, 2048 + 1024 * k_, 3072 + 1024 * k_)])
                for si in range(NSUB):
                    pb = 4 + (si % 2)
                    o1 = (id(oh1_all), si * 8, (si + 1) * 8)
                    o2 = (id(oh2_all), si * 8, (si + 1) * 8)
                    p.t(lambda e, si=si, pb=pb: e.matmul(
                        P[pb][:, 0:8], triuIb[:, :], oh1_all[:, si * 8:(si + 1) * 8],
                        start=True, stop=False),
                        r=[L(triuIb), o1], w=[PR(pb)])
                    p.t(lambda e, si=si, pb=pb: e.matmul(
                        P[pb][:, 0:8], ones1[:, :],
                        rowpool.ap()[0:1, 512 + si * 8: 512 + (si + 1) * 8],
                        start=False, stop=False),
                        r=[L(ones1), L(rowpool, 2048 + si * 32, 2048 + (si + 1) * 32)], w=[PR(pb)])
                    p.t(lambda e, si=si, pb=pb: e.matmul(
                        P[pb][:, 8:16], triuIb[:, :], oh2_all[:, si * 8:(si + 1) * 8],
                        start=False, stop=False),
                        r=[L(triuIb), o2], w=[PR(pb)])
                    p.t(lambda e, si=si, pb=pb: e.matmul(
                        P[pb][:, 8:16], ones1[:, :],
                        rowpool.ap()[0:1, 768 + si * 8: 768 + (si + 1) * 8],
                        start=False, stop=True),
                        r=[L(ones1), L(rowpool, 3072 + si * 32, 3072 + (si + 1) * 32)], w=[PR(pb)])
                    tp = top[si % 4]
                    p.v(lambda e, tp=tp, pb=pb: e.tensor_single_scalar(
                        tp[:, 8:16], P[pb][:, 0:8], CAP, ALU.is_le),
                        r=[PR(pb)], w=[L(tp, 8, 16)])
                    p.v(lambda e, tp=tp, pb=pb: e.tensor_single_scalar(
                        tp[:, 16:24], P[pb][:, 8:16], CAP, ALU.is_le),
                        r=[PR(pb)], w=[L(tp, 16, 24)])
                    p.v(lambda e, tp=tp, si=si: e.tensor_mul(
                        tp[:, 24:32], oh1_all[:, si * 8:(si + 1) * 8], tp[:, 8:16]),
                        r=[o1, L(tp, 8, 16)], w=[L(tp, 24, 32)])
                    p.v(lambda e, tp=tp, si=si: e.tensor_scalar(
                        tp[:, 40:48], tp[:, 24:32], wcols[:, si:si + 1], 0.5, ALU.mult, ALU.mult),
                        r=[L(tp, 24, 32), L(wcols, si, si + 1)], w=[L(tp, 40, 48)])
                    p.v(lambda e, tp=tp, si=si: e.tensor_mul(
                        tp[:, 32:40], oh2_all[:, si * 8:(si + 1) * 8], tp[:, 16:24]),
                        r=[o2, L(tp, 16, 24)], w=[L(tp, 32, 40)])
                    p.v(lambda e, tp=tp, si=si: e.tensor_scalar(
                        tp[:, 24:32], tp[:, 32:40], wcols[:, 32 + si:33 + si], 0.5, ALU.mult, ALU.mult),
                        r=[L(tp, 32, 40), L(wcols, 32 + si, 33 + si)], w=[L(tp, 24, 32)])
                    p.v(lambda e, tp=tp, si=si: e.tensor_add(
                        w_all[:, si * 8:(si + 1) * 8], tp[:, 40:48], tp[:, 24:32]),
                        r=[L(tp, 40, 48), L(tp, 24, 32)], w=[L(w_all, si * 8, (si + 1) * 8)])
                    p.v(lambda e, si=si: e.tensor_copy(
                        wbt[:, :], w_all[:, si * 8:(si + 1) * 8]),
                        r=[L(w_all, si * 8, (si + 1) * 8)], w=[L(wbt)])
                    p.t(lambda e, si=si: e.transpose(
                        PB[0][0:8, 0:128], wbt[:, :], identb[:, :]),
                        r=[L(wbt), L(identb)], w=[PR(6)])
                    p.v(lambda e, si=si: e.tensor_copy(
                        wT_all[0:8, si * 128:(si + 1) * 128], PB[0][0:8, 0:128]),
                        r=[PR(6)], w=[L(wT_all, si * 128, (si + 1) * 128)])

                if "d_lg" in dbg_out and pi == 0:
                    p.y(lambda e: e.dma_start(dbg_out["d_lg"], dbg_lg[:, :]), r=[L(dbg_lg)], w=[])
                if "d_w" in dbg_out and pi == 0:
                    p.y(lambda e: e.dma_start(dbg_out["d_w"], w_all[:, :]), r=[L(w_all)], w=[])
                    p.y(lambda e: e.dma_start(dbg_out["d_counts"], gath[:, :]), r=[L(gath)], w=[])
                    p.y(lambda e: e.dma_start(dbg_out["d_mu"], smu[:, :]), r=[L(smu)], w=[])
                    p.y(lambda e: e.dma_start(dbg_out["d_rs"], srs[:, :]), r=[L(srs)], w=[])
                    p.g(lambda e: e.dma_start(dbg_out["d_oh1"], oh1_all[:, :]), r=[L(oh1_all)], w=[])
                    p.g(lambda e: e.dma_start(dbg_out["d_oh2"], oh2_all[:, :]), r=[L(oh2_all)], w=[])

                w1v = [A[:, i * 4096:(i + 1) * 4096] for i in range(3)]
                w2v = [A[:, (3 + i) * 4096:(4 + i) * 4096] for i in range(3)]

                w1sched = [(g_, e_, pn) for g_ in range(NG)
                           for e_ in range(E) for pn in range(8)]
                w2sched = [(g_, e_, dh, q) for g_ in range(NG)
                           for e_ in range(E) for dh in range(2) for q in range(4)]

                def emit_w1(k):
                    if k >= len(w1sched):
                        return
                    g_, e2, pn = w1sched[k]
                    slot = k % 3
                    if pi == 0 and g_ == 0:
                        p.g(lambda e, e2=e2, pn=pn, slot=slot: e.dma_start(
                            w1v[slot].rearrange("p (dc f) -> p dc f", dc=8),
                            dr["w1s"][e2, pn]),
                            r=[], w=[L(A, slot * 4096, (slot + 1) * 4096)])
                        p.y(lambda e, e2=e2, pn=pn, slot=slot: e.dma_start(
                            w1c[e2, pn],
                            w1v[slot].rearrange("p (dc f) -> p dc f", dc=8)),
                            r=[L(A, slot * 4096, (slot + 1) * 4096)],
                            w=[("w1c", e2 * 8 + pn, e2 * 8 + pn + 1)])
                    else:
                        p.y(lambda e, e2=e2, pn=pn, slot=slot: e.dma_start(
                            w1v[slot].rearrange("p (dc f) -> p dc f", dc=8),
                            w1c[e2, pn]),
                            r=[("w1c", e2 * 8 + pn, e2 * 8 + pn + 1)],
                            w=[L(A, slot * 4096, (slot + 1) * 4096)])

                def emit_w2(k):
                    if k >= len(w2sched):
                        return
                    g_, e2, dh, q = w2sched[k]
                    slot = k % 3
                    if pi == 0 and g_ == 0:
                        p.g(lambda e, e2=e2, dh=dh, q=q, slot=slot: e.dma_start(
                            w2v[slot].rearrange("p (j f) -> p j f", j=8),
                            dr["w2s"][e2, dh, q]),
                            r=[], w=[L(A, (3 + slot) * 4096, (4 + slot) * 4096)])
                        p.y(lambda e, e2=e2, dh=dh, q=q, slot=slot: e.dma_start(
                            w2c[e2, dh, q],
                            w2v[slot].rearrange("p (j f) -> p j f", j=8)),
                            r=[L(A, (3 + slot) * 4096, (4 + slot) * 4096)],
                            w=[("w2c", e2 * 8 + dh * 4 + q, e2 * 8 + dh * 4 + q + 1)])
                    else:
                        p.y(lambda e, e2=e2, dh=dh, q=q, slot=slot: e.dma_start(
                            w2v[slot].rearrange("p (j f) -> p j f", j=8),
                            w2c[e2, dh, q]),
                            r=[("w2c", e2 * 8 + dh * 4 + q, e2 * 8 + dh * 4 + q + 1)],
                            w=[L(A, (3 + slot) * 4096, (4 + slot) * 4096)])

                for k in range(3):
                    emit_w1(k)
                    emit_w2(k)

                for g in range(NG):
                    group_rload(src, src_name, g, Rg[0])
                    ln_make_xlg(None, Rg[0], g % 2, smu, srs, g)
                    if "d_xl2" in dbg_out and pi == 0 and g == 0:
                        p.g(lambda e: e.dma_start(dbg_out["d_xl2"], xlg[0][:, :]),
                            r=[L(xlg[0], 0, 4096)], w=[])
                    rcur = 0
                    for e_ in range(E):
                        for fc in range(32):
                            pn = fc // 4
                            slot1 = (g * 64 + e_ * 8 + pn) % 3
                            pbk = fc % 2
                            for dc in range(8):
                                p.t(lambda e, fc=fc, dc=dc, pbk=pbk, slot1=slot1, g=g: e.matmul(
                                    P[pbk][:, :],
                                    w1v[slot1][:, dc * 512 + (fc % 4) * 128: dc * 512 + (fc % 4) * 128 + 128],
                                    xlg[g % 2][:, dc * 512:(dc + 1) * 512],
                                    start=(dc == 0), stop=(dc == 7)),
                                    r=[L(A, slot1 * 4096 + dc * 512 + (fc % 4) * 128,
                                         slot1 * 4096 + dc * 512 + (fc % 4) * 128 + 128),
                                       L(xlg[g % 2], dc * 512, (dc + 1) * 512)],
                                    w=[PR(pbk)])
                            if fc % 4 == 3:
                                emit_w1(g * 64 + e_ * 8 + pn + 3)
                            p.s(lambda e, fc=fc, pbk=pbk, e_=e_: e.activation(
                                B[:, fc * 512:(fc + 1) * 512], P[pbk][:, :], AF.Gelu,
                                bias=b1c[:, e_ * 32 + fc: e_ * 32 + fc + 1]),
                                r=[PR(pbk), L(b1c, e_ * 32 + fc, e_ * 32 + fc + 1)],
                                w=[L(B, fc * 512, (fc + 1) * 512)])
                            if "d_h0" in dbg_out and pi == 0 and g == 0 and e_ == 0 and fc == 0:
                                p.g(lambda e: e.dma_start(dbg_out["d_h0"], B[:, 0:512]),
                                    r=[L(B, 0, 512)], w=[])
                                p.v(lambda e, pbk=pbk: e.tensor_copy(
                                    Ot1[0][:, :] and Sf[:, 1024:1536], P[pbk][:, :]),
                                    r=[PR(pbk)], w=[L(Sf, 1024, 1536)],
                                    extra=[p.last_tok("ss")])
                                p.y(lambda e: e.dma_start(dbg_out["d_hp"], Sf[:, 1024:1536]),
                                    r=[L(Sf, 1024, 1536)], w=[])
                        for dh in range(2):
                            for q in range(4):
                                slot2 = (g * 64 + e_ * 8 + dh * 4 + q) % 3
                                for j in range(8):
                                    fc = 8 * q + j
                                    for sl in range(SPG):
                                        p.t(lambda e, fc=fc, sl=sl, slot2=slot2, j=j, q=q: e.matmul(
                                            P[2 + sl][:, :],
                                            B[:, fc * 512 + sl * 128: fc * 512 + (sl + 1) * 128],
                                            w2v[slot2][:, j * 512:(j + 1) * 512],
                                            start=(q == 0 and j == 0), stop=(q == 3 and j == 7)),
                                            r=[L(B, fc * 512 + sl * 128, fc * 512 + (sl + 1) * 128),
                                               L(A, (3 + slot2) * 4096 + j * 512,
                                                 (3 + slot2) * 4096 + (j + 1) * 512)],
                                            w=[PR(2 + sl)])
                                emit_w2(g * 64 + e_ * 8 + dh * 4 + q + 3)
                            for sl in range(SPG):
                                si = g * SPG + sl
                                lo = sl * 1024 + dh * 512
                                p.v(lambda e, sl=sl, lo=lo, si=si, e_=e_, rcur=rcur: e.scalar_tensor_tensor(
                                    Rg[1 - rcur][:, lo:lo + 512],
                                    P[2 + sl][:, :], w_all[:, si * 8 + e_: si * 8 + e_ + 1],
                                    Rg[rcur][:, lo:lo + 512],
                                    ALU.mult, ALU.add),
                                    r=[PR(2 + sl), L(w_all, si * 8 + e_, si * 8 + e_ + 1),
                                       L(Rg[rcur], lo, lo + 512)],
                                    w=[L(Rg[1 - rcur], lo, lo + 512)])
                        if "d_r0" in dbg_out and pi == 0 and g == 0 and e_ == 0:
                            _rc = rcur
                            p.y(lambda e, rc=_rc: e.dma_start(
                                dbg_out["d_r0"], Rg[1 - rc][:, 0:1024]),
                                r=[L(Rg[1 - _rc], 0, 1024)], w=[])
                        rcur = 1 - rcur
                    for sl in range(SPG):
                        si = g * SPG + sl
                        for dh in range(2):
                            p.t(lambda e, si=si, dh=dh, sl=sl: e.matmul(
                                P[2 + sl][:, :], wT_all[0:8, si * 128:(si + 1) * 128],
                                b2c[:, dh * 512:(dh + 1) * 512],
                                start=True, stop=True),
                                r=[L(wT_all, si * 128, (si + 1) * 128), L(b2c)],
                                w=[PR(2 + sl)])
                            lo = sl * 1024 + dh * 512
                            p.v(lambda e, sl=sl, lo=lo, rcur=rcur: e.tensor_add(
                                Rg[1 - rcur][:, lo:lo + 512],
                                Rg[rcur][:, lo:lo + 512],
                                P[2 + sl][:, :]),
                                r=[PR(2 + sl), L(Rg[rcur], lo, lo + 512)],
                                w=[L(Rg[1 - rcur], lo, lo + 512)])
                    dst_ap = dst.rearrange("(gg s pp) d -> gg pp s d", gg=NG, s=SPG, pp=128)[g]
                    p.g(lambda e, dap=dst_ap, rcur=rcur: e.dma_start(
                        dap, Rg[1 - rcur].ap().rearrange("p (s d) -> p s d", s=SPG)),
                        r=[L(Rg[1 - rcur])], w=[(dst_name, g, g + 1)])
                return


            def attn_pass(src, src_name, dst, dst_name):
                wq_sb = A[:, 0:8192]
                wk_sb = A[:, 8192:16384]
                wv_sb = A[:, 16384:24576]
                wo_sb = B[:, 0:8192]
                for dst_t, lo, nm in [(wq_sb, 0, "wqs"), (wk_sb, 8192, "wks"),
                                      (wv_sb, 16384, "wvs")]:
                    p.g(lambda e, dst_t=dst_t, nm=nm: e.dma_start(
                        dst_t.rearrange("p (dc c) -> p dc c", dc=8), dr[nm]),
                        r=[], w=[L(A, lo, lo + 8192)])
                p.g(lambda e: e.dma_start(
                    wo_sb.rearrange("p (oc c) -> p oc c", oc=8), dr["wos"]),
                    r=[], w=[L(B, 0, 8192)])

                def QTs(qc, c=None):
                    lo = 8192 + qc * 512 + (0 if c is None else c * 128)
                    hi = 8192 + (qc + 1) * 512 if c is None else lo + 128
                    return B[:, lo:hi], L(B, lo, hi)

                def KTs(qc, c=None):
                    lo = 12288 + qc * 512 + (0 if c is None else c * 128)
                    hi = 12288 + (qc + 1) * 512 if c is None else lo + 128
                    return B[:, lo:hi], L(B, lo, hi)

                def proj(qc, wofs, dst_lo, gs, do_sq):
                    pb = qc % 2
                    for dc in range(8):
                        p.t(lambda e, qc=qc, dc=dc, pb=pb, gs=gs, wofs=wofs: e.matmul(
                            P[pb][:, :],
                            A[:, wofs + dc * 1024 + qc * 128: wofs + dc * 1024 + (qc + 1) * 128],
                            xlg[gs][:, dc * 512:(dc + 1) * 512],
                            start=(dc == 0), stop=(dc == 7)),
                            r=[L(A, wofs + dc * 1024 + qc * 128,
                                 wofs + dc * 1024 + (qc + 1) * 128),
                               L(xlg[gs], dc * 512, (dc + 1) * 512)],
                            w=[PR(pb)])
                    p.v(lambda e, pb=pb, dst_lo=dst_lo: e.tensor_copy(
                        B[:, dst_lo:dst_lo + 512], P[pb][:, :]),
                        r=[PR(pb)], w=[L(B, dst_lo, dst_lo + 512)])
                    if do_sq:
                        p.s(lambda e, dst_lo=dst_lo: e.activation(
                            qsq[0][:, :], B[:, dst_lo:dst_lo + 512], AF.Square),
                            r=[L(B, dst_lo, dst_lo + 512)], w=[L(qsq[0])])
                        h = qc // 2
                        base = 0 if wofs == 0 else 16
                        for sl in range(SPG):
                            p.t(lambda e, sl=sl, h=h, qc=qc, base=base: e.matmul(
                                P[2][:, base + h * 4 + sl: base + h * 4 + sl + 1],
                                qsq[0][:, sl * 128:(sl + 1) * 128], onesc[:, :],
                                start=(base == 0 and qc == 0 and sl == 0), stop=False),
                                r=[L(qsq[0], sl * 128, (sl + 1) * 128), L(onesc)],
                                w=[PR(2)])
                        if wofs != 0:
                            p.t(lambda e, qc=qc: e.matmul(
                                P[3][0:1, 0:512], onesc[:, :], qsq[0][:, :],
                                start=(qc % 2 == 0), stop=(qc % 2 == 1)),
                                r=[L(qsq[0]), L(onesc)], w=[PR(3)])
                            if qc % 2 == 1:
                                h = qc // 2
                                p.s(lambda e: e.activation(
                                    qsq[1][0:1, 0:512] if False else qsq[0][0:1, 0:512],
                                    P[3][0:1, 0:512], AF.Ln),
                                    r=[PR(3)], w=[L(qsq[0], 0, 512)])
                                p.s(lambda e, h=h: e.activation(
                                    nrowb[0:1, h * 512:(h + 1) * 512],
                                    qsq[0][0:1, 0:512], AF.Exp, scale=-0.5),
                                    r=[L(qsq[0], 0, 512)],
                                    w=[L(rowpool, h * 1024, (h + 1) * 1024)])

                for b in range(2):
                    p.v(lambda e: e.memset(Sf[:, :], 0.0), r=[], w=[L(Sf)])
                    p.v(lambda e: e.memset(Sbf[:, :], 0.0), r=[], w=[("scrf", 0, 4096)])
                    for a in range(4):
                        g = b * 4 + a
                        gs = g % 2
                        group_rload(src, src_name, g, Rg[0])
                        for sl in range(SPG):
                            si = g * SPG + sl
                            lo = sl * 1024
                            p.v(lambda e, si=si, lo=lo: e.reduce_sum(
                                msum[:, si % 4: si % 4 + 1], Rg[0][:, lo:lo + 1024], AX.X),
                                r=[L(Rg[0], lo, lo + 1024)], w=[L(msum, si % 4, si % 4 + 1)])
                            p.v(lambda e, si=si: e.tensor_scalar_mul(
                                att_mu[:, si:si + 1], msum[:, si % 4: si % 4 + 1], 1.0 / D),
                                r=[L(msum, si % 4, si % 4 + 1)], w=[L(att_mu, si, si + 1)])
                            for hf in range(2):
                                p.v(lambda e, lo=lo, hf=hf: e.tensor_mul(
                                    qsq[0][:, :], Rg[0][:, lo + hf * 512:lo + (hf + 1) * 512],
                                    Rg[0][:, lo + hf * 512:lo + (hf + 1) * 512]),
                                    r=[L(Rg[0], lo + hf * 512, lo + (hf + 1) * 512)],
                                    w=[L(qsq[0])])
                                p.v(lambda e, si=si, hf=hf: e.reduce_sum(
                                    mus2[:, 8 + hf: 9 + hf], qsq[0][:, :], AX.X),
                                    r=[L(qsq[0])], w=[L(mus2, 8 + hf, 9 + hf)])
                            p.v(lambda e, si=si: e.tensor_add(
                                var_c[:, si % 4: si % 4 + 1], mus2[:, 8:9], mus2[:, 9:10]),
                                r=[L(mus2, 8, 10)], w=[L(var_c, si % 4, si % 4 + 1)])
                            p.v(lambda e, si=si: e.tensor_scalar(
                                mus2[:, si % 4: si % 4 + 1], att_mu[:, si:si + 1],
                                att_mu[:, si:si + 1], -EPS_LN, ALU.mult, ALU.add),
                                r=[L(att_mu, si, si + 1)], w=[L(mus2, si % 4, si % 4 + 1)])
                            p.v(lambda e, si=si: e.scalar_tensor_tensor(
                                var_c[:, 4 + si % 4: 5 + si % 4], var_c[:, si % 4: si % 4 + 1],
                                1.0 / D, mus2[:, si % 4: si % 4 + 1], ALU.mult, ALU.subtract),
                                r=[L(var_c, si % 4, si % 4 + 1), L(mus2, si % 4, si % 4 + 1)],
                                w=[L(var_c, 4 + si % 4, 5 + si % 4)])
                        c0 = g * SPG
                        p.s(lambda e: e.activation(var2[:, 0:4], var_c[:, 4:8], AF.Ln),
                            r=[L(var_c, 4, 8)], w=[L(var2, 0, 4)])
                        p.s(lambda e, c0=c0: e.activation(
                            att_rs[:, c0:c0 + 4], var2[:, 0:4], AF.Exp, scale=-0.5),
                            r=[L(var2, 0, 4)], w=[L(att_rs, c0, c0 + 4)])
                        ln_make_xlg(None, Rg[0], gs, att_mu, att_rs, g)

                        for qc in range(8):
                            proj(qc, 0, 8192 + qc * 512, gs, True)
                        ev_q = p.v(lambda e: e.tensor_copy(sumsq[:, 0:16], P[2][:, 0:16]),
                                   r=[PR(2)], w=[L(sumsq, 0, 16)])
                        for qc in range(8):
                            proj(qc, 8192, 12288 + qc * 512, gs, True)
                        p.v(lambda e: e.tensor_copy(sumsq[:, 16:32], P[2][:, 16:32]),
                            r=[PR(2)], w=[L(sumsq, 16, 32)])
                        p.s(lambda e: e.activation(var2[:, 0:32], sumsq[:, 0:32], AF.Ln),
                            r=[L(sumsq, 0, 32)], w=[L(var2, 0, 32)])
                        p.s(lambda e: e.activation(rstdc[:, 0:32], var2[:, 0:32], AF.Exp,
                                                   scale=-0.5),
                            r=[L(var2, 0, 32)], w=[L(rstdc, 0, 32)])
                        p.v(lambda e: e.tensor_scalar_mul(rstdc[:, 32:48], rstdc[:, 16:32], -1.0),
                            r=[L(rstdc, 16, 32)], w=[L(rstdc, 32, 48)])
                        for sl in range(SPG):
                            for half in range(2):
                                pb = 4
                                for dc in range(8):
                                    p.t(lambda e, sl=sl, half=half, dc=dc, gs=gs: e.matmul(
                                        P[4][:, :],
                                        xlg[gs][:, dc * 512 + sl * 128: dc * 512 + (sl + 1) * 128],
                                        wv_sb[:, dc * 1024 + half * 512: dc * 1024 + (half + 1) * 512],
                                        start=(dc == 0), stop=(dc == 7)),
                                        r=[L(xlg[gs], dc * 512 + sl * 128, dc * 512 + (sl + 1) * 128),
                                           L(A, 16384 + dc * 1024 + half * 512,
                                             16384 + dc * 1024 + (half + 1) * 512)],
                                        w=[PR(4)])
                                p.v(lambda e, sl=sl, half=half: e.tensor_copy(
                                    Vsb[:, sl * 1024 + half * 512: sl * 1024 + (half + 1) * 512],
                                    P[4][:, :]),
                                    r=[PR(4)],
                                    w=[L(SCRBF, sl * 1024 + half * 512, sl * 1024 + (half + 1) * 512)])
                        for sl in range(SPG):
                            for dc in range(8):
                                p.t(lambda e, sl=sl, dc=dc, gs=gs: e.matmul(
                                    P[5][:, 0:4],
                                    xlg[gs][:, dc * 512 + sl * 128: dc * 512 + (sl + 1) * 128],
                                    wbb[:, dc * 4:(dc + 1) * 4],
                                    start=(dc == 0), stop=(dc == 7)),
                                    r=[L(xlg[gs], dc * 512 + sl * 128, dc * 512 + (sl + 1) * 128),
                                       L(wbb)],
                                    w=[PR(5)])
                            p.v(lambda e, sl=sl: e.tensor_scalar_mul(
                                ebuf[:, sl * 4:(sl + 1) * 4], P[5][:, 0:4], -1.0),
                                r=[PR(5)], w=[L(ebuf, sl * 4, (sl + 1) * 4)])
                            p.s(lambda e, sl=sl: e.activation(
                                ebuf[:, 16 + sl * 4: 16 + (sl + 1) * 4],
                                ebuf[:, sl * 4:(sl + 1) * 4], AF.Exp),
                                r=[L(ebuf, sl * 4, (sl + 1) * 4)],
                                w=[L(ebuf, 16 + sl * 4, 16 + (sl + 1) * 4)])
                            p.v(lambda e, sl=sl: e.tensor_scalar_add(
                                ebuf[:, 32 + sl * 4: 32 + (sl + 1) * 4],
                                ebuf[:, 16 + sl * 4: 16 + (sl + 1) * 4], 1.0),
                                r=[L(ebuf, 16 + sl * 4, 16 + (sl + 1) * 4)],
                                w=[L(ebuf, 32 + sl * 4, 32 + (sl + 1) * 4)])
                            p.v(lambda e, sl=sl: e.reciprocal(
                                beta_s[:, sl * 4:(sl + 1) * 4],
                                ebuf[:, 32 + sl * 4: 32 + (sl + 1) * 4]),
                                r=[L(ebuf, 32 + sl * 4, 32 + (sl + 1) * 4)],
                                w=[L(beta_s, sl * 4, (sl + 1) * 4)])
                            p.v(lambda e, sl=sl: e.tensor_mul(
                                rstdc[:, 48 + sl * 4: 48 + (sl + 1) * 4],
                                beta_s[:, sl * 4:(sl + 1) * 4],
                                rstdc[:, 16 + sl * 4: 16 + (sl + 1) * 4]),
                                r=[L(beta_s, sl * 4, (sl + 1) * 4), L(rstdc, 16 + sl * 4, 16 + (sl + 1) * 4)],
                                w=[L(rstdc, 48 + sl * 4, 48 + (sl + 1) * 4)])

                        if "a_ss" in dbg_out and b == 0 and a == 0:
                            p.y(lambda e: e.dma_start(dbg_out["a_ss"], sumsq[:, :]),
                                r=[L(sumsq)], w=[])
                            p.y(lambda e: e.dma_start(dbg_out["a_rs"], rstdc[:, :]),
                                r=[L(rstdc)], w=[])
                            p.y(lambda e: e.dma_start(dbg_out["a_bs"], beta_s[:, :]),
                                r=[L(beta_s)], w=[])
                            p.g(lambda e: e.dma_start(dbg_out["a_kr"], rowpool[0:1, :]),
                                r=[L(rowpool, 0, 4096)], w=[])
                            p.g(lambda e: e.dma_start(dbg_out["a_qt"], B[:, 8192:12288]),
                                r=[L(B, 8192, 12288)], w=[])
                            p.g(lambda e: e.dma_start(dbg_out["a_kt"], B[:, 12288:16384]),
                                r=[L(B, 12288, 16384)], w=[])
                            p.g(lambda e: e.dma_start(dbg_out["a_v"], Vsb[:, :]),
                                r=[L(SCRBF, 0, 4096)], w=[])
                        bankpair = {0: (0, 1), 1: (2, 3), 2: (4, 5), 3: (0, 1)}
                        for c in range(4):
                            for h in range(NH):
                                al, be = bankpair[h]
                                qr = (id(rstdc), 0 + h * 4 + c, 1 + h * 4 + c)
                                kr = (id(rstdc), 16 + h * 4 + c, 17 + h * 4 + c)
                                nk = (id(rstdc), 32 + h * 4 + c, 33 + h * 4 + c)
                                bk = (id(rstdc), 48 + c * 4 + h, 49 + c * 4 + h)
                                bcol = (id(beta_s), c * 4 + h, c * 4 + h + 1)
                                Zh, Xh, Uh = Zb[h], Xb[h], Ub[h]
                                sA = (id(Sf), (2 * h) * 256, (2 * h + 1) * 256)
                                sB = (id(Sf), (2 * h + 1) * 256, (2 * h + 2) * 256)
                                sbA = ("scrf", (2 * h) * 512, (2 * h + 1) * 512, (2 * h) * 256)
                                sbB = ("scrf", (2 * h + 1) * 512, (2 * h + 2) * 512, (2 * h + 1) * 256)
                                for i, qc in enumerate((2 * h, 2 * h + 1)):
                                    kap, krd = KTs(qc, c)
                                    p.t(lambda e, kap=kap, al=al, i=i: e.matmul(
                                        P[al][:, 0:128], kap, kap, start=(i == 0), stop=(i == 1)),
                                        r=[krd], w=[PR(al)])
                                p.t(lambda e, h=h, c=c, be=be: e.matmul(
                                    P[be][:, 0:128], ones1b[:, :],
                                    nrowb[0:1, h * 512 + c * 128: h * 512 + (c + 1) * 128],
                                    start=True, stop=True),
                                    r=[L(ones1b),
                                       (id(rowpool), (h * 512 + c * 128) * 2, (h * 512 + (c + 1) * 128) * 2)],
                                    w=[PR(be)])
                                p.v(lambda e, h=h, be=be: e.tensor_copy(krbs[h][:, :], P[be][:, 0:128]),
                                    r=[PR(be)], w=[L(krbs[h])])
                                if "a_m" in dbg_out and b == 0 and a == 0 and c == 0 and h == 0:
                                    p.v(lambda e, al=al: e.tensor_copy(
                                        qsq[0][:, 0:128], P[al][:, 0:128]),
                                        r=[PR(al)], w=[L(qsq[0], 0, 128)])
                                    p.y(lambda e: e.dma_start(dbg_out["a_m"], qsq[0][:, 0:128]),
                                        r=[L(qsq[0], 0, 128)], w=[])
                                p.v(lambda e, h=h, al=al: e.tensor_tensor(
                                    Zt[h][:, 0:128], P[al][:, 0:128], triuSb[:, :], ALU.mult),
                                    r=[PR(al), L(triuSb)], w=[L(Zt[h], 0, 128)])
                                p.v(lambda e, h=h: e.tensor_mul(
                                    Zt[h][:, 128:256], Zt[h][:, 0:128], krbs[h][:, :]),
                                    r=[L(Zt[h], 0, 128), L(krbs[h])], w=[L(Zt[h], 128, 256)])
                                p.v(lambda e, h=h, bk=bk, Zh=Zh: e.tensor_scalar_mul(
                                    Zh[:, 0:128], Zt[h][:, 128:256], rstdc[:, bk[1]:bk[2]]),
                                    r=[L(Zt[h], 128, 256), bk], w=[L(Zh, 0, 128)])
                                if "a_m" in dbg_out and b == 0 and a == 0 and c == 0 and h == 0:
                                    p.y(lambda e: e.dma_start(dbg_out["a_zt"], Zt[0][:, :]),
                                        r=[L(Zt[0])], w=[])
                                    p.y(lambda e: e.dma_start(dbg_out["a_krb"], krbs[0][:, :]),
                                        r=[L(krbs[0])], w=[])
                                if "a_z0i" in dbg_out and b == 0 and a == 0 and c == 0 and h == 0:
                                    p.g(lambda e: e.dma_start(dbg_out["a_z0i"], Zb[0][:, 0:128]),
                                        r=[L(Zb[0], 0, 128)], w=[])
                                pbt = 6 + (h % 2)
                                p.t(lambda e, h=h, pbt=pbt: e.transpose(
                                    PB[pbt - 6][:, 0:128], Zb[h][:, 0:128], identb[:, :]),
                                    r=[L(Zh, 0, 128), L(identb)], w=[PR(pbt)])
                                p.v(lambda e, h=h, pbt=pbt, Zh=Zh: e.tensor_copy(
                                    Zh[:, 7 * 128:8 * 128], PB[pbt - 6][:, 0:128]),
                                    r=[PR(pbt)], w=[L(Zh, 7 * 128, 8 * 128)])
                                for j in range(1, 7):
                                    zt_prev = 7 * 128 if j % 2 == 1 else 8 * 128
                                    zt_new = 8 * 128 if j % 2 == 1 else 7 * 128
                                    bank = al if j % 2 == 1 else be
                                    p.t(lambda e, j=j, h=h, bank=bank, zt_prev=zt_prev, Zh=Zh: e.matmul(
                                        P[bank][:, 0:128], Zh[:, zt_prev:zt_prev + 128],
                                        Zh[:, (j - 1) * 128: j * 128], start=True, stop=True),
                                        r=[L(Zh, zt_prev, zt_prev + 128),
                                           L(Zh, (j - 1) * 128, j * 128)],
                                        w=[PR(bank)])
                                    p.v(lambda e, j=j, h=h, bank=bank, Zh=Zh: e.tensor_copy(
                                        Zh[:, j * 128:(j + 1) * 128], P[bank][:, 0:128]),
                                        r=[PR(bank)], w=[L(Zh, j * 128, (j + 1) * 128)])
                                    if j < 6:
                                        bank2 = be if j % 2 == 1 else al
                                        p.t(lambda e, j=j, h=h, bank2=bank2, zt_prev=zt_prev, Zh=Zh: e.matmul(
                                            P[bank2][:, 0:128], Zh[:, (j - 1) * 128: j * 128],
                                            Zh[:, zt_prev:zt_prev + 128], start=True, stop=True),
                                            r=[L(Zh, zt_prev, zt_prev + 128),
                                               L(Zh, (j - 1) * 128, j * 128)],
                                            w=[PR(bank2)])
                                        p.v(lambda e, j=j, h=h, bank2=bank2, zt_new=zt_new, Zh=Zh: e.tensor_copy(
                                            Zh[:, zt_new:zt_new + 128], P[bank2][:, 0:128]),
                                            r=[PR(bank2)], w=[L(Zh, zt_new, zt_new + 128)])
                                for i, qc in enumerate((2 * h, 2 * h + 1)):
                                    kap, krd = KTs(qc, c)
                                    ss_ = sbA if i == 0 else sbB
                                    p.t(lambda e, kap=kap, al=al, i=i, h=h, ss_=ss_: e.matmul(
                                        P[al][:, 0:256], kap,
                                        Sbf[:, ss_[3]:ss_[3] + 256], start=(i == 0), stop=(i == 1)),
                                        r=[krd, ss_[:3]], w=[PR(al)])
                                p.v(lambda e, h=h, c=c, al=al, nk=nk, Xh=Xh: e.scalar_tensor_tensor(
                                    Xh[:, 0:256], P[al][:, 0:256], rstdc[:, nk[1]:nk[2]],
                                    Vsb[:, c * 1024 + h * 256: c * 1024 + (h + 1) * 256],
                                    ALU.mult, ALU.add),
                                    r=[PR(al), nk,
                                       L(SCRBF, c * 1024 + h * 256, c * 1024 + (h + 1) * 256)],
                                    w=[L(Xh, 0, 256)])
                                xcur = 0
                                for j in range(6, -1, -1):
                                    bank = al if j % 2 == 0 else be
                                    p.t(lambda e, j=j, h=h, bank=bank, xcur=xcur, Zh=Zh, Xh=Xh: e.matmul(
                                        P[bank][:, 0:256], Zh[:, j * 128:(j + 1) * 128],
                                        Xh[:, xcur * 256:(xcur + 1) * 256], start=True, stop=True),
                                        r=[L(Zh, j * 128, (j + 1) * 128),
                                           L(Xh, xcur * 256, (xcur + 1) * 256)],
                                        w=[PR(bank)])
                                    op_ = ALU.add if j > 0 else ALU.subtract
                                    p.v(lambda e, h=h, bank=bank, xcur=xcur, op_=op_, Xh=Xh: e.tensor_tensor(
                                        Xh[:, (1 - xcur) * 256:(2 - xcur) * 256],
                                        Xh[:, xcur * 256:(xcur + 1) * 256], P[bank][:, 0:256], op_),
                                        r=[L(Xh, xcur * 256, (xcur + 1) * 256), PR(bank)],
                                        w=[L(Xh, (1 - xcur) * 256, (2 - xcur) * 256)])
                                    xcur = 1 - xcur
                                p.v(lambda e, h=h, xcur=xcur, bcol=bcol, Uh=Uh, Xh=Xh: e.tensor_scalar_mul(
                                    Uh[:, 0:256], Xh[:, xcur * 256:(xcur + 1) * 256],
                                    beta_s[:, bcol[1]:bcol[2]]),
                                    r=[L(Xh, xcur * 256, (xcur + 1) * 256), bcol],
                                    w=[L(Uh, 0, 256)])
                                p.v(lambda e, h=h, xcur=xcur, bk=bk, Uh=Uh, Xh=Xh: e.tensor_scalar_mul(
                                    Uh[:, 256:512], Xh[:, xcur * 256:(xcur + 1) * 256],
                                    rstdc[:, bk[1]:bk[2]]),
                                    r=[L(Xh, xcur * 256, (xcur + 1) * 256), bk],
                                    w=[L(Uh, 256, 512)])
                                for i, qc in enumerate((2 * h, 2 * h + 1)):
                                    kap, krd = KTs(qc, c)
                                    qap, qrd = QTs(qc, c)
                                    p.t(lambda e, kap=kap, qap=qap, be=be, i=i: e.matmul(
                                        P[be][:, 0:128], kap, qap, start=(i == 0), stop=(i == 1)),
                                        r=[krd, qrd], w=[PR(be)])
                                p.v(lambda e, h=h, be=be: e.tensor_tensor(
                                    Zt[h][:, 0:128], P[be][:, 0:128], triuIb[:, :], ALU.mult),
                                    r=[PR(be), L(triuIb)], w=[L(Zt[h], 0, 128)])
                                p.v(lambda e, h=h, kr=kr: e.tensor_scalar_mul(
                                    LTb[h][:, :], Zt[h][:, 0:128], rstdc[:, kr[1]:kr[2]]),
                                    r=[L(Zt[h], 0, 128), kr], w=[L(LTb[h])])
                                for i, qc in enumerate((2 * h, 2 * h + 1)):
                                    qap, qrd = QTs(qc, c)
                                    ss_ = sbA if i == 0 else sbB
                                    p.t(lambda e, qap=qap, al=al, i=i, ss_=ss_: e.matmul(
                                        P[al][:, 0:256], qap, Sbf[:, ss_[3]:ss_[3] + 256],
                                        start=(i == 0), stop=False),
                                        r=[qrd, ss_[:3]], w=[PR(al)])
                                p.t(lambda e, h=h, al=al, Uh=Uh: e.matmul(
                                    P[al][:, 0:256], LTb[h][:, :], Uh[:, 0:256],
                                    start=False, stop=True),
                                    r=[L(LTb[h]), L(Uh, 0, 256)], w=[PR(al)])
                                p.v(lambda e, h=h, al=al, qr=qr: e.tensor_scalar_mul(
                                    Ot1[h][:, :], P[al][:, 0:256], rstdc[:, qr[1]:qr[2]]),
                                    r=[PR(al), qr], w=[L(Ot1[h])])
                                p.v(lambda e, h=h: e.tensor_mul(
                                    Ot2[h % 2][:, :], Ot1[h][:, :], Ot1[h][:, :]),
                                    r=[L(Ot1[h])], w=[L(Ot2[h % 2])])
                                p.v(lambda e, h=h: e.reduce_sum(
                                    Zt[h][:, 0:1], Ot2[h % 2][:, :], AX.X),
                                    r=[L(Ot2[h % 2])], w=[L(Zt[h], 0, 1)])
                                p.v(lambda e, h=h: e.tensor_scalar(
                                    Zt[h][:, 1:2], Zt[h][:, 0:1], 1.0 / 256, EPS_RMS,
                                    ALU.mult, ALU.add),
                                    r=[L(Zt[h], 0, 1)], w=[L(Zt[h], 1, 2)])
                                p.s(lambda e, h=h: e.activation(
                                    Zt[h][:, 2:3], Zt[h][:, 1:2], AF.Ln),
                                    r=[L(Zt[h], 1, 2)], w=[L(Zt[h], 2, 3)])
                                p.s(lambda e, h=h: e.activation(
                                    Zt[h][:, 3:4], Zt[h][:, 2:3], AF.Exp, scale=-0.5),
                                    r=[L(Zt[h], 2, 3)], w=[L(Zt[h], 3, 4)])
                                p.v(lambda e, h=h: e.tensor_scalar_mul(
                                    Ot2[h % 2][:, :], Ot1[h][:, :], Zt[h][:, 3:4]),
                                    r=[L(Ot1[h]), L(Zt[h], 3, 4)], w=[L(Ot2[h % 2])])
                                p.v(lambda e, h=h: e.tensor_mul(
                                    On[h][:, :], Ot2[h % 2][:, :], gbc[:, :]),
                                    r=[L(Ot2[h % 2]), L(gbc)], w=[L(On[h])])
                                for jj in range(2):
                                    pbt = 6 + ((h + jj) % 2)
                                    oc = 2 * h + jj
                                    p.t(lambda e, h=h, jj=jj, pbt=pbt: e.transpose(
                                        PB[pbt - 6][:, 0:128], On[h][:, jj * 128:(jj + 1) * 128],
                                        identb[:, :]),
                                        r=[L(On[h], jj * 128, (jj + 1) * 128), L(identb)],
                                        w=[PR(pbt)])
                                    p.v(lambda e, c=c, oc=oc, pbt=pbt: e.tensor_copy(
                                        otb[c % 2][:, oc * 128:(oc + 1) * 128],
                                        PB[pbt - 6][:, 0:128]),
                                        r=[PR(pbt)],
                                        w=[L(otb[c % 2], oc * 128, (oc + 1) * 128)])
                                for i, qc in enumerate((2 * h, 2 * h + 1)):
                                    kap, krd = KTs(qc, c)
                                    pbt = 6 + ((h + i) % 2)
                                    p.t(lambda e, kap=kap, pbt=pbt: e.transpose(
                                        PB[pbt - 6][:, 0:128], kap, identb[:, :]),
                                        r=[krd, L(identb)], w=[PR(pbt)])
                                    p.v(lambda e, h=h, pbt=pbt, Zh=Zh: e.tensor_copy(
                                        Zh[:, 128:256], PB[pbt - 6][:, 0:128]),
                                        r=[PR(pbt)], w=[L(Zh, 128, 256)])
                                    bank = al if i == 0 else be
                                    p.t(lambda e, h=h, bank=bank, Zh=Zh, Uh=Uh: e.matmul(
                                        P[bank][:, 0:256], Zh[:, 128:256], Uh[:, 256:512],
                                        start=True, stop=True),
                                        r=[L(Zh, 128, 256), L(Uh, 256, 512)], w=[PR(bank)])
                                    ss_f = sA if i == 0 else sB
                                    ss_b = (sbA if i == 0 else sbB)
                                    st_ = stmp[i]
                                    p.v(lambda e, bank=bank, ss_f=ss_f, st_=st_: e.tensor_add(
                                        st_[:, :], Sf[:, ss_f[1]:ss_f[2]], P[bank][:, 0:256]),
                                        r=[ss_f, PR(bank)], w=[L(st_)])
                                    p.v(lambda e, ss_f=ss_f, st_=st_: e.tensor_copy(
                                        Sf[:, ss_f[1]:ss_f[2]], st_[:, :]),
                                        r=[L(st_)], w=[ss_f])
                                    p.v(lambda e, ss_b=ss_b, st_=st_: e.tensor_copy(
                                        Sbf[:, ss_b[3]:ss_b[3] + 256], st_[:, :]),
                                        r=[L(st_)], w=[ss_b[:3]])
                            if "a_x0" in dbg_out and b == 0 and a == 0 and c == 0:
                                p.g(lambda e: e.dma_start(dbg_out["a_x0"], Xb[0][:, :]),
                                    r=[L(Xb[0])], w=[])
                                p.g(lambda e: e.dma_start(dbg_out["a_u0"], Ub[0][:, :]),
                                    r=[L(Ub[0])], w=[])
                                p.g(lambda e: e.dma_start(dbg_out["a_z0"], Zb[0][:, :]),
                                    r=[L(Zb[0])], w=[])
                                p.g(lambda e: e.dma_start(dbg_out["a_on"], On[0][:, :]),
                                    r=[L(On[0])], w=[])
                                p.y(lambda e: e.dma_start(dbg_out["a_sf"], Sf[:, :]),
                                    r=[L(Sf)], w=[])
                            for dh in range(2):
                                for oc in range(8):
                                    p.t(lambda e, c=c, oc=oc, dh=dh: e.matmul(
                                        P[0][:, :], otb[c % 2][:, oc * 128:(oc + 1) * 128],
                                        wo_sb[:, oc * 1024 + dh * 512: oc * 1024 + (dh + 1) * 512],
                                        start=(oc == 0), stop=(oc == 7)),
                                        r=[L(otb[c % 2], oc * 128, (oc + 1) * 128),
                                           L(B, oc * 1024 + dh * 512, oc * 1024 + (dh + 1) * 512)],
                                        w=[PR(0)])
                                lo = c * 1024 + dh * 512
                                p.v(lambda e, lo=lo: e.tensor_add(
                                    Rg[1][:, lo:lo + 512], Rg[0][:, lo:lo + 512], P[0][:, :]),
                                    r=[L(Rg[0], lo, lo + 512), PR(0)],
                                    w=[L(Rg[1], lo, lo + 512)])
                        dst_ap = dst.rearrange("(gg s pp) d -> gg pp s d", gg=NG, s=SPG, pp=128)[g]
                        p.g(lambda e, dap=dst_ap: e.dma_start(
                            dap, Rg[1].ap().rearrange("p (s d) -> p s d", s=SPG)),
                            r=[L(Rg[1])], w=[(dst_name, g, g + 1)])
                return

            moe_pass(0, dr["x0"], "x0", rb1 if npass > 1 else out,
                     "rb1" if npass > 1 else "out", parity=0)
            if npass == 5:
                pass
            if npass == 5:
                pass
            elif npass == 2:
                attn_pass(rb1, "rb1", out, "out")
            elif npass == 4:
                moe_pass(1, rb1, "rb1", out, "out", parity=1)
            elif npass >= 3:
                attn_pass(rb1, "rb1", rb2, "rb2")
                moe_pass(1, rb2, "rb2", out, "out", parity=1)
            p.wait_only("sync", [("yq", p.counts["yq"])])
            p.wait_only("gpsimd", [("gq", p.counts["gq"])])


        p1 = Plan(nc, ctx)
        p1.phase = 1
        emit_all(p1)
        p = Plan(nc, ctx)
        p.phase = 2
        p.consumed = p1.consumed
        emit_all(p)
        block = ctx.enter_context(nc.Block())
        p.replay(block)

    return nc


def make_consts():
    c = {}
    c["ident"] = np.eye(128, dtype=np.float32)
    c["triu_inc"] = np.triu(np.ones((128, 128), np.float32))
    c["triu_str"] = np.triu(np.ones((128, 128), np.float32), 1)
    c["triu32"] = np.triu(np.ones((32, 32), np.float32), 1)
    c["iota8"] = np.broadcast_to(np.arange(8, dtype=np.float32), (128, 8)).copy()
    c["ones_col"] = np.ones((128, 1), np.float32)
    c["ones1"] = np.ones((1, 128), np.float32)
    return c


def make_weight_maps(inp):
    m = {}
    W1 = np.asarray(inp["L0_ffn_W1"], dtype=np.float32)
    m["w1s"] = np.ascontiguousarray(
        W1.reshape(E, 8, 128, 8, 512).transpose(0, 3, 2, 1, 4))
    W2 = np.asarray(inp["L0_ffn_W2"], dtype=np.float32)
    m["w2s"] = np.ascontiguousarray(
        W2.reshape(E, 4, 8, 128, 2, 512).transpose(0, 4, 1, 3, 2, 5))
    for nm, key in [("wqs", "L0_attn_Wq"), ("wks", "L0_attn_Wk"),
                    ("wvs", "L0_attn_Wv"), ("wos", "L0_attn_Wo")]:
        W = np.asarray(inp[key], dtype=np.float32)
        m[nm] = np.ascontiguousarray(W.reshape(8, 128, 1024).transpose(1, 0, 2))
    Wb = np.asarray(inp["L0_attn_Wbeta"], dtype=np.float32)
    m["wbs"] = np.ascontiguousarray(Wb.reshape(8, 128, 4).transpose(1, 0, 2))
    Wr = np.asarray(inp["L0_ffn_Wr"], dtype=np.float32)
    m["wrs"] = np.ascontiguousarray(Wr.reshape(8, 128, 8).transpose(1, 0, 2))
    b1 = np.asarray(inp["L0_ffn_b1"], dtype=np.float32)
    m["b1s"] = np.ascontiguousarray(b1.reshape(8, 32, 128).transpose(2, 0, 1))
    m["b2s"] = np.ascontiguousarray(np.asarray(inp["L0_ffn_b2"], dtype=np.float32))
    gn = np.asarray(inp["L0_attn_g_norm"], dtype=np.float32)
    m["gb"] = np.ascontiguousarray(np.broadcast_to(gn, (128, 256)))
    m.update(make_consts())
    return m


def make_in_maps(inp):
    shared = make_weight_maps(inp)
    x = np.asarray(inp["L0_attn_x"], dtype=np.float32)
    maps = []
    for c in range(NCORES):
        d = dict(shared)
        d["x0"] = np.ascontiguousarray(x[2 * c:2 * c + 2].reshape(TOK, D))
        cm = np.zeros((8, 1), np.float32)
        cm[:c] = 1.0
        d["coremask"] = cm
        maps.append(d)
    return maps


_NC_CACHE = {}
KN_NPASS = 3
KN_DBG = ()


def kernel(**inputs):
    key = (KN_NPASS, tuple(KN_DBG), PRUNE, SIZE_PROBE_G)
    if key not in _NC_CACHE:
        _NC_CACHE[key] = build(npass=KN_NPASS, dbg=KN_DBG)
    nc = _NC_CACHE[key]
    maps = make_in_maps(inputs)
    res = run_bass_kernel_spmd(nc, maps, core_ids=list(range(NCORES)))
    outs = [np.asarray(res.results[c]["out"]) for c in range(NCORES)]
    full = np.concatenate([o.reshape(2, 2048, 1024) for o in outs], axis=0)
    if KN_DBG:
        return full, res
    return full
